# version 09
# baseline (speedup 1.0000x reference)
from contextlib import ExitStack

import numpy as np
import ml_dtypes

import concourse.bass as bass
import concourse.mybir as mybir
from concourse.bass_utils import run_bass_kernel_spmd

F32 = mybir.dt.float32
BF16 = mybir.dt.bfloat16
AF = mybir.ActivationFunctionType
ALU = mybir.AluOpType
AX = mybir.AxisListType
BF = ml_dtypes.bfloat16

D = 1024
NT = 4096
TOWN = 2048
NH = 16
DH = 64
NEXP = 64
PSL = 32
NPL = 2048
EOWN = 8
DHID = 4096
LN_EPS = 1e-5

ENG_ATTR = {"t": "tensor", "v": "vector", "s": "scalar", "g": "gpsimd", "y": "sync"}
SEM_ROT = 50000


class P:

    def __init__(self):
        self.nc = bass.Bass("TRN2", target_bir_lowering=False)
        self.st = ExitStack()
        self.sems = {}
        self.cnt = {}
        self.marks = {e: {} for e in ENG_ATTR}
        self.block = None
        self.q = {}

    def sb(self, name, shape, dtype):
        return self.st.enter_context(self.nc.sbuf_tensor(name, shape, dtype))

    def psum(self, name, shape, dtype=F32):
        return self.st.enter_context(self.nc.psum_tensor(name, shape, dtype))

    def region(self):
        return self.st.enter_context(ExitStack())

    def rsb(self, region, name, shape, dtype):
        return region.enter_context(self.nc.sbuf_tensor(name, shape, dtype))

    def rps(self, region, name, shape, dtype=F32):
        return region.enter_context(self.nc.psum_tensor(name, shape, dtype))

    def din(self, name, shape, dtype=F32):
        return self.nc.dram_tensor(name, list(shape), dtype, kind="ExternalInput").ap()

    def dout(self, name, shape, dtype=F32):
        return self.nc.dram_tensor(name, list(shape), dtype, kind="ExternalOutput").ap()

    def _sem(self, name):
        if name not in self.sems:
            self.sems[name] = self.st.enter_context(self.nc.semaphore(name))
            self.cnt[name] = 0
        return self.sems[name]

    def _sig_name(self, eng, dma):
        base = eng + ("d" if dma else "")
        gen = self.cnt.get("#" + base, 0)
        name = f"{base}.{gen}"
        self._sem(name)
        if self.cnt[name] >= SEM_ROT:
            gen += 1
            self.cnt["#" + base] = gen
            name = f"{base}.{gen}"
            self._sem(name)
        return name

    def start(self):
        self.block = self.st.enter_context(self.nc.Block())

    def _emit_waits(self, engine, eng, deps):
        for tok in deps:
            if tok is None:
                continue
            sn, val, prod, isd = tok
            if prod == eng and not isd:
                continue
            if self.marks[eng].get(sn, 0) >= val:
                continue
            engine.wait_ge(self.sems[sn], val)
            self.marks[eng][sn] = val

    def do(self, eng, fn, deps=(), sig=True, force=()):
        if sig:
            sn = self._sig_name(eng, False)
            self.cnt[sn] += 1
            tok = (sn, self.cnt[sn], eng, False)
        else:
            sn, tok = None, None
        deps = tuple(deps)
        force = tuple(force)

        def body(engine, sn=sn, deps=deps, force=force, fn=fn):
            self._emit_waits(engine, eng, deps)
            for ftok in force:
                if ftok is None:
                    continue
                fsn, fval, _, _ = ftok
                if self.marks[eng].get(fsn, 0) >= fval:
                    continue
                engine.wait_ge(self.sems[fsn], fval)
                self.marks[eng][fsn] = fval
            inst = fn(engine)
            if sn is not None:
                if inst is None:
                    engine.sem_inc(self.sems[sn], 1)
                else:
                    inst.then_inc(self.sems[sn], 1)

        self.q.setdefault(eng, []).append(body)
        return tok

    def load(self, eng, pairs, deps=()):
        assert eng in ("g", "y")
        sn = self._sig_name(eng, True)
        self.cnt[sn] += 16 * len(pairs)
        tok = (sn, self.cnt[sn], eng, True)
        deps = tuple(deps)
        pairs = list(pairs)

        def body(engine, sn=sn, deps=deps, pairs=pairs):
            self._emit_waits(engine, eng, deps)
            for out, in_ in pairs:
                engine.dma_start(out=out, in_=in_).then_inc(self.sems[sn], 16)

        self.q.setdefault(eng, []).append(body)
        return tok

    def wait_only(self, eng, deps):
        self.do(eng, lambda e: None, deps=deps, sig=False)

    def flush(self):
        for eng, queue in self.q.items():
            if not queue:
                continue
            def body(engine, queue=list(queue)):
                for fn in queue:
                    fn(engine)
            getattr(self.block, ENG_ATTR[eng])(body)
            queue.clear()

    def close(self, region):
        self.flush()
        region.close()

    def finish(self):
        self.flush()
        self.st.close()
        return self.nc


def consts_region(p, want_identf=False):
    p.epsc = p.sb("epsc", [128, 1], F32)
    p.c20 = p.sb("c20", [128, 1], F32)
    p.ones_col = p.sb("ones_col", [128, 1], BF16)
    p.ones_row = p.sb("ones_row", [1, 128], BF16)
    p.ident = p.sb("ident", [128, 128], BF16)
    id_in = p.din("ident128", [128, 128])
    tok_id = p.load("g", [(p.ident[:], id_in[:, :])])
    if want_identf:
        p.identf = p.sb("identf", [128, 128], F32)
        p.identf_tok = p.load("y", [(p.identf[:], id_in[:, :])])
    tok_ms = p.do("v", lambda v: (
        v.memset(p.epsc[:], LN_EPS),
        v.memset(p.c20[:], -20.0),
        v.memset(p.ones_col[:], 1.0),
        v.memset(p.ones_row[:], 1.0))[-1])
    return [tok_id, tok_ms]


def ln_cols_emit(p, src, mcol, rcol, scr, deps):
    dd = float(src.shape[-1])
    t1 = p.do("v", lambda v: (
        v.reduce_sum(scr["sum"], src, axis=AX.X),
        v.scalar_tensor_tensor(scr["dust"], src, 1.0, src, op0=ALU.mult,
                               op1=ALU.mult, accum_out=scr["ssq"]))[-1],
        deps=deps)
    t2 = p.do("s", lambda s: (
        s.activation(mcol, scr["sum"], AF.Copy, scale=1.0 / dd),
        s.activation(scr["msq"], scr["ssq"], AF.Copy, scale=1.0 / dd),
        s.activation(scr["m2"], mcol, AF.Square))[-1], deps=[t1])
    t3 = p.do("v", lambda v: v.tensor_sub(scr["nv"], scr["m2"], scr["msq"]),
              deps=[t2])
    t4 = p.do("s", lambda s: (
        s.activation(scr["nv"], scr["nv"], AF.Ln, bias=p.epsc[:, 0:1], scale=-1.0),
        s.activation(rcol, scr["nv"], AF.Exp, scale=-0.5))[-1], deps=[t3])
    return t4



def build_ml1(dbg=False):
    p = P()
    xT = p.din("xT", [D, NT])
    xtok = p.din("xtok", [TOWN, D])
    WqkvT_i = p.din("WqkvT", [D, D])
    WoT_i = p.din("WoT", [D, D])
    phiT_i = p.din("phiT", [D, NPL])
    phibr_i = p.din("phibr", [1, NPL])
    phibc_i = p.din("phibc", [128, 16])
    IndP_i = p.din("IndP", [128, 32])
    IndPT_i = p.din("IndPT", [32, 128])
    s0T_o = p.dout("s0T", [D, TOWN], BF16)
    if dbg:
        dbg_ab = p.dout("dbg_ab", [2, NT])
        dbg_xn = p.dout("dbg_xn", [D, NT], BF16)
        dbg_qkv = p.dout("dbg_qkv", [D, NT], BF16)
        dbg_xtok = p.dout("dbg_xtok", [128, 16 * D], BF16)
        dbg_dg = p.dout("dbg_dg", [128, 16 * 512], BF16)
        dbg_eb = p.dout("dbg_eb", [128, 512], BF16)
        dbg_sd = p.dout("dbg_sd", [128, 32])
    Xp_o = p.dout("Xp", [NPL, D])
    CT_o = p.dout("CT", [NPL, TOWN], BF16)

    p.start()
    ctoks = consts_region(p)

    xn0T = [p.sb(f"xn0T{d}", [128, NT], BF16) for d in range(8)]
    phibv = p.sb("phibv", [1, NPL], BF16)
    phibcv = p.sb("phibcv", [128, 16], F32)
    indP = p.sb("indP", [128, 32], BF16)
    indPT = p.sb("indPT", [32, 128], BF16)

    misc_tok = p.load("g", [(phibv[0:1, :], phibr_i[0:1, :]),
                            (indP[:], IndP_i[:, :]),
                            (indPT[:], IndPT_i[:, :])])
    phibc_tok = p.load("y", [(phibcv[:], phibc_i[:, :])])

    r1 = p.region()
    xball = p.rsb(r1, "xball", [128, NT], BF16)
    xsq = p.rsb(r1, "xsq", [128, NT], BF16)
    rowscr = [p.rsb(r1, f"rowscr{i}", [1, 3072], F32) for i in range(2)]
    abf = [p.rsb(r1, f"abf{i}", [1, 1024], BF16) for i in range(2)]
    pstat = [p.rps(r1, f"pstat{i}", [1, 1024]) for i in range(2)]
    pab = [p.rps(r1, f"pab{i}", [128, 1024]) for i in range(2)]
    MEAN, MSQ, NV, TMP, AA, BB = 0, 512, 1024, 1536, 2048, 2560

    tg_done = [None] * 8
    sq_done = [None] * 8
    mm_done = [None] * 8
    tv_done = [None] * 8
    abf_done = [None] * 8
    for tg in range(8):
        c0 = 512 * tg
        ldeps = [ctoks[1]] + ([sq_done[tg - 1], mm_done[tg - 1], tg_done[tg - 1]]
                              if tg else [])
        tld = p.load("g", [(xball[:, 512 * d:512 * (d + 1)],
                            xT[128 * d:128 * (d + 1), c0:c0 + 512])
                           for d in range(8)], deps=ldeps)
        sq_done[tg] = p.do("s", lambda s, tg=tg: [
            s.activation(xsq[:, 512 * d:512 * (d + 1)],
                         xball[:, 512 * d:512 * (d + 1)], AF.Square)
            for d in range(8)][-1], deps=[tld])

        def statmm(t, tg=tg):
            for d in range(8):
                t.matmul(pstat[0:1, 0:512], p.ones_col[:],
                         xball[:, 512 * d:512 * (d + 1)],
                         start=(d == 0), stop=(d == 7))
            last = None
            for d in range(8):
                last = t.matmul(pstat[0:1, 512:1024], p.ones_col[:],
                                xsq[:, 512 * d:512 * (d + 1)],
                                start=(d == 0), stop=(d == 7))
            return last
        mm_done[tg] = p.do("t", statmm,
                           deps=[tld, sq_done[tg], ctoks[1],
                                 tv_done[tg - 1] if tg else None])

        def rmath_v(v, tg=tg):
            v.tensor_scalar_mul(rowscr[0:1, MEAN:MEAN + 512],
                                pstat[0:1, 0:512], 1.0 / D)
            v.tensor_scalar_mul(rowscr[0:1, MSQ:MSQ + 512],
                                pstat[0:1, 512:1024], 1.0 / D)
            v.tensor_mul(rowscr[0:1, TMP:TMP + 512], rowscr[0:1, MEAN:MEAN + 512],
                         rowscr[0:1, MEAN:MEAN + 512])
            return v.tensor_sub(rowscr[0:1, NV:NV + 512],
                                rowscr[0:1, TMP:TMP + 512],
                                rowscr[0:1, MSQ:MSQ + 512])
        tv_done[tg] = p.do("v", rmath_v,
                           deps=[mm_done[tg], abf_done[tg - 1] if tg else None])

        def rmath_s(s, tg=tg):
            s.activation(rowscr[0:1, NV:NV + 512], rowscr[0:1, NV:NV + 512],
                         AF.Ln, bias=p.epsc[0:1, 0:1], scale=-1.0)
            return s.activation(rowscr[0:1, AA:AA + 512],
                                rowscr[0:1, NV:NV + 512], AF.Exp, scale=-0.5)
        ts = p.do("s", rmath_s, deps=[tv_done[tg]])
        abf_done[tg] = p.do("v", lambda v: (
            v.tensor_mul(rowscr[0:1, TMP:TMP + 512], rowscr[0:1, MEAN:MEAN + 512],
                         rowscr[0:1, AA:AA + 512]),
            v.tensor_scalar_mul(rowscr[0:1, BB:BB + 512],
                                rowscr[0:1, TMP:TMP + 512], -1.0),
            v.tensor_copy(abf[0:1, 0:512], rowscr[0:1, AA:AA + 512]),
            v.tensor_copy(abf[0:1, 512:1024], rowscr[0:1, BB:BB + 512]))[-1],
            deps=[ts])
        if dbg:
            p.load("y", [(dbg_ab[0:1, c0:c0 + 512], rowscr[0:1, AA:AA + 512]),
                         (dbg_ab[1:2, c0:c0 + 512], rowscr[0:1, BB:BB + 512])],
                   deps=[abf_done[tg]])
        tbc = p.do("t", lambda t: (
            t.matmul(pab[:, 0:512], p.ones_row[0:1, :], abf[0:1, 0:512],
                     start=True, stop=True),
            t.matmul(pab[:, 512:1024], p.ones_row[0:1, :], abf[0:1, 512:1024],
                     start=True, stop=True))[-1],
            deps=[abf_done[tg], ctoks[1], tg_done[tg - 1] if tg else None])
        tg_done[tg] = p.do("v", lambda v, c0=c0: [
            (v.tensor_tensor(xn0T[d][:, c0:c0 + 512],
                             xball[:, 512 * d:512 * (d + 1)],
                             pab[:, 0:512], op=ALU.mult),
             v.tensor_tensor(xn0T[d][:, c0:c0 + 512], xn0T[d][:, c0:c0 + 512],
                             pab[:, 512:1024], op=ALU.add))
            for d in range(8)][-1][-1], deps=[tbc])
    if dbg:
        p.load("y", [(dbg_xn[128 * d:128 * (d + 1), :], xn0T[d][:])
                     for d in range(8)], deps=tg_done)
    p.close(r1)

    rB = p.region()
    attn0T = [p.rsb(rB, f"attn0T{d}", [128, TOWN], BF16) for d in range(8)]
    rA = p.region()
    qkvT = [p.rsb(rA, f"qkvT{c}", [128, NT], BF16) for c in range(8)]
    rW = p.region()
    wqkv = [p.rsb(rW, f"wqkv{d}", [128, D], BF16) for d in range(8)]
    pqk = p.rps(rW, "pqk", [128, 1024])
    w_tok = p.load("g", [(wqkv[d][:], WqkvT_i[128 * d:128 * (d + 1), :])
                         for d in range(8)],
                   deps=[tg_done[7], mm_done[7], sq_done[7]])

    p1_drain = [tg_done[7], mm_done[7]]
    ev2 = {}
    for og in range(8):
        for tg in range(8):
            i = og * 8 + tg
            par = i % 2
            tmm = p.do("t", lambda t, og=og, tg=tg, par=par: [
                t.matmul(pqk[:, 512 * par:512 * (par + 1)],
                         wqkv[d][:, 128 * og:128 * (og + 1)],
                         xn0T[d][:, 512 * tg:512 * (tg + 1)],
                         start=(d == 0), stop=(d == 7))
                for d in range(8)][-1],
                deps=[w_tok, tg_done[tg], ev2.get(i - 2)] + p1_drain)
            ev2[i] = p.do("s", lambda s, og=og, tg=tg, par=par: s.activation(
                qkvT[og][:, 512 * tg:512 * (tg + 1)],
                pqk[:, 512 * par:512 * (par + 1)], AF.Copy),
                deps=[tmm])
    qkv_done = ev2[63]
    if dbg:
        p.load("y", [(dbg_qkv[128 * c:128 * (c + 1), :], qkvT[c][:])
                     for c in range(8)], deps=[qkv_done])
    p.close(rW)

    v1 = [p.rsb(rA, f"v1_{i}", [128, 32 * 65], BF16) for i in range(2)]
    escr = [p.rsb(rA, f"escr{i}", [128, 1024], BF16) for i in range(2)]
    avsb = p.rsb(rA, "avsb", [64, 512], BF16)
    dnrow = p.rsb(rA, "dnrow", [1, 512], BF16)
    dnrowf = p.rsb(rA, "dnrowf", [1, 512], F32)
    ptr3 = [p.rps(rA, f"ptr3_{i}", [128, 1024], BF16) for i in range(2)]
    psc = [p.rps(rA, f"psc{i}", [128, 1024]) for i in range(2)]
    pav = p.rps(rA, "pav", [128, 512])
    pdn = p.rps(rA, "pdn", [64, 512])

    vms = p.do("v", lambda v: (v.memset(v1[0][:], 1.0),
                               v.memset(v1[1][:], 1.0))[-1], deps=[qkv_done])

    v1use = {0: None, 1: None}
    prev_tcp = None
    prev_trc = None
    prev_tno = None
    for h in range(NH):
        hc, sub = h // 2, 64 * (h % 2)
        vb = h % 2
        idsl = p.ident[sub:sub + 64, sub:sub + 64]
        tr_toks = []
        for jg in range(8):
            par = jg % 2
            ttr = p.do("t", lambda t, hc=hc, sub=sub, jg=jg, par=par, idsl=idsl: [
                t.transpose(ptr3[par][:, 64 * q:64 * q + 64],
                            qkvT[hc][sub:sub + 64,
                                     128 * (4 * jg + q):128 * (4 * jg + q) + 128],
                            idsl)
                for q in range(4)][-1],
                deps=[qkv_done, ctoks[0],
                      tr_toks[-2] if len(tr_toks) >= 2 else None])
            tev = p.do("v", lambda v, jg=jg, par=par, vb=vb: v.tensor_copy(
                v1[vb][:].rearrange("p (a b) -> p a b", a=32, b=65)
                [:, 4 * jg:4 * jg + 4, 0:64],
                ptr3[par][:, 0:256].rearrange("p (a b) -> p a b", a=4, b=64)),
                deps=[ttr, vms, v1use[vb]])
            tr_toks.append(tev)
        v1_ready = tr_toks[-1]

        for g in range(4):
            qs = 512 * g
            av_last = None
            exp_toks = {}
            av_toks = {}
            for jj in range(16):
                par = jj % 2
                def smm(t, hc=hc, sub=sub, jj=jj, par=par, qs=qs):
                    last = None
                    for u in range(2):
                        j = 2 * jj + u
                        last = t.matmul(psc[par][:, 512 * u:512 * (u + 1)],
                                        qkvT[hc][sub:sub + 64, 128 * j:128 * (j + 1)],
                                        qkvT[hc][sub:sub + 64, qs:qs + 512],
                                        start=True, stop=True)
                    return last
                tsc = p.do("t", smm, deps=[qkv_done, exp_toks.get(jj - 2)])
                texp = p.do("s", lambda s, par=par: s.activation(
                    escr[par][:], psc[par][:], AF.Exp,
                    bias=p.c20[:, 0:1], scale=0.125),
                    deps=[tsc, ctoks[1], av_toks.get(jj - 2)])
                exp_toks[jj] = texp
                def avmm(t, jj=jj, par=par, vb=vb):
                    last = None
                    for u in range(2):
                        j = 2 * jj + u
                        last = t.matmul(pav[0:65, :],
                                        v1[vb][:, 65 * j:65 * (j + 1)],
                                        escr[par][:, 512 * u:512 * (u + 1)],
                                        start=(j == 0), stop=(j == 31))
                    return last
                avdeps = [texp, v1_ready]
                if jj == 0:
                    avdeps += [prev_tcp, prev_trc]
                av_last = p.do("t", avmm, deps=avdeps)
                av_toks[jj] = av_last
            v1use[vb] = av_last
            tcp = p.do("s", lambda s: s.activation(avsb[:], pav[0:64, :], AF.Copy),
                       deps=[av_last, prev_tno])
            trc = p.do("s", lambda s: (
                s.activation(dnrowf[0:1, :], pav[64:65, :], AF.Ln),
                s.activation(dnrow[0:1, :], dnrowf[0:1, :], AF.Exp, scale=-1.0))[-1],
                deps=[av_last, prev_tno])
            tbc = p.do("t", lambda t: t.matmul(
                pdn[:, :], p.ones_row[0:1, 0:64], dnrow[0:1, :],
                start=True, stop=True), deps=[trc, ctoks[1], prev_tno])
            tno = p.do("v", lambda v, hc=hc, sub=sub, qs=qs: v.tensor_tensor(
                attn0T[hc][sub:sub + 64, qs:qs + 512],
                avsb[:], pdn[:, :], op=ALU.mult), deps=[tbc, tcp])
            prev_tcp, prev_trc, prev_tno = tcp, trc, tno
    attn_done = prev_tno
    p.close(rA)

    wo = [p.rsb(rB, f"wo{d}", [128, D], BF16) for d in range(8)]
    s0sb = p.rsb(rB, "s0sb", [128, TOWN], BF16)
    pwo = p.rps(rB, "pwo", [128, 1024])
    wo_tok = p.load("g", [(wo[d][:], WoT_i[128 * d:128 * (d + 1), :])
                          for d in range(8)], deps=[attn_done])
    s0_store = []
    ev4 = {}
    for og in range(8):
        evs = []
        for tg in range(4):
            i = og * 4 + tg
            par = i % 2
            tmm = p.do("t", lambda t, og=og, tg=tg, par=par: [
                t.matmul(pwo[:, 512 * par:512 * (par + 1)],
                         wo[d][:, 128 * og:128 * (og + 1)],
                         attn0T[d][:, 512 * tg:512 * (tg + 1)],
                         start=(d == 0), stop=(d == 7))
                for d in range(8)][-1],
                deps=[wo_tok, attn_done, ev4.get(i - 2)])
            ev4[i] = p.do("s", lambda s, tg=tg, par=par: s.activation(
                s0sb[:, 512 * tg:512 * (tg + 1)],
                pwo[:, 512 * par:512 * (par + 1)], AF.Copy),
                deps=[tmm, s0_store[-1] if s0_store else None])
            evs.append(ev4[i])
        s0_store.append(p.load(
            "y", [(s0T_o[128 * og:128 * (og + 1), :], s0sb[:])], deps=evs))
    p4_drain = [s0_store[-1], ev4[31]]
    p.close(rB)

    phiw = [p.sb(f"phiw{d}", [128, NPL], BF16) for d in range(8)]
    phi_tok = p.load("g", [(phiw[d][:], phiT_i[128 * d:128 * (d + 1), :])
                           for d in range(8)], deps=p4_drain)
    r5 = p.region()
    xn0tok = p.rsb(r5, "xn0tok", [128, 16 * D], BF16)
    xtk = p.rsb(r5, "xtk", [128, D], F32)
    colscr = p.rsb(r5, "colscr", [128, 8], F32)
    dustb = p.rsb(r5, "dustb", [128, D], F32)
    scr = {"sum": colscr[:, 0:1], "ssq": colscr[:, 1:2], "m2": colscr[:, 2:3],
           "msq": colscr[:, 3:4], "nv": colscr[:, 4:5], "dust": dustb[:]}
    xtk_free = None
    for tc in range(16):
        tl = p.load("y", [(xtk[:], xtok[128 * tc:128 * (tc + 1), :])],
                    deps=[xtk_free] + (p4_drain if tc == 0 else []))
        tln = ln_cols_emit(p, xtk[:], colscr[:, 5:6], colscr[:, 6:7], scr,
                           deps=[tl, ctoks[1]])
        xtk_free = p.do("v", lambda v, tc=tc: (
            v.tensor_scalar_sub(xtk[:], xtk[:], colscr[:, 5:6]),
            v.tensor_scalar_mul(xn0tok[:, D * tc:D * (tc + 1)], xtk[:],
                                colscr[:, 6:7]))[-1], deps=[tln])
    xn0tok_done = xtk_free
    if dbg:
        p.load("y", [(dbg_xtok[:, :], xn0tok[:, :])], deps=[xn0tok_done])

    Eb = [p.rsb(r5, f"Eb{i}", [128, 512], BF16) for i in range(2)]
    sd5 = p.rsb(r5, "sd5", [128, 16], F32)
    rsd5 = p.rsb(r5, "rsd5", [128, 16], F32)
    Dg = p.rsb(r5, "Dg", [128, 16 * 512], BF16)
    xpev = p.rsb(r5, "xpev", [128, 512], F32)
    plg = p.rps(r5, "plg", [128, 1024])
    pxp = p.rps(r5, "pxp", [128, 1024])

    xmm_last = None
    xp_done = None
    for npg in range(4):
        d_toks = []
        e_free = {}
        for tc in range(16):
            par = tc % 2
            def lgmm(t, npg=npg, tc=tc, par=par):
                for d in range(8):
                    t.matmul(plg[:, 512 * par:512 * (par + 1)],
                             xn0T[d][:, 128 * tc:128 * (tc + 1)],
                             phiw[d][:, 512 * npg:512 * (npg + 1)],
                             start=(d == 0), stop=False)
                return t.matmul(plg[:, 512 * par:512 * (par + 1)],
                                p.ones_row[0:1, :],
                                phibv[0:1, 512 * npg:512 * (npg + 1)],
                                start=False, stop=True)
            tlg = p.do("t", lgmm, deps=[phi_tok, misc_tok, ctoks[1],
                                        e_free.get(tc - 2)] +
                       (p4_drain if (npg == 0 and tc == 0) else []))
            tex = p.do("s", lambda s, par=par: s.activation(
                Eb[par][:], plg[:, 512 * par:512 * (par + 1)], AF.Exp),
                deps=[tlg])
            p5_tex_last = tex
            tsd0 = p.do("v", lambda v, par=par: v.tensor_reduce(
                sd5[:], Eb[par][:].rearrange("p (a b) -> p a b", a=16, b=32),
                axis=AX.X, op=ALU.add), deps=[tex, e_free.get(tc - 1)])
            tsd = p.do("s", lambda s: (
                s.activation(sd5[:], sd5[:], AF.Ln),
                s.activation(rsd5[:], sd5[:], AF.Exp, scale=-1.0))[-1],
                deps=[tsd0])
            tdg = p.do("v", lambda v, tc=tc, par=par: [
                v.tensor_scalar_mul(
                    Dg[:, 512 * tc + 32 * n:512 * tc + 32 * (n + 1)],
                    Eb[par][:, 32 * n:32 * (n + 1)], rsd5[:, n:n + 1])
                for n in range(16)][-1], deps=[tsd, xmm_last])
            if dbg and npg == 0 and tc == 0:
                tdump = p.load("y", [(dbg_eb[:, :], Eb[par][:]),
                                     (dbg_sd[:, 0:16], sd5[:]),
                                     (dbg_sd[:, 16:32], rsd5[:])], deps=[tdg])
                tdg = tdump
            e_free[tc] = tdg
            d_toks.append(tdg)
        if dbg and npg == 0:
            p.load("y", [(dbg_dg[:, :], Dg[:, :])], deps=d_toks)
        xp_prev = {}
        tst = None
        for s_ in range(4):
            for dg in range(2):
                i = s_ * 2 + dg
                par = i % 2
                def xmm(t, s_=s_, dg=dg, par=par):
                    last = None
                    for tc in range(16):
                        last = t.matmul(
                            pxp[:, 512 * par:512 * (par + 1)],
                            Dg[:, 512 * tc + 128 * s_:512 * tc + 128 * (s_ + 1)],
                            xn0tok[:, D * tc + 512 * dg:D * tc + 512 * (dg + 1)],
                            start=(tc == 0), stop=(tc == 15))
                    return last
                txm = p.do("t", xmm, deps=d_toks + [xn0tok_done,
                                                    xp_prev.get(i - 2)])
                tev = p.do("v", lambda v, par=par: v.tensor_copy(
                    xpev[:], pxp[:, 512 * par:512 * (par + 1)]),
                    deps=[txm, tst])
                tst = p.load("y", [(Xp_o[512 * npg + 128 * s_:
                                         512 * npg + 128 * (s_ + 1),
                                         512 * dg:512 * (dg + 1)], xpev[:])],
                             deps=[tev])
                xp_prev[i] = tev
                xmm_last = txm
        xp_done = tst
    p5_drain = [xp_done, p5_tex_last, xmm_last]
    p.close(r5)

    r6 = p.region()
    CT = [p.rsb(r6, f"CT{i}", [128, TOWN], BF16) for i in range(16)]
    rscb = p.rsb(r6, "rscb", [32, TOWN], BF16)
    rscf = p.rsb(r6, "rscf", [32, TOWN], F32)
    plf = p.rps(r6, "plf", [128, 1024])
    psc6 = p.rps(r6, "psc6", [32, TOWN])
    prb = p.rps(r6, "prb", [128, 512])

    exp6 = {}
    sc_last = {}
    for npc in range(16):
        for tg in range(4):
            i = npc * 4 + tg
            par = i % 2
            def lfmm(t, npc=npc, tg=tg, par=par):
                last = None
                for d in range(8):
                    last = t.matmul(plf[:, 512 * par:512 * (par + 1)],
                                    phiw[d][:, 128 * npc:128 * (npc + 1)],
                                    xn0T[d][:, 512 * tg:512 * (tg + 1)],
                                    start=(d == 0), stop=(d == 7))
                return last
            tlf = p.do("t", lfmm, deps=[phi_tok, exp6.get(i - 2)] +
                       (p5_drain if i == 0 else []))
            tex = p.do("s", lambda s, npc=npc, tg=tg, par=par: s.activation(
                CT[npc][:, 512 * tg:512 * (tg + 1)],
                plf[:, 512 * par:512 * (par + 1)], AF.Exp,
                bias=phibcv[:, npc:npc + 1]), deps=[tlf, phibc_tok])
            exp6[i] = tex
            sc_last[tg] = p.do("t", lambda t, npc=npc, tg=tg: t.matmul(
                psc6[:, 512 * tg:512 * (tg + 1)], indP[:],
                CT[npc][:, 512 * tg:512 * (tg + 1)],
                start=(npc == 0), stop=(npc == 15), skip_group_check=True),
                deps=[tex, misc_tok])
    trsc = p.do("s", lambda s: (
        s.activation(rscf[:], psc6[:, :], AF.Ln),
        s.activation(rscb[:], rscf[:], AF.Exp, scale=-1.0))[-1],
        deps=[sc_last[tg] for tg in range(4)])
    ct_store = []
    prb_readers = None
    for tg in range(4):
        tbc = p.do("t", lambda t, tg=tg: t.matmul(
            prb[:, :], indPT[:], rscb[:, 512 * tg:512 * (tg + 1)],
            start=True, stop=True), deps=[trsc, misc_tok, prb_readers])
        sc_mults = []
        for npc in range(16):
            tm = p.do("v", lambda v, npc=npc, tg=tg: v.tensor_tensor(
                CT[npc][:, 512 * tg:512 * (tg + 1)],
                CT[npc][:, 512 * tg:512 * (tg + 1)], prb[:, :], op=ALU.mult),
                deps=[tbc, exp6[npc * 4 + tg]])
            sc_mults.append(tm)
        prb_readers = sc_mults[-1]
        ct_store.append(p.load("y", [
            (CT_o[128 * npc:128 * (npc + 1), 512 * tg:512 * (tg + 1)],
             CT[npc][:, 512 * tg:512 * (tg + 1)]) for npc in range(16)],
            deps=sc_mults))
    p.close(r6)

    p.wait_only("y", s0_store + ct_store + [xp_done])
    return p.finish()



def build_ml2():
    p = P()
    xp_in = p.din("xp_all", [8, 256, D])
    expw = p.din("expw", [EOWN, D, D])
    expb = p.din("expb", [1, EOWN * D])
    yt_out = p.dout("yt", [EOWN, PSL, D])

    p.start()
    ctoks = consts_region(p)

    xp = [p.sb(f"xp{s}", [128, 2 * D], F32) for s in range(8)]
    xacc = [p.sb(f"xacc{c}", [128, D], F32) for c in range(2)]
    xn = [p.sb(f"xn{c}", [128, D], BF16) for c in range(2)]
    xnT = [p.sb(f"xnT{d}", [128, 256], BF16) for d in range(8)]
    cols = p.sb("cols", [128, 8], F32)
    dust = p.sb("dust", [128, D], F32)
    ebf = p.sb("ebf", [1, EOWN * D], BF16)
    strip = [p.sb(f"strip{i}", [128, D], BF16) for i in range(2)]
    ytsb = p.sb("ytsb", [32, D], F32)
    ptr = [p.psum(f"ptr{i}", [128, 1024], BF16) for i in range(2)]
    pyt = p.psum("pyt", [32, 1024], F32)

    scr = {"sum": cols[:, 0:1], "ssq": cols[:, 1:2], "m2": cols[:, 2:3],
           "msq": cols[:, 3:4], "nv": cols[:, 4:5], "dust": dust[:]}

    t_xp = p.load("g", [(xp[s][:, D * c:D * (c + 1)],
                         xp_in[s, 128 * c:128 * (c + 1), :])
                        for s in range(8) for c in range(2)])
    t_eb = p.load("g", [(ebf[0:1, :], expb[0:1, :])])

    xn_toks = []
    for c in range(2):
        tacc = p.do("v", lambda v, c=c: (
            [v.tensor_copy(xacc[c][:], xp[0][:, D * c:D * (c + 1)])] +
            [v.tensor_add(xacc[c][:], xacc[c][:], xp[s][:, D * c:D * (c + 1)])
             for s in range(1, 8)] +
            [v.tensor_scalar_max(xacc[c][:], xacc[c][:], -33000.0),
             v.tensor_scalar_min(xacc[c][:], xacc[c][:], 65000.0)])[-1],
            deps=[t_xp])
        tln = ln_cols_emit(p, xacc[c][:], cols[:, 5:6], cols[:, 6:7], scr,
                           deps=[tacc, ctoks[1]])
        tap = p.do("v", lambda v, c=c: (
            v.tensor_scalar_sub(xacc[c][:], xacc[c][:], cols[:, 5:6]),
            v.tensor_scalar_mul(xacc[c][:], xacc[c][:], cols[:, 6:7]))[-1],
            deps=[tln])
        ttn = p.do("s", lambda s, c=c: s.activation(
            xn[c][:], xacc[c][:], AF.Tanh), deps=[tap])
        xn_toks.append(ttn)

    tr_last = None
    prev_ev = {}
    for c in range(2):
        for d in range(8):
            par = (c * 8 + d) % 2
            ttr = p.do("t", lambda t, c=c, d=d, par=par: t.transpose(
                ptr[par][:, 0:128],
                xn[c][:, 128 * d:128 * (d + 1)], p.ident[:]),
                deps=[xn_toks[c], ctoks[0], prev_ev.get((c * 8 + d) - 2)])
            tev = p.do("v", lambda v, c=c, d=d, par=par: v.tensor_copy(
                xnT[d][:, 128 * c:128 * (c + 1)],
                ptr[par][:, 0:128]), deps=[ttr])
            prev_ev[c * 8 + d] = tev
            tr_last = tev

    yt_stores = []
    sfree = {}
    ev_prev = {}
    for n in range(EOWN):
        last_mm = None
        for dd in range(8):
            sb_i = (n * 8 + dd) % 2
            tld = p.load("g", [(strip[sb_i][:], expw[n, 128 * dd:128 * (dd + 1), :])],
                         deps=[sfree.get(sb_i)])
            def ymm(t, n=n, dd=dd, sb_i=sb_i):
                last = None
                for hg in range(2):
                    last = t.matmul(pyt[0:32, 512 * hg:512 * (hg + 1)],
                                    xnT[dd][:, 32 * n:32 * (n + 1)],
                                    strip[sb_i][:, 512 * hg:512 * (hg + 1)],
                                    start=(dd == 0), stop=False)
                return last
            last_mm = p.do("t", ymm, deps=[tld, tr_last, ev_prev.get(n - 1)])
            sfree[sb_i] = last_mm
        tbias = p.do("t", lambda t, n=n: [
            t.matmul(pyt[0:32, 512 * hg:512 * (hg + 1)],
                     p.ones_row[0:1, 0:32],
                     ebf[0:1, D * n + 512 * hg:D * n + 512 * (hg + 1)],
                     start=False, stop=True)
            for hg in range(2)][-1], deps=[t_eb, ctoks[1], last_mm])
        tev = p.do("v", lambda v, n=n: v.tensor_copy(ytsb[:], pyt[0:32, :]),
                   deps=[tbias, yt_stores[-1] if yt_stores else None])
        ev_prev[n] = tev
        yt_stores.append(p.load("y", [(yt_out[n, :, :], ytsb[0:32, :])],
                                deps=[tev]))
    p.wait_only("y", yt_stores)
    return p.finish()



def build_ml3():
    p = P()
    Yt_i = p.din("YtA", [NPL, D])
    CT_i = p.din("CTi", [NPL, TOWN], BF16)
    s0_i = p.din("s0Ti", [D, TOWN], BF16)
    x_i = p.din("xTown", [D, TOWN])
    x1_o = p.dout("x1T", [D, TOWN])
    xn1_o = p.dout("xn1T", [D, TOWN], BF16)

    p.start()
    ctoks = consts_region(p)
    g0T = [p.sb(f"g0T{d}", [128, TOWN], BF16) for d in range(8)]

    rc = p.region()
    ytb = [p.rsb(rc, f"ytb{i}", [128, D], BF16) for i in range(16)]
    CTv = [p.rsb(rc, f"CTv{i}", [128, TOWN], BF16) for i in range(16)]
    pdis = p.rps(rc, "pdis", [128, 1024])
    t_yt = p.load("g", [(ytb[i][:], Yt_i[128 * i:128 * (i + 1), :])
                        for i in range(16)])
    t_ct = p.load("y", [(CTv[i][:], CT_i[128 * i:128 * (i + 1), :])
                        for i in range(16)])
    evA = {}
    for Hc in range(8):
        for tg in range(4):
            i = Hc * 4 + tg
            par = i % 2
            tmm = p.do("t", lambda t, Hc=Hc, tg=tg, par=par: [
                t.matmul(pdis[:, 512 * par:512 * (par + 1)],
                         ytb[npc][:, 128 * Hc:128 * (Hc + 1)],
                         CTv[npc][:, 512 * tg:512 * (tg + 1)],
                         start=(npc == 0), stop=(npc == 15))
                for npc in range(16)][-1],
                deps=[t_yt, t_ct, evA.get(i - 2)])
            evA[i] = p.do("s", lambda s, Hc=Hc, tg=tg, par=par: s.activation(
                g0T[Hc][:, 512 * tg:512 * (tg + 1)],
                pdis[:, 512 * par:512 * (par + 1)], AF.Silu), deps=[tmm])
    a_drain = [evA[31], evA[30]]
    p.close(rc)

    s0v = [p.sb(f"s0v{d}", [128, TOWN], BF16) for d in range(8)]
    x1f = [p.sb(f"x1f{d}", [128, TOWN], F32) for d in range(8)]
    xn1v = [p.sb(f"xn1v{d}", [128, TOWN], BF16) for d in range(8)]
    xb16 = p.sb("xb16", [128, 512], BF16)
    xsqB = p.sb("xsqB", [128, 512], BF16)
    rowsB = p.sb("rowsB", [1, 3072], F32)
    abfB = p.sb("abfB", [1, 1024], BF16)
    pstatB = p.psum("pstatB", [1, 1024])
    pabB = p.psum("pabB", [128, 1024])
    MEAN, MSQ, NV, TMP, AA, BB = 0, 512, 1024, 1536, 2048, 2560

    t_s0 = p.load("y", [(s0v[d][:], s0_i[128 * d:128 * (d + 1), :])
                        for d in range(8)], deps=a_drain)
    t_x = p.load("g", [(x1f[d][:], x_i[128 * d:128 * (d + 1), :])
                       for d in range(8)], deps=a_drain)
    x1_toks = []
    for d in range(8):
        tx = p.do("v", lambda v, d=d: (
            v.tensor_tensor(g0T[d][:], s0v[d][:], g0T[d][:], op=ALU.mult),
            v.tensor_tensor(x1f[d][:], x1f[d][:], g0T[d][:], op=ALU.add))[-1],
            deps=[t_s0, t_x, evA[31]])
        x1_toks.append(tx)
    x1_st = p.load("y", [(x1_o[128 * d:128 * (d + 1), :], x1f[d][:])
                         for d in range(8)], deps=x1_toks)

    xn1_toks = []
    sqd = None
    tgd = None
    tvd = None
    for tg in range(4):
        c0 = 512 * tg
        sq_toks = []
        for d in range(8):
            tq = p.do("s", lambda s, d=d, c0=c0: (
                s.activation(xb16[:], x1f[d][:, c0:c0 + 512], AF.Copy),
                s.activation(xsqB[:], x1f[d][:, c0:c0 + 512], AF.Square))[-1],
                deps=[x1_toks[d], sqd] if sqd else [x1_toks[d]])
            tmm = p.do("t", lambda t, d=d: (
                t.matmul(pstatB[0:1, 0:512], p.ones_col[:], xb16[:],
                         start=(d == 0), stop=(d == 7)),
                t.matmul(pstatB[0:1, 512:1024], p.ones_col[:], xsqB[:],
                         start=(d == 0), stop=(d == 7)))[-1],
                deps=[tq, ctoks[1], tvd])
            sqd = tmm
            sq_toks.append(tmm)
        tvd = p.do("v", lambda v: (
            v.tensor_scalar_mul(rowsB[0:1, MEAN:MEAN + 512],
                                pstatB[0:1, 0:512], 1.0 / D),
            v.tensor_scalar_mul(rowsB[0:1, MSQ:MSQ + 512],
                                pstatB[0:1, 512:1024], 1.0 / D),
            v.tensor_mul(rowsB[0:1, TMP:TMP + 512], rowsB[0:1, MEAN:MEAN + 512],
                         rowsB[0:1, MEAN:MEAN + 512]),
            v.tensor_sub(rowsB[0:1, NV:NV + 512], rowsB[0:1, TMP:TMP + 512],
                         rowsB[0:1, MSQ:MSQ + 512]))[-1],
            deps=[sq_toks[-1], tgd])
        tsd = p.do("s", lambda s: (
            s.activation(rowsB[0:1, NV:NV + 512], rowsB[0:1, NV:NV + 512],
                         AF.Ln, bias=p.epsc[0:1, 0:1], scale=-1.0),
            s.activation(rowsB[0:1, AA:AA + 512], rowsB[0:1, NV:NV + 512],
                         AF.Exp, scale=-0.5))[-1], deps=[tvd])
        tab = p.do("v", lambda v: (
            v.tensor_mul(rowsB[0:1, TMP:TMP + 512], rowsB[0:1, MEAN:MEAN + 512],
                         rowsB[0:1, AA:AA + 512]),
            v.tensor_scalar_mul(rowsB[0:1, BB:BB + 512],
                                rowsB[0:1, TMP:TMP + 512], -1.0),
            v.tensor_copy(abfB[0:1, 0:512], rowsB[0:1, AA:AA + 512]),
            v.tensor_copy(abfB[0:1, 512:1024], rowsB[0:1, BB:BB + 512]))[-1],
            deps=[tsd])
        tbc = p.do("t", lambda t: (
            t.matmul(pabB[:, 0:512], p.ones_row[0:1, :], abfB[0:1, 0:512],
                     start=True, stop=True),
            t.matmul(pabB[:, 512:1024], p.ones_row[0:1, :], abfB[0:1, 512:1024],
                     start=True, stop=True))[-1],
            deps=[tab, ctoks[1], tgd])
        tgd = p.do("v", lambda v, c0=c0: [
            (v.tensor_tensor(xn1v[d][:, c0:c0 + 512], x1f[d][:, c0:c0 + 512],
                             pabB[:, 0:512], op=ALU.mult),
             v.tensor_tensor(xn1v[d][:, c0:c0 + 512], xn1v[d][:, c0:c0 + 512],
                             pabB[:, 512:1024], op=ALU.add))
            for d in range(8)][-1][-1], deps=[tbc])
        xn1_toks.append(tgd)
    xn1_st = p.load("y", [(xn1_o[128 * d:128 * (d + 1), :], xn1v[d][:])
                          for d in range(8)], deps=xn1_toks)
    p.wait_only("y", [x1_st, xn1_st])
    return p.finish()



def emit_l1(p, ctoks, xn1_i, x1_i, wq2_i, WpT_i, bp_i, w1T_i, w2T_i,
            h1T_i, h2T_i, w3T_i, w1s_i, w2s_i, h1s_i, h2s_i, w3s_i, x2_o,
            entry_deps):
    xn1A = [p.sb(f"xn1A{d}", [128, TOWN], BF16) for d in range(8)]
    s1T = [p.sb(f"s1T{d}", [128, TOWN], BF16) for d in range(8)]
    t_xn1 = p.load("y", [(xn1A[d][:], xn1_i[128 * d:128 * (d + 1), 0:TOWN])
                         for d in range(8)], deps=entry_deps)

    rAtt = p.region()
    attn1T = [p.rsb(rAtt, f"attn1T{d}", [128, TOWN], BF16) for d in range(8)]
    rQ = p.region()
    qT = [p.rsb(rQ, f"qT{c}", [128, NT], BF16) for c in range(8)]
    wq2 = p.rsb(rQ, "wq2", [128, NH * DH], BF16)
    rXb = p.region()
    xn1B = [p.rsb(rXb, f"xn1B{d}", [128, TOWN], BF16) for d in range(8)]
    t_xn1b = p.load("y", [(xn1B[d][:], xn1_i[128 * d:128 * (d + 1), TOWN:NT])
                          for d in range(8)], deps=entry_deps)
    t_wq = p.load("g", [(wq2[:], wq2_i[:, :])], deps=entry_deps)

    rQp = p.region()
    pq = p.rps(rQp, "pq", [64, 1024])
    evq = {}
    for h in range(NH):
        hc, sub = h // 2, 64 * (h % 2)
        for tg in range(8):
            i = h * 8 + tg
            par = i % 2
            xsrc = (xn1A[hc] if tg < 4 else xn1B[hc])
            tgo = tg % 4
            tmm = p.do("t", lambda t, h=h, hc=hc, sub=sub, tgo=tgo, par=par,
                       xsrc=xsrc:
                t.matmul(pq[0:64, 512 * par:512 * (par + 1)],
                         wq2[sub:sub + 64, 64 * h:64 * (h + 1)],
                         xsrc[sub:sub + 64, 512 * tgo:512 * (tgo + 1)],
                         start=True, stop=True),
                deps=[t_wq, t_xn1, t_xn1b, evq.get(i - 2)])
            evq[i] = p.do("s", lambda s, hc=hc, sub=sub, tg=tg, par=par:
                s.activation(qT[hc][sub:sub + 64, 512 * tg:512 * (tg + 1)],
                             pq[0:64, 512 * par:512 * (par + 1)], AF.Copy),
                deps=[tmm])
    q_done = evq[NH * 8 - 1]
    p.close(rQp)
    p.close(rXb)

    v1b = p.rsb(rQ, "v1b", [128, 32 * 65], BF16)
    v1 = [v1b, v1b]
    aql0 = p.rsb(rQ, "aql0", [128, NT], BF16)
    escrT = p.rsb(rQ, "escrT", [128, 2048], BF16)
    escr = [escrT[:, 0:1024], escrT[:, 1024:2048]]
    arho = p.rsb(rQ, "arho", [128, 512], BF16)
    nrm = p.rsb(rQ, "nrm", [128, 16], F32)
    nrm2n = p.rsb(rQ, "nrm2n", [128, 16], F32)
    nrm8 = p.rsb(rQ, "nrm8", [128, 16], F32)
    stabc = p.rsb(rQ, "stabc", [128, 16], F32)
    r1a = p.rsb(rQ, "r1a", [128, 4], F32)
    r1scr = escrT
    avsb = p.rsb(rQ, "avsb", [64, 512], BF16)
    dnrow = p.rsb(rQ, "dnrow", [1, 512], BF16)
    dnrowf = p.rsb(rQ, "dnrowf", [1, 512], F32)
    pr1 = p.rps(rQ, "pr1", [128, 2048])
    pavt = p.rps(rQ, "pav", [128, 1024])
    pav = [pavt[:, 0:512], pavt[:, 512:1024]]
    pdnb = p.rps(rQ, "pdnb", [128, 1024])
    psc2 = [pr1[:, 0:1024], pr1[:, 1024:2048]]
    ptrv = [pr1[:, 0:512].bitcast(BF16), pr1[:, 512:1024].bitcast(BF16)]
    pdn = pdnb[0:64, 0:512]
    pstb = pdnb[0:1, 512:1024]
    vms = p.do("v", lambda v: (v.memset(v1b[:], 1.0),
                               v.memset(aql0[64:65, :], 1.0))[-1],
               deps=[q_done])

    v1use = {0: None, 1: None}
    aqluse = [None]
    exp_g1 = {}
    av_g1 = {}
    jc1 = [0]
    prev_norm1 = {0: (None, None), 1: (None, None)}
    prev_tcp = None
    prev_dn = None
    prev_tno = None
    prev_arho_tok = None
    for h in range(NH):
        hc, sub = h // 2, 64 * (h % 2)
        vb = h % 2
        idsl = p.ident[sub:sub + 64, sub:sub + 64]
        tr_toks = []
        for jg in range(8):
            par = jg % 2
            ttr = p.do("t", lambda t, hc=hc, sub=sub, jg=jg, par=par, idsl=idsl: [
                t.transpose(ptrv[par][:, 64 * q:64 * q + 64],
                            qT[hc][sub:sub + 64,
                                   128 * (4 * jg + q):128 * (4 * jg + q) + 128],
                            idsl)
                for q in range(4)][-1],
                deps=[q_done, ctoks[0], prev_tcp, prev_dn,
                      tr_toks[-2] if len(tr_toks) >= 2 else None])
            tev = p.do("v", lambda v, jg=jg, par=par: v.tensor_copy(
                v1b[:].rearrange("p (a b) -> p a b", a=32, b=65)
                [:, 4 * jg:4 * jg + 4, 0:64],
                ptrv[par][:, 0:256].rearrange("p (a b) -> p a b", a=4, b=64)),
                deps=[ttr, vms, v1use[0], v1use[1]])
            tr_toks.append(tev)
        v1_ready = tr_toks[-1]
        tnrm = p.do("v", lambda v, vb=vb: [
            v.scalar_tensor_tensor(r1scr[:, 0:64], v1[vb][:, 65 * j:65 * j + 64],
                                   1.0, v1[vb][:, 65 * j:65 * j + 64],
                                   op0=ALU.mult, op1=ALU.mult,
                                   accum_out=nrm[:, j:j + 1])
            for j in range(16)][-1], deps=[v1_ready])
        tnsc = p.do("s", lambda s: (
            s.activation(nrm2n[:], nrm[:], AF.Copy, scale=-2.0),
            s.activation(nrm8[:], nrm[:], AF.Copy, scale=8.0))[-1],
            deps=[tnrm])
        taq = p.do("v", lambda v, hc=hc, sub=sub: v.tensor_copy(
            aql0[0:64, :], qT[hc][sub:sub + 64, :]),
            deps=[q_done, vms, aqluse[0]])

        r1_last = None
        for qt in range(16):
            ex_toks = []
            for half in range(2):
                tmm = p.do("t", lambda t, hc=hc, sub=sub, qt=qt, half=half: [
                    t.matmul(pr1[:, 512 * u:512 * (u + 1)],
                             qT[hc][sub:sub + 64, 128 * qt:128 * (qt + 1)],
                             qT[hc][sub:sub + 64,
                                    2048 * half + 512 * u:
                                    2048 * half + 512 * (u + 1)],
                             start=True, stop=True)
                    for u in range(4)][-1],
                    deps=[q_done, tr_toks[-1],
                          ex_toks[-1] if ex_toks else r1_last,
                          prev_tcp if qt == 0 else None,
                          prev_dn if qt == 0 else None])
                tex = p.do("s", lambda s, qt=qt, half=half: s.activation(
                    r1scr[:, :], pr1[:, :], AF.Exp, bias=nrm2n[:, qt:qt + 1],
                    scale=2.0, accum_out=r1a[:, half:half + 1]),
                    deps=[tmm, tnsc])
                ex_toks.append(tex)
            tsum = p.do("v", lambda v: v.tensor_add(
                r1a[:, 2:3], r1a[:, 0:1], r1a[:, 1:2]), deps=ex_toks)
            tln = p.do("s", lambda s: s.activation(
                r1a[:, 3:4], r1a[:, 2:3], AF.Ln), deps=[tsum])
            r1_last = p.do("v", lambda v, qt=qt: v.scalar_tensor_tensor(
                stabc[:, qt:qt + 1], r1a[:, 3:4], 4.0, nrm8[:, qt:qt + 1],
                op0=ALU.mult, op1=ALU.add), deps=[tln])
        r1_all = r1_last

        for g in range(4):
            qs = 512 * g
            gp = g % 2
            tstb = p.do("t", lambda t, g=g: [
                t.transpose(pdnb[0:1, 512 + 128 * k:640 + 128 * k],
                            stabc[:, 4 * g + k:4 * g + k + 1],
                            p.identf[:, :])
                for k in range(4)][-1],
                deps=[r1_all, ctoks[0], p.identf_tok, prev_tcp, prev_dn])
            tar = p.do("v", lambda v, hc=hc, sub=sub, qs=qs: v.tensor_copy(
                arho[0:64, :], qT[hc][sub:sub + 64, qs:qs + 512]),
                deps=[q_done, prev_arho_tok])
            tar2 = p.do("s", lambda s: s.activation(
                arho[64:65, :], pdnb[0:1, 512:1024], AF.Copy, scale=-0.125),
                deps=[tstb, tar, prev_arho_tok])
            av_last = None
            for jj in range(16):
                par = jj % 2
                def smm(t, jj=jj, par=par):
                    last = None
                    for u in range(2):
                        j = 2 * jj + u
                        last = t.matmul(psc2[par][:, 512 * u:512 * (u + 1)],
                                        aql0[0:65, 128 * j:128 * (j + 1)],
                                        arho[0:65, :], start=True, stop=True)
                    return last
                tsc = p.do("t", smm, deps=[tar2, taq, exp_g1.get(jc1[0] - 2)])
                texp = p.do("s", lambda s, par=par: s.activation(
                    escr[par][:], psc2[par][:, :], AF.Exp, scale=8.0),
                    deps=[tsc, av_g1.get(jc1[0] - 2)])
                exp_g1[jc1[0]] = texp
                def avmm(t, jj=jj, par=par, vb=vb, gp=gp):
                    last = None
                    for u in range(2):
                        j = 2 * jj + u
                        last = t.matmul(pav[gp][0:65, :],
                                        v1[vb][:, 65 * j:65 * (j + 1)],
                                        escr[par][:, 512 * u:512 * (u + 1)],
                                        start=(j == 0), stop=(j == 31))
                    return last
                avdeps = [texp, v1_ready]
                if jj == 0:
                    avdeps += list(prev_norm1[0]) + list(prev_norm1[1])
                av_last = p.do("t", avmm, deps=avdeps)
                av_g1[jc1[0]] = av_last
                jc1[0] += 1
            v1use[vb] = av_last
            aqluse[0] = av_last
            prev_arho_tok = av_last
            tcp = p.do("s", lambda s, gp=gp: s.activation(
                avsb[:], pav[gp][0:64, :], AF.Copy),
                deps=[av_last, prev_tno])
            tdn = p.do("s", lambda s, gp=gp: (
                s.activation(dnrowf[0:1, :], pav[gp][64:65, :], AF.Ln),
                s.activation(dnrow[0:1, :], dnrowf[0:1, :], AF.Exp,
                             scale=-1.0))[-1], deps=[av_last, prev_tno])
            tbc = p.do("t", lambda t: t.matmul(
                pdn, p.ones_row[0:1, 0:64], dnrow[0:1, :],
                start=True, stop=True), deps=[tdn, ctoks[1], prev_tno])
            tno = p.do("v", lambda v, hc=hc, sub=sub, qs=qs: v.tensor_tensor(
                attn1T[hc][sub:sub + 64, qs:qs + 512],
                avsb[:], pdn, op=ALU.mult), deps=[tbc, tcp])
            prev_norm1[gp] = (tcp, tdn)
            prev_tcp, prev_dn, prev_tno = tcp, tdn, tno
    attn_done = prev_tno
    p.close(rQ)

    rWp = p.region()
    wp = [p.rsb(rWp, f"wp{d}", [128, D], BF16) for d in range(8)]
    ones512 = p.rsb(rWp, "ones512", [1, 512], BF16)
    t_o512 = p.do("v", lambda v: v.memset(ones512[:], 1.0), deps=[attn_done])
    bpv = p.rsb(rWp, "bpv", [1, D], BF16)
    pwp = p.rps(rWp, "pwp", [128, 1024])
    wp_tok = p.load("g", [(wp[d][:], WpT_i[128 * d:128 * (d + 1), :])
                          for d in range(8)] + [(bpv[0:1, :], bp_i[0:1, :])],
                    deps=[attn_done])
    evW = {}
    for og in range(8):
        for tg in range(4):
            i = og * 4 + tg
            par = i % 2
            def wpmm(t, og=og, tg=tg, par=par):
                for d in range(8):
                    t.matmul(pwp[:, 512 * par:512 * (par + 1)],
                             wp[d][:, 128 * og:128 * (og + 1)],
                             attn1T[d][:, 512 * tg:512 * (tg + 1)],
                             start=(d == 0), stop=False)
                return t.matmul(pwp[:, 512 * par:512 * (par + 1)],
                                bpv[0:1, 128 * og:128 * (og + 1)],
                                ones512[0:1, :], start=False, stop=True)
            tmm = p.do("t", wpmm, deps=[wp_tok, attn_done, t_o512,
                                        evW.get(i - 2)])
            evW[i] = p.do("s", lambda s, og=og, tg=tg, par=par: s.activation(
                s1T[og][:, 512 * tg:512 * (tg + 1)],
                pwp[:, 512 * par:512 * (par + 1)], AF.Copy), deps=[tmm])
    wp_drain = [evW[31], evW[30]]
    p.close(rWp)

    rF = p.region()
    bufA = [p.rsb(rF, f"bufA{i}", [128, 512], BF16) for i in range(32)]
    bufB = [p.rsb(rF, f"bufB{i}", [128, 512], BF16) for i in range(32)]
    g1b = [p.rsb(rF, f"g1b{i}", [128, 512], BF16) for i in range(8)]
    strw = [p.rsb(rF, f"strw{i}", [128, 1024], BF16) for i in range(6)]
    r_tmp = [p.rsb(rF, f"rtmp{i}", [128, 512], BF16) for i in range(2)]
    scw = p.rsb(rF, "scw", [128, 128], F32)
    w3sv = p.rsb(rF, "w3sv", [128, 8], F32)
    xch = p.rsb(rF, "xch", [128, 512], F32)
    x2sb = p.rsb(rF, "x2sb", [128, 512], F32)
    pf = p.rps(rF, "pf", [128, 8 * 512])

    t_sc = p.load("y", [
        (scw[:, 0:32], w1s_i[:, :]), (scw[:, 32:64], w2s_i[:, :]),
        (scw[:, 64:96], h1s_i[:, :]), (scw[:, 96:128], h2s_i[:, :]),
        (w3sv[:, :], w3s_i[:, :])], deps=wp_drain)

    x2_stores = []
    str_free = {}
    sidx = [0]
    prev_round = [None, None, None]

    def stream_mm(wdram, ocg, rhs_deps, prev_evs, rhs_fn, n_dc, evac_fn):
        mm_tok = None
        for dc in range(n_dc):
            si = sidx[0] % 6
            tld = p.load("g",
                         [(strw[si][:],
                           wdram[128 * dc:128 * (dc + 1),
                                 1024 * ocg:1024 * (ocg + 1)])],
                         deps=[str_free.get(si)])
            sidx[0] += 1
            def mmf(t, dc=dc, si=si):
                last = None
                for oo in range(8):
                    last = t.matmul(pf[:, 512 * oo:512 * (oo + 1)],
                                    strw[si][:, 128 * oo:128 * (oo + 1)],
                                    rhs_fn(dc),
                                    start=(dc == 0), stop=(dc == n_dc - 1))
                return last
            mm_tok = p.do("t", mmf,
                          deps=[tld] + rhs_deps +
                          (prev_evs if dc == 0 else []))
            str_free[si] = mm_tok
        return [evac_fn(ocg, oo, mm_tok) for oo in range(8)]

    for tgi in range(4):
        ts0 = 512 * tgi
        pe0 = [t_sc, evW[31]] + [x for x in prev_round if x is not None]
        evA1 = []
        for ocg in range(4):
            def ev1(ocg, oo, mmt):
                oc = 8 * ocg + oo
                return p.do("s", lambda s, oc=oc, oo=oo: s.activation(
                    bufA[oc][:], pf[:, 512 * oo:512 * (oo + 1)], AF.Silu,
                    scale=scw[:, oc:oc + 1]),
                    deps=[mmt, t_sc, prev_round[0]])
            evA1 += stream_mm(w1T_i, ocg, [t_xn1],
                              (pe0 if ocg == 0 else evA1[-8:]),
                              lambda dc, ts0=ts0: xn1A[dc][:, ts0:ts0 + 512],
                              8, ev1)
        evB1 = []
        ev2_prev = {}
        for ocg in range(4):
            def ev2(ocg, oo, mmt):
                oc = 8 * ocg + oo
                ts = p.do("s", lambda s, oc=oc, oo=oo: s.activation(
                    r_tmp[oc % 2][:], pf[:, 512 * oo:512 * (oo + 1)], AF.Copy,
                    scale=scw[:, 32 + oc:33 + oc]),
                    deps=[mmt, t_sc, ev2_prev.get(oc % 2)])
                tv = p.do("v", lambda v, oc=oc: v.tensor_tensor(
                    bufB[oc][:], bufA[oc][:], r_tmp[oc % 2][:], op=ALU.mult),
                    deps=[ts, evA1[oc], prev_round[1]])
                ev2_prev[oc % 2] = tv
                return tv
            evB1 += stream_mm(w2T_i, ocg, [t_xn1],
                              (evA1[-8:] if ocg == 0 else evB1[-8:]),
                              lambda dc, ts0=ts0: xn1A[dc][:, ts0:ts0 + 512],
                              8, ev2)
        evG1 = []
        for ocg in range(4):
            def ev3(ocg, oo, mmt):
                oc = 8 * ocg + oo
                return p.do("s", lambda s, oc=oc, oo=oo: s.activation(
                    bufA[oc][:], pf[:, 512 * oo:512 * (oo + 1)], AF.Gelu,
                    scale=scw[:, 64 + oc:65 + oc]),
                    deps=[mmt, t_sc, evB1[-1]])
            evG1 += stream_mm(h1T_i, ocg, [evB1[-1]],
                              (evB1[-8:] if ocg == 0 else evG1[-8:]),
                              lambda dc: bufB[dc][:], 32, ev3)
        evG2 = []
        for ocg in range(4):
            def ev4(ocg, oo, mmt):
                oc = 8 * ocg + oo
                return p.do("s", lambda s, oc=oc, oo=oo: s.activation(
                    bufB[oc][:], pf[:, 512 * oo:512 * (oo + 1)], AF.Gelu,
                    scale=scw[:, 96 + oc:97 + oc]),
                    deps=[mmt, t_sc, evG1[-1]])
            evG2 += stream_mm(h2T_i, ocg, [evG1[-1]],
                              (evG1[-8:] if ocg == 0 else evG2[-8:]),
                              lambda dc: bufA[dc][:], 32, ev4)
        def ev5(ocg, oo, mmt):
            return p.do("s", lambda s, oo=oo: s.activation(
                g1b[oo][:], pf[:, 512 * oo:512 * (oo + 1)], AF.Silu,
                scale=w3sv[:, oo:oo + 1]),
                deps=[mmt, t_sc, evG2[-1], prev_round[2]])
        evG3 = stream_mm(w3T_i, 0, [evG2[-1]], evG2[-8:],
                         lambda dc: bufB[dc][:], 32, ev5)
        prev_g = None
        for dc in range(8):
            tlx = p.load("y", [(xch[:], x1_i[128 * dc:128 * (dc + 1),
                                             ts0:ts0 + 512])],
                         deps=[prev_g])
            tg2 = p.do("v", lambda v, dc=dc, ts0=ts0: (
                v.tensor_tensor(g1b[dc][:], s1T[dc][:, ts0:ts0 + 512],
                                g1b[dc][:], op=ALU.mult),
                v.tensor_tensor(x2sb[:], xch[:], g1b[dc][:], op=ALU.add))[-1],
                deps=[tlx, evG3[dc], evW[31],
                      x2_stores[-1] if x2_stores else None])
            tsx = p.load("y", [(x2_o[128 * dc:128 * (dc + 1), ts0:ts0 + 512],
                                x2sb[:])], deps=[tg2])
            x2_stores.append(tsx)
            prev_g = tg2
        prev_round = [evG2[-1], evG3[-1], x2_stores[-1]]
    p.close(rF)
    p.wait_only("y", x2_stores[-8:])

def build_ml4():
    p = P()
    xn1_i = p.din("xn1f", [D, NT], BF16)
    x1_i = p.din("x1Ti", [D, TOWN])
    wq2_i = p.din("Wq2", [128, NH * DH])
    WpT_i = p.din("WpT", [D, D])
    bp_i = p.din("bpr", [1, D])
    w1T_i = p.din("w1T", [D, DHID])
    w2T_i = p.din("w2T", [D, DHID])
    h1T_i = p.din("h1T", [DHID, DHID])
    h2T_i = p.din("h2T", [DHID, DHID])
    w3T_i = p.din("w3T", [DHID, D])
    w1s_i = p.din("w1sc", [128, 32])
    w2s_i = p.din("w2sc", [128, 32])
    h1s_i = p.din("h1sc", [128, 32])
    h2s_i = p.din("h2sc", [128, 32])
    w3s_i = p.din("w3sc", [128, 8])
    x2_o = p.dout("x2T", [D, TOWN])

    p.start()
    ctoks = consts_region(p, want_identf=True)
    ctoks = [ctoks[0], ctoks[1]]
    emit_l1(p, ctoks, xn1_i, x1_i, wq2_i, WpT_i, bp_i, w1T_i, w2T_i,
            h1T_i, h2T_i, w3T_i, w1s_i, w2s_i, h1s_i, h2s_i, w3s_i, x2_o, [])
    return p.finish()



def build_fused(dbg=False):
    p = P()
    xTa = p.din("xTa", [D, 4 * NT])
    xta = p.din("xta", [4 * NT, D])
    WqkvT_i = p.din("WqkvT", [D, D])
    WoT_i = p.din("WoT", [D, D])
    phiT_i = p.din("phiT", [D, NPL])
    phibr_i = p.din("phibr", [1, NPL])
    phibc_i = p.din("phibc", [128, 16])
    IndP_i = p.din("IndP", [128, 32])
    IndPT_i = p.din("IndPT", [32, 128])
    expw_i = p.din("expwA", [NEXP, D, D])
    expb_i = p.din("expbA", [1, NEXP * D])
    wq2_i = p.din("Wq2", [128, NH * DH])
    WpT_i = p.din("WpT", [D, D])
    bp_i = p.din("bpr", [1, D])
    w1T_i = p.din("w1T", [D, DHID])
    w2T_i = p.din("w2T", [D, DHID])
    h1T_i = p.din("h1T", [DHID, DHID])
    h2T_i = p.din("h2T", [DHID, DHID])
    w3T_i = p.din("w3T", [DHID, D])
    w1s_i = p.din("w1sc", [128, 32])
    w2s_i = p.din("w2sc", [128, 32])
    h1s_i = p.din("h1sc", [128, 32])
    h2s_i = p.din("h2sc", [128, 32])
    w3s_i = p.din("w3sc", [128, 8])
    x2_o = p.dout("x2T", [D, TOWN])
    if dbg:
        dbg_xn0 = p.dout("dbg_xn0", [D, NT], BF16)
        dbg_s0 = p.dout("dbg_s0", [D, NT], BF16)
        dbg_X = p.dout("dbg_X", [NPL, D])
        dbg_yt = p.dout("dbg_yt", [NPL, D], BF16)
        dbg_x1 = p.dout("dbg_x1", [D, TOWN])
        dbg_xn1 = p.dout("dbg_xn1", [D, NT], BF16)
        dbg_xn6 = p.dout("dbg_xn6", [NPL, D], BF16)
        dbg_xnt = p.dout("dbg_xnt", [D, NPL], BF16)

    xn0_sp = p.nc.dram_tensor("xn0_sp", [D, NT], BF16, kind="Internal").ap()
    s0_sp = p.nc.dram_tensor("s0_sp", [D, NT], BF16, kind="Internal").ap()
    yt_sp = p.nc.dram_tensor("yt_sp", [NPL, D], BF16, kind="Internal").ap()
    x1_sp = p.nc.dram_tensor("x1_sp", [D, TOWN], F32, kind="Internal").ap()
    xn1_sp = p.nc.dram_tensor("xn1_sp", [D, NT], BF16, kind="Internal").ap()

    p.start()
    ctoks = consts_region(p, want_identf=True)
    MEAN, MSQ, NV, TMP, AA, BB = 0, 512, 1024, 1536, 2048, 2560

    rX0 = p.region()
    xn0T = [p.rsb(rX0, f"xn0T{d}", [128, NT], BF16) for d in range(8)]
    r1 = p.region()
    xball = p.rsb(r1, "xball", [128, NT], BF16)
    xsq = p.rsb(r1, "xsq", [128, NT], BF16)
    rowscr = [p.rsb(r1, f"rowscr{i}", [1, 3072], F32) for i in range(2)]
    abf = [p.rsb(r1, f"abf{i}", [1, 1024], BF16) for i in range(2)]
    pstat = [p.rps(r1, f"pstat{i}", [1, 1024]) for i in range(2)]
    pab = [p.rps(r1, f"pab{i}", [128, 1024]) for i in range(2)]

    def emit_row_ln(srcT, n_tg, dst_write, extra_ld_deps, loader_eng="g"):
        tg_done = [None] * n_tg
        sq_done = [None] * n_tg
        mm_done = [None] * n_tg
        tv_done = [None] * n_tg
        ts_done = [None] * n_tg
        tbc_done = [None] * n_tg
        for tg in range(n_tg):
            c0 = 512 * tg
            pr = tg % 2
            ldeps = list(extra_ld_deps) + [ctoks[1]] + (
                [sq_done[tg - 1], mm_done[tg - 1], tg_done[tg - 1]] if tg else [])
            tld = p.load(loader_eng, [(xball[:, 512 * d:512 * (d + 1)],
                                       srcT[128 * d:128 * (d + 1), c0:c0 + 512])
                                      for d in range(8)], deps=ldeps)
            sq_done[tg] = p.do("s", lambda s: [
                s.activation(xsq[:, 512 * d:512 * (d + 1)],
                             xball[:, 512 * d:512 * (d + 1)], AF.Square)
                for d in range(8)][-1], deps=[tld])

            def statmm(t, pr=pr):
                for d in range(8):
                    t.matmul(pstat[pr][0:1, 0:512], p.ones_col[:],
                             xball[:, 512 * d:512 * (d + 1)],
                             start=(d == 0), stop=(d == 7))
                last = None
                for d in range(8):
                    last = t.matmul(pstat[pr][0:1, 512:1024], p.ones_col[:],
                                    xsq[:, 512 * d:512 * (d + 1)],
                                    start=(d == 0), stop=(d == 7))
                return last
            mm_done[tg] = p.do("t", statmm,
                               deps=[tld, sq_done[tg], ctoks[1],
                                     tv_done[tg - 2] if tg >= 2 else None])

            def rmath_v(v, pr=pr):
                v.tensor_scalar_mul(rowscr[pr][0:1, MEAN:MEAN + 512],
                                    pstat[pr][0:1, 0:512], 1.0 / D)
                v.tensor_scalar_mul(rowscr[pr][0:1, MSQ:MSQ + 512],
                                    pstat[pr][0:1, 512:1024], 1.0 / D)
                v.tensor_mul(rowscr[pr][0:1, TMP:TMP + 512],
                             rowscr[pr][0:1, MEAN:MEAN + 512],
                             rowscr[pr][0:1, MEAN:MEAN + 512])
                return v.tensor_sub(rowscr[pr][0:1, NV:NV + 512],
                                    rowscr[pr][0:1, TMP:TMP + 512],
                                    rowscr[pr][0:1, MSQ:MSQ + 512])
            tv_done[tg] = p.do("v", rmath_v,
                               deps=[mm_done[tg],
                                     ts_done[tg - 2] if tg >= 2 else None])

            def rmath_s(s, pr=pr):
                s.activation(rowscr[pr][0:1, NV:NV + 512],
                             rowscr[pr][0:1, NV:NV + 512],
                             AF.Ln, bias=p.epsc[0:1, 0:1], scale=-1.0)
                return s.activation(rowscr[pr][0:1, AA:AA + 512],
                                    rowscr[pr][0:1, NV:NV + 512],
                                    AF.Exp, scale=-0.5)
            ts_done[tg] = p.do("s", rmath_s, deps=[tv_done[tg]])
            tab = p.do("v", lambda v, pr=pr: (
                v.tensor_mul(rowscr[pr][0:1, TMP:TMP + 512],
                             rowscr[pr][0:1, MEAN:MEAN + 512],
                             rowscr[pr][0:1, AA:AA + 512]),
                v.tensor_scalar_mul(rowscr[pr][0:1, BB:BB + 512],
                                    rowscr[pr][0:1, TMP:TMP + 512], -1.0),
                v.tensor_copy(abf[pr][0:1, 0:512],
                              rowscr[pr][0:1, AA:AA + 512]),
                v.tensor_copy(abf[pr][0:1, 512:1024],
                              rowscr[pr][0:1, BB:BB + 512]))[-1],
                deps=[ts_done[tg],
                      tbc_done[tg - 2] if tg >= 2 else None])
            tbc = p.do("t", lambda t, pr=pr: (
                t.matmul(pab[pr][:, 0:512], p.ones_row[0:1, :],
                         abf[pr][0:1, 0:512], start=True, stop=True),
                t.matmul(pab[pr][:, 512:1024], p.ones_row[0:1, :],
                         abf[pr][0:1, 512:1024], start=True, stop=True))[-1],
                deps=[tab, ctoks[1], tg_done[tg - 2] if tg >= 2 else None])
            tbc_done[tg] = tbc
            tg_done[tg] = dst_write(tg, c0, [tbc], pr)
        return tg_done, xball

    def xn0_write(tg, c0, deps, pr):
        return p.do("v", lambda v: [
            (v.tensor_tensor(xn0T[d][:, c0:c0 + 512],
                             xball[:, 512 * d:512 * (d + 1)],
                             pab[pr][:, 0:512], op=ALU.mult),
             v.tensor_tensor(xn0T[d][:, c0:c0 + 512], xn0T[d][:, c0:c0 + 512],
                             pab[pr][:, 512:1024], op=ALU.add))
            for d in range(8)][-1][-1], deps=deps)

    tg_done, _ = emit_row_ln(xTa, 8, xn0_write, [])
    xn0_spt = p.load("y", [(xn0_sp[128 * d:128 * (d + 1), :], xn0T[d][:])
                           for d in range(8)], deps=tg_done)
    p.close(r1)

    attn_sp = p.nc.dram_tensor("attn_sp", [D, NT], BF16, kind="Internal").ap()
    rA0 = p.region()
    qkvT = [p.rsb(rA0, f"qkvT{c}", [128, NT], BF16) for c in range(8)]
    rW0 = p.region()
    wqkv = [p.rsb(rW0, f"wqkv{d}", [128, D], BF16) for d in range(8)]
    pqk = p.rps(rW0, "pqk", [128, 1024])
    w_tok = p.load("g", [(wqkv[d][:], WqkvT_i[128 * d:128 * (d + 1), :])
                         for d in range(8)],
                   deps=tg_done + [xn0_spt])
    p1_drain = [tg_done[7]]
    ev2 = {}
    for og in range(8):
        for tg in range(8):
            i = og * 8 + tg
            par = i % 2
            tmm = p.do("t", lambda t, og=og, tg=tg, par=par: [
                t.matmul(pqk[:, 512 * par:512 * (par + 1)],
                         wqkv[d][:, 128 * og:128 * (og + 1)],
                         xn0T[d][:, 512 * tg:512 * (tg + 1)],
                         start=(d == 0), stop=(d == 7))
                for d in range(8)][-1],
                deps=[w_tok, tg_done[tg], ev2.get(i - 2)] + p1_drain)
            ev2[i] = p.do("s", lambda s, og=og, tg=tg, par=par: s.activation(
                qkvT[og][:, 512 * tg:512 * (tg + 1)],
                pqk[:, 512 * par:512 * (par + 1)], AF.Copy),
                deps=[tmm])
    qkv_done = ev2[63]
    p.close(rW0)

    attnc = [p.rsb(rA0, f"attnc{i}", [128, NT], BF16) for i in range(2)]
    v1_0 = [p.rsb(rA0, f"v1z{i}", [128, 32 * 65], BF16) for i in range(2)]
    escr0 = [p.rsb(rA0, f"escrz{i}", [128, 1024], BF16) for i in range(2)]
    avsb0 = p.rsb(rA0, "avsbz", [64, 512], BF16)
    dnrow0 = p.rsb(rA0, "dnrowz", [1, 512], BF16)
    dnrowf0 = p.rsb(rA0, "dnrowfz", [1, 512], F32)
    ptr0t = p.rps(rA0, "ptrz", [128, 1024], BF16)
    ptr0 = [ptr0t[:, 0:512], ptr0t[:, 512:1024]]
    psc0 = [p.rps(rA0, f"pscz{i}", [128, 1024]) for i in range(2)]
    pav0t = p.rps(rA0, "pavz", [128, 1024])
    pav0 = [pav0t[:, 0:512], pav0t[:, 512:1024]]
    pdn0 = p.rps(rA0, "pdnz", [64, 512])
    vms0 = p.do("v", lambda v: (v.memset(v1_0[0][:], 1.0),
                                v.memset(v1_0[1][:], 1.0))[-1], deps=[qkv_done])

    v1use = {0: None, 1: None}
    prev_tcp = None
    prev_trc = None
    prev_tno = None
    attn_spills = {}
    exp_g0 = {}
    av_g0 = {}
    jc0 = [0]
    for h in range(NH):
        hc, sub = h // 2, 64 * (h % 2)
        vb = h % 2
        idsl = p.ident[sub:sub + 64, sub:sub + 64]
        tr_toks = []
        for jg in range(8):
            par = jg % 2
            ttr = p.do("t", lambda t, hc=hc, sub=sub, jg=jg, par=par, idsl=idsl: [
                t.transpose(ptr0[par][:, 64 * q:64 * q + 64],
                            qkvT[hc][sub:sub + 64,
                                     128 * (4 * jg + q):128 * (4 * jg + q) + 128],
                            idsl)
                for q in range(4)][-1],
                deps=[qkv_done, ctoks[0],
                      tr_toks[-1] if tr_toks else None])
            tev = p.do("v", lambda v, jg=jg, par=par, vb=vb: v.tensor_copy(
                v1_0[vb][:].rearrange("p (a b) -> p a b", a=32, b=65)
                [:, 4 * jg:4 * jg + 4, 0:64],
                ptr0[par][:, 0:256].rearrange("p (a b) -> p a b", a=4, b=64)),
                deps=[ttr, vms0, v1use[vb]])
            tr_toks.append(tev)
        v1_ready = tr_toks[-1]
        prev_norm = {0: (None, None), 1: (None, None)}
        for g in range(8):
            qs = 512 * g
            gp = g % 2
            av_last = None
            for jj in range(16):
                par = jj % 2
                def smm(t, hc=hc, sub=sub, jj=jj, par=par, qs=qs):
                    last = None
                    for u in range(2):
                        j = 2 * jj + u
                        last = t.matmul(psc0[par][:, 512 * u:512 * (u + 1)],
                                        qkvT[hc][sub:sub + 64,
                                                 128 * j:128 * (j + 1)],
                                        qkvT[hc][sub:sub + 64, qs:qs + 512],
                                        start=True, stop=True)
                    return last
                tsc = p.do("t", smm, deps=[qkv_done, exp_g0.get(jc0[0] - 2)])
                texp = p.do("s", lambda s, par=par: s.activation(
                    escr0[par][:], psc0[par][:], AF.Exp,
                    bias=p.c20[:, 0:1], scale=0.125),
                    deps=[tsc, ctoks[1], av_g0.get(jc0[0] - 2)])
                exp_g0[jc0[0]] = texp
                def avmm(t, jj=jj, par=par, vb=vb, gp=gp):
                    last = None
                    for u in range(2):
                        j = 2 * jj + u
                        last = t.matmul(pav0[gp][0:65, :],
                                        v1_0[vb][:, 65 * j:65 * (j + 1)],
                                        escr0[par][:, 512 * u:512 * (u + 1)],
                                        start=(j == 0), stop=(j == 31))
                    return last
                avdeps = [texp, v1_ready]
                if jj == 0:
                    avdeps += list(prev_norm[0]) + list(prev_norm[1])
                av_last = p.do("t", avmm, deps=avdeps)
                av_g0[jc0[0]] = av_last
                jc0[0] += 1
            v1use[vb] = av_last
            tcp = p.do("s", lambda s, gp=gp: s.activation(
                avsb0[:], pav0[gp][0:64, :], AF.Copy),
                deps=[av_last, prev_tno])
            trc = p.do("s", lambda s, gp=gp: (
                s.activation(dnrowf0[0:1, :], pav0[gp][64:65, :], AF.Ln),
                s.activation(dnrow0[0:1, :], dnrowf0[0:1, :], AF.Exp,
                             scale=-1.0))[-1], deps=[av_last, prev_tno])
            tbc = p.do("t", lambda t: t.matmul(
                pdn0[:, :], p.ones_row[0:1, 0:64], dnrow0[0:1, :],
                start=True, stop=True), deps=[trc, ctoks[1], prev_tno])
            tno = p.do("v", lambda v, hc=hc, sub=sub, qs=qs: v.tensor_tensor(
                attnc[hc % 2][sub:sub + 64, qs:qs + 512],
                avsb0[:], pdn0[:, :], op=ALU.mult),
                deps=[tbc, tcp, attn_spills.get(hc - 2)])
            prev_norm[gp] = (tcp, trc)
            prev_tcp, prev_trc, prev_tno = tcp, trc, tno
        if h % 2 == 1:
            attn_spills[hc] = p.load(
                "y", [(attn_sp[128 * hc:128 * (hc + 1), :], attnc[hc % 2][:])],
                deps=[prev_tno])
    attn_done = attn_spills[7]
    p.close(rA0)
    p.close(rX0)

    rF4 = p.region()
    wo = [p.rsb(rF4, f"wo{d}", [128, D], BF16) for d in range(8)]
    s0sb = p.rsb(rF4, "s0sb", [128, 512], BF16)
    arh = [p.rsb(rF4, f"arh{i}", [128, 512], BF16) for i in range(2)]
    pwo = p.rps(rF4, "pwo", [128, 8 * 512])
    wo_tok = p.load("g", [(wo[d][:], WoT_i[128 * d:128 * (d + 1), :])
                          for d in range(8)],
                    deps=[attn_done, prev_tno, prev_tcp])
    s0_store = []
    arh_free = {}
    ev4 = {}
    for tg in range(8):
        mm_last = None
        for d in range(8):
            par = (tg * 8 + d) % 2
            tla = p.load("y", [(arh[par][:],
                                attn_sp[128 * d:128 * (d + 1),
                                        512 * tg:512 * (tg + 1)])],
                         deps=[arh_free.get(par), attn_spills[d]])
            def womm(t, d=d, par=par):
                last = None
                for og in range(8):
                    last = t.matmul(pwo[:, 512 * og:512 * (og + 1)],
                                    wo[d][:, 128 * og:128 * (og + 1)],
                                    arh[par][:],
                                    start=(d == 0), stop=(d == 7))
                return last
            mm_last = p.do("t", womm,
                           deps=[tla, wo_tok] +
                           ([ev4.get(tg - 1)] if d == 0 else []))
            arh_free[par] = mm_last
        evs = []
        for og in range(8):
            tev = p.do("s", lambda s, og=og: s.activation(
                s0sb[:], pwo[:, 512 * og:512 * (og + 1)], AF.Copy),
                deps=[mm_last, s0_store[-1] if s0_store else None])
            tst = p.load("y", [(s0_sp[128 * og:128 * (og + 1),
                                      512 * tg:512 * (tg + 1)], s0sb[:])],
                         deps=[tev])
            s0_store.append(tst)
            evs.append(tev)
        ev4[tg] = evs[-1]
    p4_drain = [s0_store[-1], ev4[7]]
    p.close(rF4)

    rXa = p.region()
    Xacc = [p.rsb(rXa, f"Xacc{i}", [128, D], F32) for i in range(16)]
    rS = p.region()
    xnsT = [p.rsb(rS, f"xnsT{d}", [128, 1024], BF16) for d in range(8)]
    xnstk = p.rsb(rS, "xnstk", [128, 8 * D], BF16)
    xtkf = p.rsb(rS, "xtkf", [128, D], F32)
    phiw = [p.rsb(rS, f"phiw{d}", [128, NPL], BF16) for d in range(8)]
    phibv = p.rsb(rS, "phibv", [1, NPL], BF16)
    colscr = p.rsb(rS, "colscr", [128, 8], F32)
    dustb = p.rsb(rS, "dustb", [128, D], F32)
    Eb5 = [p.rsb(rS, f"Eb5{i}", [128, 512], BF16) for i in range(2)]
    sd5 = p.rsb(rS, "sd5", [128, 16], F32)
    rsd5 = p.rsb(rS, "rsd5", [128, 16], F32)
    Dg5 = p.rsb(rS, "Dg5", [128, 8 * 512], BF16)
    sb_r = p.rsb(rS, "sb_r", [1, 3072], F32)
    ab_r = p.rsb(rS, "ab_r", [1, 1024], BF16)
    xbs = p.rsb(rS, "xbs", [128, 1024], BF16)
    xqs = p.rsb(rS, "xqs", [128, 1024], BF16)
    pst5 = p.rps(rS, "pst5", [1, 1024])
    pab5 = p.rps(rS, "pab5", [128, 1024])
    plg5 = p.rps(rS, "plg5", [128, 1024])
    pxp5 = p.rps(rS, "pxp5", [128, 1024])
    scr5 = {"sum": colscr[:, 0:1], "ssq": colscr[:, 1:2], "m2": colscr[:, 2:3],
            "msq": colscr[:, 3:4], "nv": colscr[:, 4:5], "dust": dustb[:]}

    phi_tok = p.load("g", [(phiw[d][:], phiT_i[128 * d:128 * (d + 1), :])
                           for d in range(8)] +
                     [(phibv[0:1, :], phibr_i[0:1, :])], deps=p4_drain)
    xacc_init = p.do("v", lambda v: [v.memset(Xacc[i][:], 0.0)
                                     for i in range(16)][-1], deps=p4_drain)

    if dbg:
        p.load("y", [(dbg_xn0[:, :], xn0_sp[:, :])], deps=[xn0_spt])
        p.load("y", [(dbg_s0[:, :], s0_sp[:, :])], deps=[s0_store[-1]])
    f5_gate = p.do("v", lambda v: None, deps=p4_drain + [phi_tok])
    e_free5 = {}
    gcnt5 = [0]
    xpf5 = {}
    xcnt5 = [0]
    sup_free = f5_gate
    xmm5_last = None
    xa_tok = {i: xacc_init for i in range(16)}
    for sup in range(16):
        t0s = 1024 * sup
        tok_toks = []
        xtk_free = sup_free
        for tc in range(8):
            tl = p.load("y", [(xtkf[:], xta[t0s + 128 * tc:t0s + 128 * (tc + 1), :])],
                        deps=[xtk_free])
            tln = ln_cols_emit(p, xtkf[:], colscr[:, 5:6], colscr[:, 6:7], scr5,
                               deps=[tl, ctoks[1]])
            xtk_free = p.do("v", lambda v, tc=tc: (
                v.tensor_scalar_sub(xtkf[:], xtkf[:], colscr[:, 5:6]),
                v.tensor_scalar_mul(xnstk[:, D * tc:D * (tc + 1)], xtkf[:],
                                    colscr[:, 6:7]))[-1], deps=[tln])
            tok_toks.append(xtk_free)
        ftoks = []
        prev = {"sq": None, "mm": None, "ap": sup_free, "tv": None}
        for tg2 in range(2):
            c0 = 512 * tg2
            sqs = []
            mms = []
            for d in range(8):
                tldd = p.load("g", [(xbs[:, 0:512],
                                     xTa[128 * d:128 * (d + 1),
                                         t0s + c0:t0s + c0 + 512])],
                              deps=[prev["mm"], prev["ap"], prev["sq"]])
                tsq = p.do("s", lambda s: s.activation(
                    xqs[:, 0:512], xbs[:, 0:512], AF.Square), deps=[tldd])
                tm = p.do("t", lambda t, d=d: (
                    t.matmul(pst5[0:1, 0:512], p.ones_col[:], xbs[:, 0:512],
                             start=(d == 0), stop=(d == 7)),
                    t.matmul(pst5[0:1, 512:1024], p.ones_col[:], xqs[:, 0:512],
                             start=(d == 0), stop=(d == 7)))[-1],
                    deps=[tldd, tsq, ctoks[1], prev["tv"]])
                prev["mm"] = tm
                prev["sq"] = tm
                sqs.append(tsq)
                mms.append(tm)
            tv = p.do("v", lambda v: (
                v.tensor_scalar_mul(sb_r[0:1, MEAN:MEAN + 512],
                                    pst5[0:1, 0:512], 1.0 / D),
                v.tensor_scalar_mul(sb_r[0:1, MSQ:MSQ + 512],
                                    pst5[0:1, 512:1024], 1.0 / D),
                v.tensor_mul(sb_r[0:1, TMP:TMP + 512],
                             sb_r[0:1, MEAN:MEAN + 512],
                             sb_r[0:1, MEAN:MEAN + 512]),
                v.tensor_sub(sb_r[0:1, NV:NV + 512],
                             sb_r[0:1, TMP:TMP + 512],
                             sb_r[0:1, MSQ:MSQ + 512]))[-1], deps=[mms[-1]])
            prev["tv"] = tv
            tss = p.do("s", lambda s: (
                s.activation(sb_r[0:1, NV:NV + 512], sb_r[0:1, NV:NV + 512],
                             AF.Ln, bias=p.epsc[0:1, 0:1], scale=-1.0),
                s.activation(sb_r[0:1, AA:AA + 512], sb_r[0:1, NV:NV + 512],
                             AF.Exp, scale=-0.5))[-1], deps=[tv])
            tabr = p.do("v", lambda v: (
                v.tensor_mul(sb_r[0:1, TMP:TMP + 512],
                             sb_r[0:1, MEAN:MEAN + 512],
                             sb_r[0:1, AA:AA + 512]),
                v.tensor_scalar_mul(sb_r[0:1, BB:BB + 512],
                                    sb_r[0:1, TMP:TMP + 512], -1.0),
                v.tensor_copy(ab_r[0:1, 0:512], sb_r[0:1, AA:AA + 512]),
                v.tensor_copy(ab_r[0:1, 512:1024], sb_r[0:1, BB:BB + 512]))[-1],
                deps=[tss])
            tbc5 = p.do("t", lambda t: (
                t.matmul(pab5[:, 0:512], p.ones_row[0:1, :], ab_r[0:1, 0:512],
                         start=True, stop=True),
                t.matmul(pab5[:, 512:1024], p.ones_row[0:1, :],
                         ab_r[0:1, 512:1024], start=True, stop=True))[-1],
                deps=[tabr, ctoks[1], ftoks[-1] if ftoks else None])
            ap_prev = None
            for d in range(8):
                tldd = p.load("g", [(xbs[:, 512:1024],
                                     xTa[128 * d:128 * (d + 1),
                                         t0s + c0:t0s + c0 + 512])],
                              deps=[ap_prev, tbc5])
                ap_prev = p.do("v", lambda v, d=d, c0=c0: (
                    v.tensor_tensor(xnsT[d][:, c0:c0 + 512], xbs[:, 512:1024],
                                    pab5[:, 0:512], op=ALU.mult),
                    v.tensor_tensor(xnsT[d][:, c0:c0 + 512],
                                    xnsT[d][:, c0:c0 + 512],
                                    pab5[:, 512:1024], op=ALU.add))[-1],
                    deps=[tldd, tbc5])
            ftoks.append(ap_prev)
        xn_sup = ftoks[-1]

        sup_last = None
        for npg in range(4):
            d_toks = []
            for tc in range(8):
                par = tc % 2
                gi = npg * 8 + tc
                def lgmm(t, npg=npg, tc=tc, par=par):
                    for d in range(8):
                        t.matmul(plg5[:, 512 * par:512 * (par + 1)],
                                 xnsT[d][:, 128 * tc:128 * (tc + 1)],
                                 phiw[d][:, 512 * npg:512 * (npg + 1)],
                                 start=(d == 0), stop=False)
                    return t.matmul(plg5[:, 512 * par:512 * (par + 1)],
                                    p.ones_row[0:1, :],
                                    phibv[0:1, 512 * npg:512 * (npg + 1)],
                                    start=False, stop=True)
                tlg = p.do("t", lgmm, deps=[phi_tok, ctoks[1], xn_sup,
                                            e_free5.get(gcnt5[0] - 2)])
                tex = p.do("s", lambda s, par=par: s.activation(
                    Eb5[par][:], plg5[:, 512 * par:512 * (par + 1)], AF.Exp),
                    deps=[tlg])
                tsd0 = p.do("v", lambda v, par=par: v.tensor_reduce(
                    sd5[:], Eb5[par][:].rearrange("p (a b) -> p a b", a=16, b=32),
                    axis=AX.X, op=ALU.add), deps=[tex, e_free5.get(gcnt5[0] - 1)])
                tsd = p.do("s", lambda s: (
                    s.activation(sd5[:], sd5[:], AF.Ln),
                    s.activation(rsd5[:], sd5[:], AF.Exp, scale=-1.0))[-1],
                    deps=[tsd0])
                tdg = p.do("v", lambda v, tc=tc, par=par: [
                    v.tensor_scalar_mul(
                        Dg5[:, 512 * tc + 32 * n:512 * tc + 32 * (n + 1)],
                        Eb5[par][:, 32 * n:32 * (n + 1)], rsd5[:, n:n + 1])
                    for n in range(16)][-1], deps=[tsd, xmm5_last])
                e_free5[gcnt5[0]] = tdg
                gcnt5[0] += 1
                d_toks.append(tdg)
            for s_ in range(4):
                npc = 4 * npg + s_
                for dg in range(2):
                    i = s_ * 2 + dg
                    par = i % 2
                    def xmm(t, s_=s_, dg=dg, par=par):
                        last = None
                        for tc in range(8):
                            last = t.matmul(
                                pxp5[:, 512 * par:512 * (par + 1)],
                                Dg5[:, 512 * tc + 128 * s_:
                                    512 * tc + 128 * (s_ + 1)],
                                xnstk[:, D * tc + 512 * dg:
                                      D * tc + 512 * (dg + 1)],
                                start=(tc == 0), stop=(tc == 7))
                        return last
                    txm = p.do("t", xmm,
                               deps=d_toks + tok_toks +
                               [xpf5.get(xcnt5[0] - 2)])
                    xmm5_last = txm
                    tacc = p.do("v", lambda v, npc=npc, dg=dg, par=par:
                        v.tensor_tensor(Xacc[npc][:, 512 * dg:512 * (dg + 1)],
                                        Xacc[npc][:, 512 * dg:512 * (dg + 1)],
                                        pxp5[:, 512 * par:512 * (par + 1)],
                                        op=ALU.add),
                        deps=[txm, xa_tok[npc]])
                    xa_tok[npc] = tacc
                    xpf5[xcnt5[0]] = tacc
                    xcnt5[0] += 1
                    sup_last = tacc
        sup_free = sup_last
    p5_tex_last = sup_last
    p.close(rS)

    rY = p.region()
    Xn6 = [p.rsb(rY, f"Xn6_{i}", [128, D], BF16) for i in range(16)]
    XnT6 = [p.rsb(rY, f"XnT6_{d}", [128, NPL], BF16) for d in range(8)]
    cols6 = p.rsb(rY, "cols6", [128, 8], F32)
    dust6 = p.rsb(rY, "dust6", [128, D], F32)
    ebrow = [p.rsb(rY, f"ebrow{i}", [1, D], BF16) for i in range(2)]
    strip6 = [p.rsb(rY, f"strip6_{i}", [128, D], BF16) for i in range(4)]
    ytsb6 = [p.rsb(rY, f"ytsb6_{i}", [32, D], BF16) for i in range(2)]
    ptr6 = [p.rps(rY, f"ptr6_{i}", [128, 1024], BF16) for i in range(2)]
    pyt6t = [p.rps(rY, f"pyt6_{i}", [32, 1024]) for i in range(2)]
    scr6 = {"sum": cols6[:, 0:1], "ssq": cols6[:, 1:2], "m2": cols6[:, 2:3],
            "msq": cols6[:, 3:4], "nv": cols6[:, 4:5], "dust": dust6[:]}

    if dbg:
        dbgx = p.rsb(rY, "dbgx", [128, D], F32)
        tds = None
        for i in range(16):
            tdc = p.do("v", lambda v, i=i: v.tensor_copy(dbgx[:], Xacc[i][:]),
                       deps=[xa_tok[i], tds])
            tds = p.load("y", [(dbg_X[128 * i:128 * (i + 1), :], dbgx[:])],
                         deps=[tdc])
            xa_tok[i] = tds
    xn6_toks = []
    for i in range(16):
        tcl = p.do("v", lambda v, i=i: (
            v.tensor_scalar_max(Xacc[i][:], Xacc[i][:], -33000.0),
            v.tensor_scalar_min(Xacc[i][:], Xacc[i][:], 65000.0))[-1],
            deps=[xa_tok[i]])
        tln = ln_cols_emit(p, Xacc[i][:], cols6[:, 5:6], cols6[:, 6:7], scr6,
                           deps=[tcl, ctoks[1], xn6_toks[-1] if xn6_toks else None])
        tap = p.do("v", lambda v, i=i: (
            v.tensor_scalar_sub(Xacc[i][:], Xacc[i][:], cols6[:, 5:6]),
            v.tensor_scalar_mul(Xacc[i][:], Xacc[i][:], cols6[:, 6:7]))[-1],
            deps=[tln])
        ttn = p.do("s", lambda s, i=i: s.activation(
            Xn6[i][:], Xacc[i][:], AF.Tanh), deps=[tap])
        xn6_toks.append(ttn)
    tr6_last = None
    prev_ev6 = {}
    k6 = 0
    for i in range(16):
        for d in range(8):
            par = k6 % 2
            ttr = p.do("t", lambda t, i=i, d=d, par=par: t.transpose(
                ptr6[par][:, 0:128], Xn6[i][:, 128 * d:128 * (d + 1)],
                p.ident[:]),
                deps=[xn6_toks[i], ctoks[0], prev_ev6.get(k6 - 2)])
            tev = p.do("v", lambda v, i=i, d=d, par=par: v.tensor_copy(
                XnT6[d][:, 128 * i:128 * (i + 1)], ptr6[par][:, 0:128]),
                deps=[ttr])
            prev_ev6[k6] = tev
            tr6_last = tev
            k6 += 1
    dbg_dump6 = []
    if dbg:
        for i in range(16):
            dbg_dump6.append(
                p.load("y", [(dbg_xn6[128 * i:128 * (i + 1), :], Xn6[i][:])],
                       deps=[tr6_last]))
        for dch in range(8):
            dbg_dump6.append(
                p.load("y", [(dbg_xnt[128 * dch:128 * (dch + 1), :],
                              XnT6[dch][:])], deps=[tr6_last]))
    yt_st = []
    sfree6 = {}
    ebuse = {}
    ev6 = {}
    st6 = {}
    last_mm_any = xmm5_last
    for n in range(NEXP):
        np_ = n % 2
        last_mm = None
        for dd in range(8):
            sb_i = (n * 8 + dd) % 4
            tld = p.load("g", [(strip6[sb_i][:],
                                expw_i[n, 128 * dd:128 * (dd + 1), :])],
                         deps=[last_mm_any,
                               None if sb_i in sfree6 else p5_tex_last])
            def ymm(t, n=n, dd=dd, sb_i=sb_i, np_=np_):
                last = None
                for hg in range(2):
                    last = t.matmul(pyt6t[np_][0:32, 512 * hg:512 * (hg + 1)],
                                    XnT6[dd][:, 32 * n:32 * (n + 1)],
                                    strip6[sb_i][:, 512 * hg:512 * (hg + 1)],
                                    start=(dd == 0), stop=False)
                return last
            last_mm = p.do("t", ymm, deps=[tld, tr6_last, ev6.get(n - 1)])
            sfree6[sb_i] = last_mm
            last_mm_any = last_mm
        teb = p.load("g", [(ebrow[n % 2][0:1, :],
                            expb_i[0:1, D * n:D * (n + 1)])],
                     deps=[last_mm_any,
                           None if n >= 2 else p5_tex_last])
        tbias = p.do("t", lambda t, n=n, np_=np_: [
            t.matmul(pyt6t[np_][0:32, 512 * hg:512 * (hg + 1)],
                     p.ones_row[0:1, 0:32],
                     ebrow[n % 2][0:1, 512 * hg:512 * (hg + 1)],
                     start=False, stop=True)
            for hg in range(2)][-1], deps=[teb, ctoks[1], last_mm])
        ebuse[n] = tbias
        tev = p.do("v", lambda v, np_=np_: v.tensor_copy(
            ytsb6[np_][:], pyt6t[np_][0:32, :]),
            deps=[tbias, st6.get(n - 2)])
        ev6[n] = tev
        st = p.load("y", [(yt_sp[PSL * n:PSL * (n + 1), :],
                           ytsb6[np_][0:32, :])], deps=[tev])
        st6[n] = st
        yt_st.append(st)
    if dbg:
        p.load("y", [(dbg_yt[:, :], yt_sp[:, :])], deps=[yt_st[-1]])
    p6_drain = [yt_st[-1], ev6[NEXP - 1], ev6[NEXP - 2], tr6_last] + dbg_dump6
    p.close(rY)
    p.close(rXa)

    xn1_sts = []
    for th in range(2):
        tc0 = TOWN * th
        rHalf = p.region()
        g0T7 = [p.rsb(rHalf, f"g0T7_{th}_{d}", [128, TOWN], BF16) for d in range(8)]
        r7a = p.region()
        CT7 = [p.rsb(r7a, f"CT7_{th}_{i}", [128, TOWN], BF16) for i in range(16)]
        r7aa = p.region()
        xnh = [p.rsb(r7aa, f"xnh{th}_{d}", [128, TOWN], BF16) for d in range(8)]
        phw7 = [p.rsb(r7aa, f"phw7_{th}_{d}", [128, NPL], BF16) for d in range(8)]
        phbc7 = p.rsb(r7aa, f"phbc7_{th}", [128, 16], F32)
        indP7 = p.rsb(r7aa, f"indP7_{th}", [128, 32], BF16)
        indPT7 = p.rsb(r7aa, f"indPT7_{th}", [32, 128], BF16)
        rscb7 = p.rsb(r7aa, f"rscb7_{th}", [32, TOWN], BF16)
        rscf7 = p.rsb(r7aa, f"rscf7_{th}", [32, TOWN], F32)
        plf7 = p.rps(r7aa, f"plf7_{th}", [128, 1024])
        psc7 = p.rps(r7aa, f"psc7_{th}", [32, TOWN])
        prb7 = p.rps(r7aa, f"prb7_{th}", [128, 512])
        t_in7 = p.load("y", [(xnh[d][:], xn0_sp[128 * d:128 * (d + 1),
                                                tc0:tc0 + TOWN])
                             for d in range(8)] +
                       [(phbc7[:], phibc_i[:, :])],
                       deps=p6_drain + [xn0_spt] +
                       ([xn1_sts[th - 1]] if th else []))
        t_ph7 = p.load("g", [(phw7[d][:], phiT_i[128 * d:128 * (d + 1), :])
                             for d in range(8)] +
                       [(indP7[:], IndP_i[:, :]), (indPT7[:], IndPT_i[:, :])],
                       deps=p6_drain + ([xn1_sts[th - 1]] if th else []))
        exp7 = {}
        sc7 = {}
        for npc in range(16):
            for tg in range(4):
                i = npc * 4 + tg
                par = i % 2
                def lfmm(t, npc=npc, tg=tg, par=par):
                    last = None
                    for d in range(8):
                        last = t.matmul(plf7[:, 512 * par:512 * (par + 1)],
                                        phw7[d][:, 128 * npc:128 * (npc + 1)],
                                        xnh[d][:, 512 * tg:512 * (tg + 1)],
                                        start=(d == 0), stop=(d == 7))
                    return last
                tlf = p.do("t", lfmm, deps=[t_ph7, t_in7, exp7.get(i - 2)])
                tex = p.do("s", lambda s, npc=npc, tg=tg, par=par: s.activation(
                    CT7[npc][:, 512 * tg:512 * (tg + 1)],
                    plf7[:, 512 * par:512 * (par + 1)], AF.Exp,
                    bias=phbc7[:, npc:npc + 1]), deps=[tlf, t_in7])
                exp7[i] = tex
                sc7[tg] = p.do("t", lambda t, npc=npc, tg=tg: t.matmul(
                    psc7[:, 512 * tg:512 * (tg + 1)], indP7[:],
                    CT7[npc][:, 512 * tg:512 * (tg + 1)],
                    start=(npc == 0), stop=(npc == 15), skip_group_check=True),
                    deps=[tex, t_ph7])
        trsc7 = p.do("s", lambda s: (
            s.activation(rscf7[:], psc7[:, :], AF.Ln),
            s.activation(rscb7[:], rscf7[:], AF.Exp, scale=-1.0))[-1],
            deps=[sc7[tg] for tg in range(4)])
        prb_rd = None
        ct_done = []
        for tg in range(4):
            tbc7 = p.do("t", lambda t, tg=tg: t.matmul(
                prb7[:, :], indPT7[:], rscb7[:, 512 * tg:512 * (tg + 1)],
                start=True, stop=True), deps=[trsc7, t_ph7, prb_rd])
            mults = []
            for npc in range(16):
                tm = p.do("v", lambda v, npc=npc, tg=tg: v.tensor_tensor(
                    CT7[npc][:, 512 * tg:512 * (tg + 1)],
                    CT7[npc][:, 512 * tg:512 * (tg + 1)], prb7[:, :],
                    op=ALU.mult), deps=[tbc7, exp7[npc * 4 + tg]])
                mults.append(tm)
            prb_rd = mults[-1]
            ct_done += mults
        p.close(r7aa)

        r7bb = p.region()
        ytb7 = [p.rsb(r7bb, f"ytb7_{th}_{i}", [128, D], BF16) for i in range(16)]
        pdis7 = p.rps(r7bb, f"pdis7_{th}", [128, 1024])
        t_yt7 = p.load("y", [(ytb7[i][:], yt_sp[128 * i:128 * (i + 1), :])
                             for i in range(16)], deps=ct_done[-1:] + [prb_rd])
        evA7 = {}
        for Hc in range(8):
            for tg in range(4):
                i = Hc * 4 + tg
                par = i % 2
                tmm = p.do("t", lambda t, Hc=Hc, tg=tg, par=par: [
                    t.matmul(pdis7[:, 512 * par:512 * (par + 1)],
                             ytb7[npc][:, 128 * Hc:128 * (Hc + 1)],
                             CT7[npc][:, 512 * tg:512 * (tg + 1)],
                             start=(npc == 0), stop=(npc == 15))
                    for npc in range(16)][-1],
                    deps=[t_yt7, evA7.get(i - 2)] + ct_done)
                evA7[i] = p.do("s", lambda s, Hc=Hc, tg=tg, par=par:
                    s.activation(g0T7[Hc][:, 512 * tg:512 * (tg + 1)],
                                 pdis7[:, 512 * par:512 * (par + 1)], AF.Silu),
                    deps=[tmm])
        a7_drain = [evA7[31], evA7[30]]
        p.close(r7bb)
        p.close(r7a)

        r7c = p.region()
        s0v7 = [p.rsb(r7c, f"s0v7_{th}_{d}", [128, TOWN], BF16) for d in range(8)]
        x1f7 = [p.rsb(r7c, f"x1f7_{th}_{d}", [128, TOWN], F32) for d in range(8)]
        xn1v7 = [p.rsb(r7c, f"xn1v7_{th}_{d}", [128, TOWN], BF16) for d in range(8)]
        xb167 = p.rsb(r7c, f"xb167_{th}", [128, 512], BF16)
        xsqB7 = p.rsb(r7c, f"xsqB7_{th}", [128, 512], BF16)
        rowsB7 = p.rsb(r7c, f"rowsB7_{th}", [1, 3072], F32)
        abfB7 = p.rsb(r7c, f"abfB7_{th}", [1, 1024], BF16)
        pstatB7 = p.rps(r7c, f"pstatB7_{th}", [1, 1024])
        pabB7 = p.rps(r7c, f"pabB7_{th}", [128, 1024])
        t_s07 = p.load("y", [(s0v7[d][:], s0_sp[128 * d:128 * (d + 1),
                                               tc0:tc0 + TOWN])
                             for d in range(8)], deps=a7_drain)
        t_x7 = p.load("g", [(x1f7[d][:], xTa[128 * d:128 * (d + 1),
                                             tc0:tc0 + TOWN])
                            for d in range(8)], deps=a7_drain)
        x1_toks7 = []
        for d in range(8):
            tx = p.do("v", lambda v, d=d: (
                v.tensor_tensor(g0T7[d][:], s0v7[d][:], g0T7[d][:], op=ALU.mult),
                v.tensor_tensor(x1f7[d][:], x1f7[d][:], g0T7[d][:],
                                op=ALU.add))[-1],
                deps=[t_s07, t_x7, evA7[31]])
            x1_toks7.append(tx)
        if th == 0:
            x1_st7 = p.load("y", [(x1_sp[128 * d:128 * (d + 1), :], x1f7[d][:])
                                  for d in range(8)], deps=x1_toks7)
        else:
            x1_st7 = None
        sqd = None
        tgd = None
        tvd = None
        xn1_toks7 = []
        for tg in range(4):
            c0 = 512 * tg
            sq_toks = []
            for d in range(8):
                tq = p.do("s", lambda s, d=d, c0=c0: (
                    s.activation(xb167[:], x1f7[d][:, c0:c0 + 512], AF.Copy),
                    s.activation(xsqB7[:], x1f7[d][:, c0:c0 + 512],
                                 AF.Square))[-1],
                    deps=[x1_toks7[d], sqd] if sqd else [x1_toks7[d]])
                tmm = p.do("t", lambda t, d=d: (
                    t.matmul(pstatB7[0:1, 0:512], p.ones_col[:], xb167[:],
                             start=(d == 0), stop=(d == 7)),
                    t.matmul(pstatB7[0:1, 512:1024], p.ones_col[:], xsqB7[:],
                             start=(d == 0), stop=(d == 7)))[-1],
                    deps=[tq, ctoks[1], tvd])
                sqd = tmm
                sq_toks.append(tmm)
            tvd = p.do("v", lambda v: (
                v.tensor_scalar_mul(rowsB7[0:1, MEAN:MEAN + 512],
                                    pstatB7[0:1, 0:512], 1.0 / D),
                v.tensor_scalar_mul(rowsB7[0:1, MSQ:MSQ + 512],
                                    pstatB7[0:1, 512:1024], 1.0 / D),
                v.tensor_mul(rowsB7[0:1, TMP:TMP + 512],
                             rowsB7[0:1, MEAN:MEAN + 512],
                             rowsB7[0:1, MEAN:MEAN + 512]),
                v.tensor_sub(rowsB7[0:1, NV:NV + 512],
                             rowsB7[0:1, TMP:TMP + 512],
                             rowsB7[0:1, MSQ:MSQ + 512]))[-1],
                deps=[sq_toks[-1], tgd])
            tsd = p.do("s", lambda s: (
                s.activation(rowsB7[0:1, NV:NV + 512], rowsB7[0:1, NV:NV + 512],
                             AF.Ln, bias=p.epsc[0:1, 0:1], scale=-1.0),
                s.activation(rowsB7[0:1, AA:AA + 512],
                             rowsB7[0:1, NV:NV + 512], AF.Exp,
                             scale=-0.5))[-1], deps=[tvd])
            tab = p.do("v", lambda v: (
                v.tensor_mul(rowsB7[0:1, TMP:TMP + 512],
                             rowsB7[0:1, MEAN:MEAN + 512],
                             rowsB7[0:1, AA:AA + 512]),
                v.tensor_scalar_mul(rowsB7[0:1, BB:BB + 512],
                                    rowsB7[0:1, TMP:TMP + 512], -1.0),
                v.tensor_copy(abfB7[0:1, 0:512], rowsB7[0:1, AA:AA + 512]),
                v.tensor_copy(abfB7[0:1, 512:1024],
                              rowsB7[0:1, BB:BB + 512]))[-1], deps=[tsd])
            tbc = p.do("t", lambda t: (
                t.matmul(pabB7[:, 0:512], p.ones_row[0:1, :], abfB7[0:1, 0:512],
                         start=True, stop=True),
                t.matmul(pabB7[:, 512:1024], p.ones_row[0:1, :],
                         abfB7[0:1, 512:1024], start=True, stop=True))[-1],
                deps=[tab, ctoks[1], tgd])
            tgd = p.do("v", lambda v, c0=c0: [
                (v.tensor_tensor(xn1v7[d][:, c0:c0 + 512], x1f7[d][:, c0:c0 + 512],
                                 pabB7[:, 0:512], op=ALU.mult),
                 v.tensor_tensor(xn1v7[d][:, c0:c0 + 512],
                                 xn1v7[d][:, c0:c0 + 512],
                                 pabB7[:, 512:1024], op=ALU.add))
                for d in range(8)][-1][-1], deps=[tbc])
            xn1_toks7.append(tgd)
        st = p.load("y", [(xn1_sp[128 * d:128 * (d + 1), tc0:tc0 + TOWN],
                           xn1v7[d][:]) for d in range(8)],
                    deps=xn1_toks7 + ([x1_st7] if x1_st7 else []))
        xn1_sts.append(st)
        p.close(r7c)
        p.close(rHalf)
    if dbg:
        p.load("y", [(dbg_x1[:, :], x1_sp[:, :])], deps=xn1_sts)
        p.load("y", [(dbg_xn1[:, :], xn1_sp[:, :])], deps=xn1_sts)
    f7_drain = xn1_sts

    emit_l1(p, ctoks, xn1_sp, x1_sp, wq2_i, WpT_i, bp_i,
            w1T_i, w2T_i, h1T_i, h2T_i, w3T_i,
            w1s_i, w2s_i, h1s_i, h2s_i, w3s_i, x2_o, f7_drain)
    return p.finish()



_IDENT = np.eye(128, dtype=np.float32)


def _indPT():
    m = np.zeros((32, 128), dtype=np.float32)
    for r in range(128):
        m[r % 32, r] = 1.0
    return m


def _f32(a):
    return np.ascontiguousarray(np.asarray(a, dtype=np.float32))


def _prep_weights(inputs):
    w = {}
    w["WqkvT"] = _f32(np.asarray(inputs["L0_attn_W_qkv"]).T)
    w["WoT"] = _f32(np.asarray(inputs["L0_attn_W_o"]).T)
    w["phiT"] = _f32(np.asarray(inputs["L0_ffn_phi_w"]).T)
    phib = _f32(inputs["L0_ffn_phi_b"]).reshape(-1)
    w["phibr"] = phib.reshape(1, NPL)
    w["phibc"] = _f32(phib.reshape(16, 128).T)
    w["IndPT"] = _indPT()
    w["IndP"] = np.ascontiguousarray(w["IndPT"].T)
    w["ident128"] = _IDENT
    Wq = _f32(inputs["L1_attn_Wq"])
    wq2 = np.zeros((128, NH * DH), np.float32)
    for h in range(NH):
        wq2[0:64, DH * h:DH * (h + 1)] = Wq[h].T
        wq2[64:128, DH * h:DH * (h + 1)] = Wq[h].T
    w["Wq2"] = np.ascontiguousarray(wq2)
    w["WpT"] = _f32(np.asarray(inputs["L1_attn_Wp"]).T)
    w["bpr"] = _f32(inputs["L1_attn_bp"]).reshape(1, D)
    w["w1T"] = _f32(np.asarray(inputs["L1_ffn_w1_w"]).T)
    w["w2T"] = _f32(np.asarray(inputs["L1_ffn_w2_w"]).T)
    w["h1T"] = _f32(np.asarray(inputs["L1_ffn_h1_w"]).T)
    w["h2T"] = _f32(np.asarray(inputs["L1_ffn_h2_w"]).T)
    w["w3T"] = _f32(np.asarray(inputs["L1_ffn_w3_w"]).T)
    w["w1sc"] = _f32(_f32(inputs["L1_ffn_w1_s"]).reshape(32, 128).T)
    w["w2sc"] = _f32(_f32(inputs["L1_ffn_w2_s"]).reshape(32, 128).T)
    w["h1sc"] = _f32(_f32(inputs["L1_ffn_h1_s"]).reshape(32, 128).T)
    w["h2sc"] = _f32(_f32(inputs["L1_ffn_h2_s"]).reshape(32, 128).T)
    w["w3sc"] = _f32(_f32(inputs["L1_ffn_w3_s"]).reshape(8, 128).T)
    return w


def kernel(**inputs):
    x = _f32(inputs["L0_attn_x"])
    w = _prep_weights(inputs)
    expw = _f32(inputs["L0_ffn_expert_w"])
    expb = _f32(inputs["L0_ffn_expert_b"]).reshape(1, NEXP * D)
    ims = []
    for c in range(8):
        b, half = c // 2, c % 2
        own = x[b, TOWN * half:TOWN * (half + 1)]
        oth = x[b, TOWN * (1 - half):TOWN * (2 - half)]
        others = np.concatenate([x[bb] for bb in range(4) if bb != b], 0)
        allt = np.concatenate([own, oth, others], 0)
        im = dict(w)
        im["xTa"] = np.ascontiguousarray(allt.T)
        im["xta"] = np.ascontiguousarray(allt)
        im["expwA"] = expw
        im["expbA"] = expb
        ims.append(im)
    res = run_bass_kernel_spmd(build_fused(), ims, core_ids=list(range(8))).results
    out = np.empty((4, NT, D), np.float32)
    for c in range(8):
        b, half = c // 2, c % 2
        out[b, TOWN * half:TOWN * (half + 1)] = np.asarray(res[c]["x2T"]).T
    return out


def kernel_multi(**inputs):
    x = _f32(inputs["L0_attn_x"])
    w = _prep_weights(inputs)
    expw = _f32(inputs["L0_ffn_expert_w"])
    expb = _f32(inputs["L0_ffn_expert_b"])
    xTs, xtoks = [], []
    for c in range(8):
        b, half = c // 2, c % 2
        own = x[b, TOWN * half:TOWN * (half + 1)]
        oth = x[b, TOWN * (1 - half):TOWN * (2 - half)]
        xTs.append(np.ascontiguousarray(np.concatenate([own, oth], 0).T))
        xtoks.append(np.ascontiguousarray(own))

    im1 = [{
        "xT": xTs[c], "xtok": xtoks[c], "WqkvT": w["WqkvT"], "WoT": w["WoT"],
        "phiT": w["phiT"], "phibr": w["phibr"], "phibc": w["phibc"],
        "IndP": w["IndP"], "IndPT": w["IndPT"], "ident128": _IDENT,
    } for c in range(8)]
    r1 = run_bass_kernel_spmd(build_ml1(), im1, core_ids=list(range(8))).results

    im2 = [{
        "xp_all": np.stack([r1[s]["Xp"][256 * c:256 * (c + 1)] for s in range(8)]),
        "expw": np.ascontiguousarray(expw[EOWN * c:EOWN * (c + 1)]),
        "expb": np.ascontiguousarray(expb[EOWN * c:EOWN * (c + 1)].reshape(1, -1)),
        "ident128": _IDENT,
    } for c in range(8)]
    r2 = run_bass_kernel_spmd(build_ml2(), im2, core_ids=list(range(8))).results

    YtA = np.ascontiguousarray(
        np.concatenate([np.asarray(r2[c]["yt"]).reshape(EOWN * PSL, D)
                        for c in range(8)], 0))
    im3 = [{
        "YtA": YtA, "CTi": r1[c]["CT"], "s0Ti": r1[c]["s0T"],
        "xTown": np.ascontiguousarray(xTs[c][:, 0:TOWN]), "ident128": _IDENT,
    } for c in range(8)]
    r3 = run_bass_kernel_spmd(build_ml3(), im3, core_ids=list(range(8))).results

    im4 = [{
        "xn1f": np.ascontiguousarray(np.concatenate(
            [np.asarray(r3[c]["xn1T"]), np.asarray(r3[c ^ 1]["xn1T"])], 1)),
        "x1Ti": r3[c]["x1T"],
        "Wq2": w["Wq2"], "WpT": w["WpT"], "bpr": w["bpr"],
        "w1T": w["w1T"], "w2T": w["w2T"], "h1T": w["h1T"], "h2T": w["h2T"],
        "w3T": w["w3T"], "w1sc": w["w1sc"], "w2sc": w["w2sc"],
        "h1sc": w["h1sc"], "h2sc": w["h2sc"], "w3sc": w["w3sc"],
        "ident128": _IDENT,
    } for c in range(8)]
    r4 = run_bass_kernel_spmd(build_ml4(), im4, core_ids=list(range(8))).results

    out = np.empty((4, NT, D), np.float32)
    for c in range(8):
        b, half = c // 2, c % 2
        out[b, TOWN * half:TOWN * (half + 1)] = np.asarray(r4[c]["x2T"]).T
    return out


# version 10
# speedup vs baseline: 1.0664x; 1.0664x over previous
from contextlib import ExitStack

import numpy as np
import ml_dtypes

import concourse.bass as bass
import concourse.mybir as mybir
from concourse.bass_utils import run_bass_kernel_spmd

F32 = mybir.dt.float32
BF16 = mybir.dt.bfloat16
AF = mybir.ActivationFunctionType
ALU = mybir.AluOpType
AX = mybir.AxisListType
BF = ml_dtypes.bfloat16

D = 1024
NT = 4096
TOWN = 2048
NH = 16
DH = 64
NEXP = 64
PSL = 32
NPL = 2048
EOWN = 8
DHID = 4096
LN_EPS = 1e-5

ENG_ATTR = {"t": "tensor", "v": "vector", "s": "scalar", "g": "gpsimd", "y": "sync"}
SEM_ROT = 50000


class P:

    def __init__(self):
        self.nc = bass.Bass("TRN2", target_bir_lowering=False)
        self.st = ExitStack()
        self.sems = {}
        self.cnt = {}
        self.marks = {e: {} for e in ENG_ATTR}
        self.block = None
        self.q = {}

    def sb(self, name, shape, dtype):
        return self.st.enter_context(self.nc.sbuf_tensor(name, shape, dtype))

    def psum(self, name, shape, dtype=F32):
        return self.st.enter_context(self.nc.psum_tensor(name, shape, dtype))

    def region(self):
        return self.st.enter_context(ExitStack())

    def rsb(self, region, name, shape, dtype):
        return region.enter_context(self.nc.sbuf_tensor(name, shape, dtype))

    def rps(self, region, name, shape, dtype=F32):
        return region.enter_context(self.nc.psum_tensor(name, shape, dtype))

    def din(self, name, shape, dtype=F32):
        return self.nc.dram_tensor(name, list(shape), dtype, kind="ExternalInput").ap()

    def dout(self, name, shape, dtype=F32):
        return self.nc.dram_tensor(name, list(shape), dtype, kind="ExternalOutput").ap()

    def _sem(self, name):
        if name not in self.sems:
            self.sems[name] = self.st.enter_context(self.nc.semaphore(name))
            self.cnt[name] = 0
        return self.sems[name]

    def _sig_name(self, eng, dma):
        base = eng + ("d" if dma else "")
        gen = self.cnt.get("#" + base, 0)
        name = f"{base}.{gen}"
        self._sem(name)
        if self.cnt[name] >= SEM_ROT:
            gen += 1
            self.cnt["#" + base] = gen
            name = f"{base}.{gen}"
            self._sem(name)
        return name

    def start(self):
        self.block = self.st.enter_context(self.nc.Block())

    def _emit_waits(self, engine, eng, deps):
        for tok in deps:
            if tok is None:
                continue
            sn, val, prod, isd = tok
            if prod == eng and not isd:
                continue
            if self.marks[eng].get(sn, 0) >= val:
                continue
            engine.wait_ge(self.sems[sn], val)
            self.marks[eng][sn] = val

    def do(self, eng, fn, deps=(), sig=True, force=()):
        if sig:
            sn = self._sig_name(eng, False)
            self.cnt[sn] += 1
            tok = (sn, self.cnt[sn], eng, False)
        else:
            sn, tok = None, None
        deps = tuple(deps)
        force = tuple(force)

        def body(engine, sn=sn, deps=deps, force=force, fn=fn):
            self._emit_waits(engine, eng, deps)
            for ftok in force:
                if ftok is None:
                    continue
                fsn, fval, _, _ = ftok
                if self.marks[eng].get(fsn, 0) >= fval:
                    continue
                engine.wait_ge(self.sems[fsn], fval)
                self.marks[eng][fsn] = fval
            inst = fn(engine)
            if sn is not None:
                if inst is None:
                    engine.sem_inc(self.sems[sn], 1)
                else:
                    inst.then_inc(self.sems[sn], 1)

        self.q.setdefault(eng, []).append(body)
        return tok

    def load(self, eng, pairs, deps=()):
        assert eng in ("g", "y")
        sn = self._sig_name(eng, True)
        self.cnt[sn] += 16 * len(pairs)
        tok = (sn, self.cnt[sn], eng, True)
        deps = tuple(deps)
        pairs = list(pairs)

        def body(engine, sn=sn, deps=deps, pairs=pairs):
            self._emit_waits(engine, eng, deps)
            for out, in_ in pairs:
                engine.dma_start(out=out, in_=in_).then_inc(self.sems[sn], 16)

        self.q.setdefault(eng, []).append(body)
        return tok

    def wait_only(self, eng, deps):
        self.do(eng, lambda e: None, deps=deps, sig=False)

    def flush(self):
        for eng, queue in self.q.items():
            if not queue:
                continue
            def body(engine, queue=list(queue)):
                for fn in queue:
                    fn(engine)
            getattr(self.block, ENG_ATTR[eng])(body)
            queue.clear()

    def close(self, region):
        self.flush()
        region.close()

    def finish(self):
        self.flush()
        self.st.close()
        return self.nc


def consts_region(p, want_identf=False):
    p.epsc = p.sb("epsc", [128, 1], F32)
    p.c20 = p.sb("c20", [128, 1], F32)
    p.ones_col = p.sb("ones_col", [128, 1], BF16)
    p.ones_row = p.sb("ones_row", [1, 128], BF16)
    p.ident = p.sb("ident", [128, 128], BF16)
    id_in = p.din("ident128", [128, 128])
    tok_id = p.load("g", [(p.ident[:], id_in[:, :])])
    if want_identf:
        p.identf = p.sb("identf", [128, 128], F32)
        p.identf_tok = p.load("y", [(p.identf[:], id_in[:, :])])
    tok_ms = p.do("v", lambda v: (
        v.memset(p.epsc[:], LN_EPS),
        v.memset(p.c20[:], -20.0),
        v.memset(p.ones_col[:], 1.0),
        v.memset(p.ones_row[:], 1.0))[-1])
    return [tok_id, tok_ms]


def ln_cols_emit(p, src, mcol, rcol, scr, deps):
    dd = float(src.shape[-1])
    t1 = p.do("v", lambda v: (
        v.reduce_sum(scr["sum"], src, axis=AX.X),
        v.scalar_tensor_tensor(scr["dust"], src, 1.0, src, op0=ALU.mult,
                               op1=ALU.mult, accum_out=scr["ssq"]))[-1],
        deps=deps)
    t2 = p.do("s", lambda s: (
        s.activation(mcol, scr["sum"], AF.Copy, scale=1.0 / dd),
        s.activation(scr["msq"], scr["ssq"], AF.Copy, scale=1.0 / dd),
        s.activation(scr["m2"], mcol, AF.Square))[-1], deps=[t1])
    t3 = p.do("v", lambda v: v.tensor_sub(scr["nv"], scr["m2"], scr["msq"]),
              deps=[t2])
    t4 = p.do("s", lambda s: (
        s.activation(scr["nv"], scr["nv"], AF.Ln, bias=p.epsc[:, 0:1], scale=-1.0),
        s.activation(rcol, scr["nv"], AF.Exp, scale=-0.5))[-1], deps=[t3])
    return t4



def build_ml1(dbg=False):
    p = P()
    xT = p.din("xT", [D, NT])
    xtok = p.din("xtok", [TOWN, D])
    WqkvT_i = p.din("WqkvT", [D, D])
    WoT_i = p.din("WoT", [D, D])
    phiT_i = p.din("phiT", [D, NPL])
    phibr_i = p.din("phibr", [1, NPL])
    phibc_i = p.din("phibc", [128, 16])
    IndP_i = p.din("IndP", [128, 32])
    IndPT_i = p.din("IndPT", [32, 128])
    s0T_o = p.dout("s0T", [D, TOWN], BF16)
    if dbg:
        dbg_ab = p.dout("dbg_ab", [2, NT])
        dbg_xn = p.dout("dbg_xn", [D, NT], BF16)
        dbg_qkv = p.dout("dbg_qkv", [D, NT], BF16)
        dbg_xtok = p.dout("dbg_xtok", [128, 16 * D], BF16)
        dbg_dg = p.dout("dbg_dg", [128, 16 * 512], BF16)
        dbg_eb = p.dout("dbg_eb", [128, 512], BF16)
        dbg_sd = p.dout("dbg_sd", [128, 32])
    Xp_o = p.dout("Xp", [NPL, D])
    CT_o = p.dout("CT", [NPL, TOWN], BF16)

    p.start()
    ctoks = consts_region(p)

    xn0T = [p.sb(f"xn0T{d}", [128, NT], BF16) for d in range(8)]
    phibv = p.sb("phibv", [1, NPL], BF16)
    phibcv = p.sb("phibcv", [128, 16], F32)
    indP = p.sb("indP", [128, 32], BF16)
    indPT = p.sb("indPT", [32, 128], BF16)

    misc_tok = p.load("g", [(phibv[0:1, :], phibr_i[0:1, :]),
                            (indP[:], IndP_i[:, :]),
                            (indPT[:], IndPT_i[:, :])])
    phibc_tok = p.load("y", [(phibcv[:], phibc_i[:, :])])

    r1 = p.region()
    xball = p.rsb(r1, "xball", [128, NT], BF16)
    xsq = p.rsb(r1, "xsq", [128, NT], BF16)
    rowscr = [p.rsb(r1, f"rowscr{i}", [1, 3072], F32) for i in range(2)]
    abf = [p.rsb(r1, f"abf{i}", [1, 1024], BF16) for i in range(2)]
    pstat = [p.rps(r1, f"pstat{i}", [1, 1024]) for i in range(2)]
    pab = [p.rps(r1, f"pab{i}", [128, 1024]) for i in range(2)]
    MEAN, MSQ, NV, TMP, AA, BB = 0, 512, 1024, 1536, 2048, 2560

    tg_done = [None] * 8
    sq_done = [None] * 8
    mm_done = [None] * 8
    tv_done = [None] * 8
    abf_done = [None] * 8
    for tg in range(8):
        c0 = 512 * tg
        ldeps = [ctoks[1]] + ([sq_done[tg - 1], mm_done[tg - 1], tg_done[tg - 1]]
                              if tg else [])
        tld = p.load("g", [(xball[:, 512 * d:512 * (d + 1)],
                            xT[128 * d:128 * (d + 1), c0:c0 + 512])
                           for d in range(8)], deps=ldeps)
        sq_done[tg] = p.do("s", lambda s, tg=tg: [
            s.activation(xsq[:, 512 * d:512 * (d + 1)],
                         xball[:, 512 * d:512 * (d + 1)], AF.Square)
            for d in range(8)][-1], deps=[tld])

        def statmm(t, tg=tg):
            for d in range(8):
                t.matmul(pstat[0:1, 0:512], p.ones_col[:],
                         xball[:, 512 * d:512 * (d + 1)],
                         start=(d == 0), stop=(d == 7))
            last = None
            for d in range(8):
                last = t.matmul(pstat[0:1, 512:1024], p.ones_col[:],
                                xsq[:, 512 * d:512 * (d + 1)],
                                start=(d == 0), stop=(d == 7))
            return last
        mm_done[tg] = p.do("t", statmm,
                           deps=[tld, sq_done[tg], ctoks[1],
                                 tv_done[tg - 1] if tg else None])

        def rmath_v(v, tg=tg):
            v.tensor_scalar_mul(rowscr[0:1, MEAN:MEAN + 512],
                                pstat[0:1, 0:512], 1.0 / D)
            v.tensor_scalar_mul(rowscr[0:1, MSQ:MSQ + 512],
                                pstat[0:1, 512:1024], 1.0 / D)
            v.tensor_mul(rowscr[0:1, TMP:TMP + 512], rowscr[0:1, MEAN:MEAN + 512],
                         rowscr[0:1, MEAN:MEAN + 512])
            return v.tensor_sub(rowscr[0:1, NV:NV + 512],
                                rowscr[0:1, TMP:TMP + 512],
                                rowscr[0:1, MSQ:MSQ + 512])
        tv_done[tg] = p.do("v", rmath_v,
                           deps=[mm_done[tg], abf_done[tg - 1] if tg else None])

        def rmath_s(s, tg=tg):
            s.activation(rowscr[0:1, NV:NV + 512], rowscr[0:1, NV:NV + 512],
                         AF.Ln, bias=p.epsc[0:1, 0:1], scale=-1.0)
            return s.activation(rowscr[0:1, AA:AA + 512],
                                rowscr[0:1, NV:NV + 512], AF.Exp, scale=-0.5)
        ts = p.do("s", rmath_s, deps=[tv_done[tg]])
        abf_done[tg] = p.do("v", lambda v: (
            v.tensor_mul(rowscr[0:1, TMP:TMP + 512], rowscr[0:1, MEAN:MEAN + 512],
                         rowscr[0:1, AA:AA + 512]),
            v.tensor_scalar_mul(rowscr[0:1, BB:BB + 512],
                                rowscr[0:1, TMP:TMP + 512], -1.0),
            v.tensor_copy(abf[0:1, 0:512], rowscr[0:1, AA:AA + 512]),
            v.tensor_copy(abf[0:1, 512:1024], rowscr[0:1, BB:BB + 512]))[-1],
            deps=[ts])
        if dbg:
            p.load("y", [(dbg_ab[0:1, c0:c0 + 512], rowscr[0:1, AA:AA + 512]),
                         (dbg_ab[1:2, c0:c0 + 512], rowscr[0:1, BB:BB + 512])],
                   deps=[abf_done[tg]])
        tbc = p.do("t", lambda t: (
            t.matmul(pab[:, 0:512], p.ones_row[0:1, :], abf[0:1, 0:512],
                     start=True, stop=True),
            t.matmul(pab[:, 512:1024], p.ones_row[0:1, :], abf[0:1, 512:1024],
                     start=True, stop=True))[-1],
            deps=[abf_done[tg], ctoks[1], tg_done[tg - 1] if tg else None])
        tg_done[tg] = p.do("v", lambda v, c0=c0: [
            (v.tensor_tensor(xn0T[d][:, c0:c0 + 512],
                             xball[:, 512 * d:512 * (d + 1)],
                             pab[:, 0:512], op=ALU.mult),
             v.tensor_tensor(xn0T[d][:, c0:c0 + 512], xn0T[d][:, c0:c0 + 512],
                             pab[:, 512:1024], op=ALU.add))
            for d in range(8)][-1][-1], deps=[tbc])
    if dbg:
        p.load("y", [(dbg_xn[128 * d:128 * (d + 1), :], xn0T[d][:])
                     for d in range(8)], deps=tg_done)
    p.close(r1)

    rB = p.region()
    attn0T = [p.rsb(rB, f"attn0T{d}", [128, TOWN], BF16) for d in range(8)]
    rA = p.region()
    qkvT = [p.rsb(rA, f"qkvT{c}", [128, NT], BF16) for c in range(8)]
    rW = p.region()
    wqkv = [p.rsb(rW, f"wqkv{d}", [128, D], BF16) for d in range(8)]
    pqk = p.rps(rW, "pqk", [128, 1024])
    w_tok = p.load("g", [(wqkv[d][:], WqkvT_i[128 * d:128 * (d + 1), :])
                         for d in range(8)],
                   deps=[tg_done[7], mm_done[7], sq_done[7]])

    p1_drain = [tg_done[7], mm_done[7]]
    ev2 = {}
    for og in range(8):
        for tg in range(8):
            i = og * 8 + tg
            par = i % 2
            tmm = p.do("t", lambda t, og=og, tg=tg, par=par: [
                t.matmul(pqk[:, 512 * par:512 * (par + 1)],
                         wqkv[d][:, 128 * og:128 * (og + 1)],
                         xn0T[d][:, 512 * tg:512 * (tg + 1)],
                         start=(d == 0), stop=(d == 7))
                for d in range(8)][-1],
                deps=[w_tok, tg_done[tg], ev2.get(i - 2)] + p1_drain)
            ev2[i] = p.do("s", lambda s, og=og, tg=tg, par=par: s.activation(
                qkvT[og][:, 512 * tg:512 * (tg + 1)],
                pqk[:, 512 * par:512 * (par + 1)], AF.Copy),
                deps=[tmm])
    qkv_done = ev2[63]
    if dbg:
        p.load("y", [(dbg_qkv[128 * c:128 * (c + 1), :], qkvT[c][:])
                     for c in range(8)], deps=[qkv_done])
    p.close(rW)

    v1 = [p.rsb(rA, f"v1_{i}", [128, 32 * 65], BF16) for i in range(2)]
    escr = [p.rsb(rA, f"escr{i}", [128, 1024], BF16) for i in range(2)]
    avsb = p.rsb(rA, "avsb", [64, 512], BF16)
    dnrow = p.rsb(rA, "dnrow", [1, 512], BF16)
    dnrowf = p.rsb(rA, "dnrowf", [1, 512], F32)
    ptr3 = [p.rps(rA, f"ptr3_{i}", [128, 1024], BF16) for i in range(2)]
    psc = [p.rps(rA, f"psc{i}", [128, 1024]) for i in range(2)]
    pav = p.rps(rA, "pav", [128, 512])
    pdn = p.rps(rA, "pdn", [64, 512])

    vms = p.do("v", lambda v: (v.memset(v1[0][:], 1.0),
                               v.memset(v1[1][:], 1.0))[-1], deps=[qkv_done])

    v1use = {0: None, 1: None}
    prev_tcp = None
    prev_trc = None
    prev_tno = None
    for h in range(NH):
        hc, sub = h // 2, 64 * (h % 2)
        vb = h % 2
        idsl = p.ident[sub:sub + 64, sub:sub + 64]
        tr_toks = []
        for jg in range(8):
            par = jg % 2
            ttr = p.do("t", lambda t, hc=hc, sub=sub, jg=jg, par=par, idsl=idsl: [
                t.transpose(ptr3[par][:, 64 * q:64 * q + 64],
                            qkvT[hc][sub:sub + 64,
                                     128 * (4 * jg + q):128 * (4 * jg + q) + 128],
                            idsl)
                for q in range(4)][-1],
                deps=[qkv_done, ctoks[0],
                      tr_toks[-2] if len(tr_toks) >= 2 else None])
            tev = p.do("v", lambda v, jg=jg, par=par, vb=vb: v.tensor_copy(
                v1[vb][:].rearrange("p (a b) -> p a b", a=32, b=65)
                [:, 4 * jg:4 * jg + 4, 0:64],
                ptr3[par][:, 0:256].rearrange("p (a b) -> p a b", a=4, b=64)),
                deps=[ttr, vms, v1use[vb]])
            tr_toks.append(tev)
        v1_ready = tr_toks[-1]

        for g in range(4):
            qs = 512 * g
            av_last = None
            exp_toks = {}
            av_toks = {}
            for jj in range(16):
                par = jj % 2
                def smm(t, hc=hc, sub=sub, jj=jj, par=par, qs=qs):
                    last = None
                    for u in range(2):
                        j = 2 * jj + u
                        last = t.matmul(psc[par][:, 512 * u:512 * (u + 1)],
                                        qkvT[hc][sub:sub + 64, 128 * j:128 * (j + 1)],
                                        qkvT[hc][sub:sub + 64, qs:qs + 512],
                                        start=True, stop=True)
                    return last
                tsc = p.do("t", smm, deps=[qkv_done, exp_toks.get(jj - 2)])
                texp = p.do("s", lambda s, par=par: s.activation(
                    escr[par][:], psc[par][:], AF.Exp,
                    bias=p.c20[:, 0:1], scale=0.125),
                    deps=[tsc, ctoks[1], av_toks.get(jj - 2)])
                exp_toks[jj] = texp
                def avmm(t, jj=jj, par=par, vb=vb):
                    last = None
                    for u in range(2):
                        j = 2 * jj + u
                        last = t.matmul(pav[0:65, :],
                                        v1[vb][:, 65 * j:65 * (j + 1)],
                                        escr[par][:, 512 * u:512 * (u + 1)],
                                        start=(j == 0), stop=(j == 31))
                    return last
                avdeps = [texp, v1_ready]
                if jj == 0:
                    avdeps += [prev_tcp, prev_trc]
                av_last = p.do("t", avmm, deps=avdeps)
                av_toks[jj] = av_last
            v1use[vb] = av_last
            tcp = p.do("s", lambda s: s.activation(avsb[:], pav[0:64, :], AF.Copy),
                       deps=[av_last, prev_tno])
            trc = p.do("s", lambda s: (
                s.activation(dnrowf[0:1, :], pav[64:65, :], AF.Ln),
                s.activation(dnrow[0:1, :], dnrowf[0:1, :], AF.Exp, scale=-1.0))[-1],
                deps=[av_last, prev_tno])
            tbc = p.do("t", lambda t: t.matmul(
                pdn[:, :], p.ones_row[0:1, 0:64], dnrow[0:1, :],
                start=True, stop=True), deps=[trc, ctoks[1], prev_tno])
            tno = p.do("v", lambda v, hc=hc, sub=sub, qs=qs: v.tensor_tensor(
                attn0T[hc][sub:sub + 64, qs:qs + 512],
                avsb[:], pdn[:, :], op=ALU.mult), deps=[tbc, tcp])
            prev_tcp, prev_trc, prev_tno = tcp, trc, tno
    attn_done = prev_tno
    p.close(rA)

    wo = [p.rsb(rB, f"wo{d}", [128, D], BF16) for d in range(8)]
    s0sb = p.rsb(rB, "s0sb", [128, TOWN], BF16)
    pwo = p.rps(rB, "pwo", [128, 1024])
    wo_tok = p.load("g", [(wo[d][:], WoT_i[128 * d:128 * (d + 1), :])
                          for d in range(8)], deps=[attn_done])
    s0_store = []
    ev4 = {}
    for og in range(8):
        evs = []
        for tg in range(4):
            i = og * 4 + tg
            par = i % 2
            tmm = p.do("t", lambda t, og=og, tg=tg, par=par: [
                t.matmul(pwo[:, 512 * par:512 * (par + 1)],
                         wo[d][:, 128 * og:128 * (og + 1)],
                         attn0T[d][:, 512 * tg:512 * (tg + 1)],
                         start=(d == 0), stop=(d == 7))
                for d in range(8)][-1],
                deps=[wo_tok, attn_done, ev4.get(i - 2)])
            ev4[i] = p.do("s", lambda s, tg=tg, par=par: s.activation(
                s0sb[:, 512 * tg:512 * (tg + 1)],
                pwo[:, 512 * par:512 * (par + 1)], AF.Copy),
                deps=[tmm, s0_store[-1] if s0_store else None])
            evs.append(ev4[i])
        s0_store.append(p.load(
            "y", [(s0T_o[128 * og:128 * (og + 1), :], s0sb[:])], deps=evs))
    p4_drain = [s0_store[-1], ev4[31]]
    p.close(rB)

    phiw = [p.sb(f"phiw{d}", [128, NPL], BF16) for d in range(8)]
    phi_tok = p.load("g", [(phiw[d][:], phiT_i[128 * d:128 * (d + 1), :])
                           for d in range(8)], deps=p4_drain)
    r5 = p.region()
    xn0tok = p.rsb(r5, "xn0tok", [128, 16 * D], BF16)
    xtk = p.rsb(r5, "xtk", [128, D], F32)
    colscr = p.rsb(r5, "colscr", [128, 8], F32)
    dustb = p.rsb(r5, "dustb", [128, D], F32)
    scr = {"sum": colscr[:, 0:1], "ssq": colscr[:, 1:2], "m2": colscr[:, 2:3],
           "msq": colscr[:, 3:4], "nv": colscr[:, 4:5], "dust": dustb[:]}
    xtk_free = None
    for tc in range(16):
        tl = p.load("y", [(xtk[:], xtok[128 * tc:128 * (tc + 1), :])],
                    deps=[xtk_free] + (p4_drain if tc == 0 else []))
        tln = ln_cols_emit(p, xtk[:], colscr[:, 5:6], colscr[:, 6:7], scr,
                           deps=[tl, ctoks[1]])
        xtk_free = p.do("v", lambda v, tc=tc: (
            v.tensor_scalar_sub(xtk[:], xtk[:], colscr[:, 5:6]),
            v.tensor_scalar_mul(xn0tok[:, D * tc:D * (tc + 1)], xtk[:],
                                colscr[:, 6:7]))[-1], deps=[tln])
    xn0tok_done = xtk_free
    if dbg:
        p.load("y", [(dbg_xtok[:, :], xn0tok[:, :])], deps=[xn0tok_done])

    Eb = [p.rsb(r5, f"Eb{i}", [128, 512], BF16) for i in range(2)]
    sd5 = p.rsb(r5, "sd5", [128, 16], F32)
    rsd5 = p.rsb(r5, "rsd5", [128, 16], F32)
    Dg = p.rsb(r5, "Dg", [128, 16 * 512], BF16)
    xpev = p.rsb(r5, "xpev", [128, 512], F32)
    plg = p.rps(r5, "plg", [128, 1024])
    pxp = p.rps(r5, "pxp", [128, 1024])

    xmm_last = None
    xp_done = None
    for npg in range(4):
        d_toks = []
        e_free = {}
        for tc in range(16):
            par = tc % 2
            def lgmm(t, npg=npg, tc=tc, par=par):
                for d in range(8):
                    t.matmul(plg[:, 512 * par:512 * (par + 1)],
                             xn0T[d][:, 128 * tc:128 * (tc + 1)],
                             phiw[d][:, 512 * npg:512 * (npg + 1)],
                             start=(d == 0), stop=False)
                return t.matmul(plg[:, 512 * par:512 * (par + 1)],
                                p.ones_row[0:1, :],
                                phibv[0:1, 512 * npg:512 * (npg + 1)],
                                start=False, stop=True)
            tlg = p.do("t", lgmm, deps=[phi_tok, misc_tok, ctoks[1],
                                        e_free.get(tc - 2)] +
                       (p4_drain if (npg == 0 and tc == 0) else []))
            tex = p.do("s", lambda s, par=par: s.activation(
                Eb[par][:], plg[:, 512 * par:512 * (par + 1)], AF.Exp),
                deps=[tlg])
            p5_tex_last = tex
            tsd0 = p.do("v", lambda v, par=par: v.tensor_reduce(
                sd5[:], Eb[par][:].rearrange("p (a b) -> p a b", a=16, b=32),
                axis=AX.X, op=ALU.add), deps=[tex, e_free.get(tc - 1)])
            tsd = p.do("s", lambda s: (
                s.activation(sd5[:], sd5[:], AF.Ln),
                s.activation(rsd5[:], sd5[:], AF.Exp, scale=-1.0))[-1],
                deps=[tsd0])
            tdg = p.do("v", lambda v, tc=tc, par=par: [
                v.tensor_scalar_mul(
                    Dg[:, 512 * tc + 32 * n:512 * tc + 32 * (n + 1)],
                    Eb[par][:, 32 * n:32 * (n + 1)], rsd5[:, n:n + 1])
                for n in range(16)][-1], deps=[tsd, xmm_last])
            if dbg and npg == 0 and tc == 0:
                tdump = p.load("y", [(dbg_eb[:, :], Eb[par][:]),
                                     (dbg_sd[:, 0:16], sd5[:]),
                                     (dbg_sd[:, 16:32], rsd5[:])], deps=[tdg])
                tdg = tdump
            e_free[tc] = tdg
            d_toks.append(tdg)
        if dbg and npg == 0:
            p.load("y", [(dbg_dg[:, :], Dg[:, :])], deps=d_toks)
        xp_prev = {}
        tst = None
        for s_ in range(4):
            for dg in range(2):
                i = s_ * 2 + dg
                par = i % 2
                def xmm(t, s_=s_, dg=dg, par=par):
                    last = None
                    for tc in range(16):
                        last = t.matmul(
                            pxp[:, 512 * par:512 * (par + 1)],
                            Dg[:, 512 * tc + 128 * s_:512 * tc + 128 * (s_ + 1)],
                            xn0tok[:, D * tc + 512 * dg:D * tc + 512 * (dg + 1)],
                            start=(tc == 0), stop=(tc == 15))
                    return last
                txm = p.do("t", xmm, deps=d_toks + [xn0tok_done,
                                                    xp_prev.get(i - 2)])
                tev = p.do("v", lambda v, par=par: v.tensor_copy(
                    xpev[:], pxp[:, 512 * par:512 * (par + 1)]),
                    deps=[txm, tst])
                tst = p.load("y", [(Xp_o[512 * npg + 128 * s_:
                                         512 * npg + 128 * (s_ + 1),
                                         512 * dg:512 * (dg + 1)], xpev[:])],
                             deps=[tev])
                xp_prev[i] = tev
                xmm_last = txm
        xp_done = tst
    p5_drain = [xp_done, p5_tex_last, xmm_last]
    p.close(r5)

    r6 = p.region()
    CT = [p.rsb(r6, f"CT{i}", [128, TOWN], BF16) for i in range(16)]
    rscb = p.rsb(r6, "rscb", [32, TOWN], BF16)
    rscf = p.rsb(r6, "rscf", [32, TOWN], F32)
    plf = p.rps(r6, "plf", [128, 1024])
    psc6 = p.rps(r6, "psc6", [32, TOWN])
    prb = p.rps(r6, "prb", [128, 512])

    exp6 = {}
    sc_last = {}
    for npc in range(16):
        for tg in range(4):
            i = npc * 4 + tg
            par = i % 2
            def lfmm(t, npc=npc, tg=tg, par=par):
                last = None
                for d in range(8):
                    last = t.matmul(plf[:, 512 * par:512 * (par + 1)],
                                    phiw[d][:, 128 * npc:128 * (npc + 1)],
                                    xn0T[d][:, 512 * tg:512 * (tg + 1)],
                                    start=(d == 0), stop=(d == 7))
                return last
            tlf = p.do("t", lfmm, deps=[phi_tok, exp6.get(i - 2)] +
                       (p5_drain if i == 0 else []))
            tex = p.do("s", lambda s, npc=npc, tg=tg, par=par: s.activation(
                CT[npc][:, 512 * tg:512 * (tg + 1)],
                plf[:, 512 * par:512 * (par + 1)], AF.Exp,
                bias=phibcv[:, npc:npc + 1]), deps=[tlf, phibc_tok])
            exp6[i] = tex
            sc_last[tg] = p.do("t", lambda t, npc=npc, tg=tg: t.matmul(
                psc6[:, 512 * tg:512 * (tg + 1)], indP[:],
                CT[npc][:, 512 * tg:512 * (tg + 1)],
                start=(npc == 0), stop=(npc == 15), skip_group_check=True),
                deps=[tex, misc_tok])
    trsc = p.do("s", lambda s: (
        s.activation(rscf[:], psc6[:, :], AF.Ln),
        s.activation(rscb[:], rscf[:], AF.Exp, scale=-1.0))[-1],
        deps=[sc_last[tg] for tg in range(4)])
    ct_store = []
    prb_readers = None
    for tg in range(4):
        tbc = p.do("t", lambda t, tg=tg: t.matmul(
            prb[:, :], indPT[:], rscb[:, 512 * tg:512 * (tg + 1)],
            start=True, stop=True), deps=[trsc, misc_tok, prb_readers])
        sc_mults = []
        for npc in range(16):
            tm = p.do("v", lambda v, npc=npc, tg=tg: v.tensor_tensor(
                CT[npc][:, 512 * tg:512 * (tg + 1)],
                CT[npc][:, 512 * tg:512 * (tg + 1)], prb[:, :], op=ALU.mult),
                deps=[tbc, exp6[npc * 4 + tg]])
            sc_mults.append(tm)
        prb_readers = sc_mults[-1]
        ct_store.append(p.load("y", [
            (CT_o[128 * npc:128 * (npc + 1), 512 * tg:512 * (tg + 1)],
             CT[npc][:, 512 * tg:512 * (tg + 1)]) for npc in range(16)],
            deps=sc_mults))
    p.close(r6)

    p.wait_only("y", s0_store + ct_store + [xp_done])
    return p.finish()



def build_ml2():
    p = P()
    xp_in = p.din("xp_all", [8, 256, D])
    expw = p.din("expw", [EOWN, D, D])
    expb = p.din("expb", [1, EOWN * D])
    yt_out = p.dout("yt", [EOWN, PSL, D])

    p.start()
    ctoks = consts_region(p)

    xp = [p.sb(f"xp{s}", [128, 2 * D], F32) for s in range(8)]
    xacc = [p.sb(f"xacc{c}", [128, D], F32) for c in range(2)]
    xn = [p.sb(f"xn{c}", [128, D], BF16) for c in range(2)]
    xnT = [p.sb(f"xnT{d}", [128, 256], BF16) for d in range(8)]
    cols = p.sb("cols", [128, 8], F32)
    dust = p.sb("dust", [128, D], F32)
    ebf = p.sb("ebf", [1, EOWN * D], BF16)
    strip = [p.sb(f"strip{i}", [128, D], BF16) for i in range(2)]
    ytsb = p.sb("ytsb", [32, D], F32)
    ptr = [p.psum(f"ptr{i}", [128, 1024], BF16) for i in range(2)]
    pyt = p.psum("pyt", [32, 1024], F32)

    scr = {"sum": cols[:, 0:1], "ssq": cols[:, 1:2], "m2": cols[:, 2:3],
           "msq": cols[:, 3:4], "nv": cols[:, 4:5], "dust": dust[:]}

    t_xp = p.load("g", [(xp[s][:, D * c:D * (c + 1)],
                         xp_in[s, 128 * c:128 * (c + 1), :])
                        for s in range(8) for c in range(2)])
    t_eb = p.load("g", [(ebf[0:1, :], expb[0:1, :])])

    xn_toks = []
    for c in range(2):
        tacc = p.do("v", lambda v, c=c: (
            [v.tensor_copy(xacc[c][:], xp[0][:, D * c:D * (c + 1)])] +
            [v.tensor_add(xacc[c][:], xacc[c][:], xp[s][:, D * c:D * (c + 1)])
             for s in range(1, 8)] +
            [v.tensor_scalar_max(xacc[c][:], xacc[c][:], -33000.0),
             v.tensor_scalar_min(xacc[c][:], xacc[c][:], 65000.0)])[-1],
            deps=[t_xp])
        tln = ln_cols_emit(p, xacc[c][:], cols[:, 5:6], cols[:, 6:7], scr,
                           deps=[tacc, ctoks[1]])
        tap = p.do("v", lambda v, c=c: (
            v.tensor_scalar_sub(xacc[c][:], xacc[c][:], cols[:, 5:6]),
            v.tensor_scalar_mul(xacc[c][:], xacc[c][:], cols[:, 6:7]))[-1],
            deps=[tln])
        ttn = p.do("s", lambda s, c=c: s.activation(
            xn[c][:], xacc[c][:], AF.Tanh), deps=[tap])
        xn_toks.append(ttn)

    tr_last = None
    prev_ev = {}
    for c in range(2):
        for d in range(8):
            par = (c * 8 + d) % 2
            ttr = p.do("t", lambda t, c=c, d=d, par=par: t.transpose(
                ptr[par][:, 0:128],
                xn[c][:, 128 * d:128 * (d + 1)], p.ident[:]),
                deps=[xn_toks[c], ctoks[0], prev_ev.get((c * 8 + d) - 2)])
            tev = p.do("v", lambda v, c=c, d=d, par=par: v.tensor_copy(
                xnT[d][:, 128 * c:128 * (c + 1)],
                ptr[par][:, 0:128]), deps=[ttr])
            prev_ev[c * 8 + d] = tev
            tr_last = tev

    yt_stores = []
    sfree = {}
    ev_prev = {}
    for n in range(EOWN):
        last_mm = None
        for dd in range(8):
            sb_i = (n * 8 + dd) % 2
            tld = p.load("g", [(strip[sb_i][:], expw[n, 128 * dd:128 * (dd + 1), :])],
                         deps=[sfree.get(sb_i)])
            def ymm(t, n=n, dd=dd, sb_i=sb_i):
                last = None
                for hg in range(2):
                    last = t.matmul(pyt[0:32, 512 * hg:512 * (hg + 1)],
                                    xnT[dd][:, 32 * n:32 * (n + 1)],
                                    strip[sb_i][:, 512 * hg:512 * (hg + 1)],
                                    start=(dd == 0), stop=False)
                return last
            last_mm = p.do("t", ymm, deps=[tld, tr_last, ev_prev.get(n - 1)])
            sfree[sb_i] = last_mm
        tbias = p.do("t", lambda t, n=n: [
            t.matmul(pyt[0:32, 512 * hg:512 * (hg + 1)],
                     p.ones_row[0:1, 0:32],
                     ebf[0:1, D * n + 512 * hg:D * n + 512 * (hg + 1)],
                     start=False, stop=True)
            for hg in range(2)][-1], deps=[t_eb, ctoks[1], last_mm])
        tev = p.do("v", lambda v, n=n: v.tensor_copy(ytsb[:], pyt[0:32, :]),
                   deps=[tbias, yt_stores[-1] if yt_stores else None])
        ev_prev[n] = tev
        yt_stores.append(p.load("y", [(yt_out[n, :, :], ytsb[0:32, :])],
                                deps=[tev]))
    p.wait_only("y", yt_stores)
    return p.finish()



def build_ml3():
    p = P()
    Yt_i = p.din("YtA", [NPL, D])
    CT_i = p.din("CTi", [NPL, TOWN], BF16)
    s0_i = p.din("s0Ti", [D, TOWN], BF16)
    x_i = p.din("xTown", [D, TOWN])
    x1_o = p.dout("x1T", [D, TOWN])
    xn1_o = p.dout("xn1T", [D, TOWN], BF16)

    p.start()
    ctoks = consts_region(p)
    g0T = [p.sb(f"g0T{d}", [128, TOWN], BF16) for d in range(8)]

    rc = p.region()
    ytb = [p.rsb(rc, f"ytb{i}", [128, D], BF16) for i in range(16)]
    CTv = [p.rsb(rc, f"CTv{i}", [128, TOWN], BF16) for i in range(16)]
    pdis = p.rps(rc, "pdis", [128, 1024])
    t_yt = p.load("g", [(ytb[i][:], Yt_i[128 * i:128 * (i + 1), :])
                        for i in range(16)])
    t_ct = p.load("y", [(CTv[i][:], CT_i[128 * i:128 * (i + 1), :])
                        for i in range(16)])
    evA = {}
    for Hc in range(8):
        for tg in range(4):
            i = Hc * 4 + tg
            par = i % 2
            tmm = p.do("t", lambda t, Hc=Hc, tg=tg, par=par: [
                t.matmul(pdis[:, 512 * par:512 * (par + 1)],
                         ytb[npc][:, 128 * Hc:128 * (Hc + 1)],
                         CTv[npc][:, 512 * tg:512 * (tg + 1)],
                         start=(npc == 0), stop=(npc == 15))
                for npc in range(16)][-1],
                deps=[t_yt, t_ct, evA.get(i - 2)])
            evA[i] = p.do("s", lambda s, Hc=Hc, tg=tg, par=par: s.activation(
                g0T[Hc][:, 512 * tg:512 * (tg + 1)],
                pdis[:, 512 * par:512 * (par + 1)], AF.Silu), deps=[tmm])
    a_drain = [evA[31], evA[30]]
    p.close(rc)

    s0v = [p.sb(f"s0v{d}", [128, TOWN], BF16) for d in range(8)]
    x1f = [p.sb(f"x1f{d}", [128, TOWN], F32) for d in range(8)]
    xn1v = [p.sb(f"xn1v{d}", [128, TOWN], BF16) for d in range(8)]
    xb16 = p.sb("xb16", [128, 512], BF16)
    xsqB = p.sb("xsqB", [128, 512], BF16)
    rowsB = p.sb("rowsB", [1, 3072], F32)
    abfB = p.sb("abfB", [1, 1024], BF16)
    pstatB = p.psum("pstatB", [1, 1024])
    pabB = p.psum("pabB", [128, 1024])
    MEAN, MSQ, NV, TMP, AA, BB = 0, 512, 1024, 1536, 2048, 2560

    t_s0 = p.load("y", [(s0v[d][:], s0_i[128 * d:128 * (d + 1), :])
                        for d in range(8)], deps=a_drain)
    t_x = p.load("g", [(x1f[d][:], x_i[128 * d:128 * (d + 1), :])
                       for d in range(8)], deps=a_drain)
    x1_toks = []
    for d in range(8):
        tx = p.do("v", lambda v, d=d: (
            v.tensor_tensor(g0T[d][:], s0v[d][:], g0T[d][:], op=ALU.mult),
            v.tensor_tensor(x1f[d][:], x1f[d][:], g0T[d][:], op=ALU.add))[-1],
            deps=[t_s0, t_x, evA[31]])
        x1_toks.append(tx)
    x1_st = p.load("y", [(x1_o[128 * d:128 * (d + 1), :], x1f[d][:])
                         for d in range(8)], deps=x1_toks)

    xn1_toks = []
    sqd = None
    tgd = None
    tvd = None
    for tg in range(4):
        c0 = 512 * tg
        sq_toks = []
        for d in range(8):
            tq = p.do("s", lambda s, d=d, c0=c0: (
                s.activation(xb16[:], x1f[d][:, c0:c0 + 512], AF.Copy),
                s.activation(xsqB[:], x1f[d][:, c0:c0 + 512], AF.Square))[-1],
                deps=[x1_toks[d], sqd] if sqd else [x1_toks[d]])
            tmm = p.do("t", lambda t, d=d: (
                t.matmul(pstatB[0:1, 0:512], p.ones_col[:], xb16[:],
                         start=(d == 0), stop=(d == 7)),
                t.matmul(pstatB[0:1, 512:1024], p.ones_col[:], xsqB[:],
                         start=(d == 0), stop=(d == 7)))[-1],
                deps=[tq, ctoks[1], tvd])
            sqd = tmm
            sq_toks.append(tmm)
        tvd = p.do("v", lambda v: (
            v.tensor_scalar_mul(rowsB[0:1, MEAN:MEAN + 512],
                                pstatB[0:1, 0:512], 1.0 / D),
            v.tensor_scalar_mul(rowsB[0:1, MSQ:MSQ + 512],
                                pstatB[0:1, 512:1024], 1.0 / D),
            v.tensor_mul(rowsB[0:1, TMP:TMP + 512], rowsB[0:1, MEAN:MEAN + 512],
                         rowsB[0:1, MEAN:MEAN + 512]),
            v.tensor_sub(rowsB[0:1, NV:NV + 512], rowsB[0:1, TMP:TMP + 512],
                         rowsB[0:1, MSQ:MSQ + 512]))[-1],
            deps=[sq_toks[-1], tgd])
        tsd = p.do("s", lambda s: (
            s.activation(rowsB[0:1, NV:NV + 512], rowsB[0:1, NV:NV + 512],
                         AF.Ln, bias=p.epsc[0:1, 0:1], scale=-1.0),
            s.activation(rowsB[0:1, AA:AA + 512], rowsB[0:1, NV:NV + 512],
                         AF.Exp, scale=-0.5))[-1], deps=[tvd])
        tab = p.do("v", lambda v: (
            v.tensor_mul(rowsB[0:1, TMP:TMP + 512], rowsB[0:1, MEAN:MEAN + 512],
                         rowsB[0:1, AA:AA + 512]),
            v.tensor_scalar_mul(rowsB[0:1, BB:BB + 512],
                                rowsB[0:1, TMP:TMP + 512], -1.0),
            v.tensor_copy(abfB[0:1, 0:512], rowsB[0:1, AA:AA + 512]),
            v.tensor_copy(abfB[0:1, 512:1024], rowsB[0:1, BB:BB + 512]))[-1],
            deps=[tsd])
        tbc = p.do("t", lambda t: (
            t.matmul(pabB[:, 0:512], p.ones_row[0:1, :], abfB[0:1, 0:512],
                     start=True, stop=True),
            t.matmul(pabB[:, 512:1024], p.ones_row[0:1, :], abfB[0:1, 512:1024],
                     start=True, stop=True))[-1],
            deps=[tab, ctoks[1], tgd])
        tgd = p.do("v", lambda v, c0=c0: [
            (v.tensor_tensor(xn1v[d][:, c0:c0 + 512], x1f[d][:, c0:c0 + 512],
                             pabB[:, 0:512], op=ALU.mult),
             v.tensor_tensor(xn1v[d][:, c0:c0 + 512], xn1v[d][:, c0:c0 + 512],
                             pabB[:, 512:1024], op=ALU.add))
            for d in range(8)][-1][-1], deps=[tbc])
        xn1_toks.append(tgd)
    xn1_st = p.load("y", [(xn1_o[128 * d:128 * (d + 1), :], xn1v[d][:])
                          for d in range(8)], deps=xn1_toks)
    p.wait_only("y", [x1_st, xn1_st])
    return p.finish()



def emit_l1(p, ctoks, xn1_i, x1_i, wq2_i, WpT_i, bp_i, w1T_i, w2T_i,
            h1T_i, h2T_i, w3T_i, w1s_i, w2s_i, h1s_i, h2s_i, w3s_i, x2_o,
            wq="g",
            entry_deps=()):
    xn1A = [p.sb(f"xn1A{d}", [128, TOWN], BF16) for d in range(8)]
    s1T = [p.sb(f"s1T{d}", [128, TOWN], BF16) for d in range(8)]
    t_xn1 = p.load("y", [(xn1A[d][:], xn1_i[128 * d:128 * (d + 1), 0:TOWN])
                         for d in range(8)], deps=entry_deps)

    rAtt = p.region()
    attn1T = [p.rsb(rAtt, f"attn1T{d}", [128, TOWN], BF16) for d in range(8)]
    rQ = p.region()
    qT = [p.rsb(rQ, f"qT{c}", [128, NT], BF16) for c in range(8)]
    wq2 = p.rsb(rQ, "wq2", [128, NH * DH], BF16)
    rXb = p.region()
    xn1B = [p.rsb(rXb, f"xn1B{d}", [128, TOWN], BF16) for d in range(8)]
    t_xn1b = p.load("y", [(xn1B[d][:], xn1_i[128 * d:128 * (d + 1), TOWN:NT])
                          for d in range(8)], deps=entry_deps)
    t_wq = p.load("g", [(wq2[:], wq2_i[:, :])], deps=entry_deps)

    rQp = p.region()
    pq = p.rps(rQp, "pq", [64, 1024])
    evq = {}
    for h in range(NH):
        hc, sub = h // 2, 64 * (h % 2)
        for tg in range(8):
            i = h * 8 + tg
            par = i % 2
            xsrc = (xn1A[hc] if tg < 4 else xn1B[hc])
            tgo = tg % 4
            tmm = p.do("t", lambda t, h=h, hc=hc, sub=sub, tgo=tgo, par=par,
                       xsrc=xsrc:
                t.matmul(pq[0:64, 512 * par:512 * (par + 1)],
                         wq2[sub:sub + 64, 64 * h:64 * (h + 1)],
                         xsrc[sub:sub + 64, 512 * tgo:512 * (tgo + 1)],
                         start=True, stop=True),
                deps=[t_wq, t_xn1, t_xn1b, evq.get(i - 2)])
            evq[i] = p.do("s", lambda s, hc=hc, sub=sub, tg=tg, par=par:
                s.activation(qT[hc][sub:sub + 64, 512 * tg:512 * (tg + 1)],
                             pq[0:64, 512 * par:512 * (par + 1)], AF.Copy),
                deps=[tmm])
    q_done = evq[NH * 8 - 1]
    p.close(rQp)
    p.close(rXb)

    v1b = p.rsb(rQ, "v1b", [128, 32 * 65], BF16)
    v1 = [v1b, v1b]
    aql0 = p.rsb(rQ, "aql0", [128, NT], BF16)
    escrT = p.rsb(rQ, "escrT", [128, 2048], BF16)
    escr = [escrT[:, 0:1024], escrT[:, 1024:2048]]
    arho = p.rsb(rQ, "arho", [128, 512], BF16)
    nrm = p.rsb(rQ, "nrm", [128, 16], F32)
    nrm2n = p.rsb(rQ, "nrm2n", [128, 16], F32)
    nrm8 = p.rsb(rQ, "nrm8", [128, 16], F32)
    stabc = p.rsb(rQ, "stabc", [128, 16], F32)
    r1a = p.rsb(rQ, "r1a", [128, 4], F32)
    r1scr = escrT
    avsb = p.rsb(rQ, "avsb", [64, 512], BF16)
    dnrow = p.rsb(rQ, "dnrow", [1, 512], BF16)
    dnrowf = p.rsb(rQ, "dnrowf", [1, 512], F32)
    pr1 = p.rps(rQ, "pr1", [128, 2048])
    pavt = p.rps(rQ, "pav", [128, 1024])
    pav = [pavt[:, 0:512], pavt[:, 512:1024]]
    pdnb = p.rps(rQ, "pdnb", [128, 1024])
    psc2 = [pr1[:, 0:1024], pr1[:, 1024:2048]]
    ptrv = [pr1[:, 0:512].bitcast(BF16), pr1[:, 512:1024].bitcast(BF16)]
    pdn = pdnb[0:64, 0:512]
    pstb = pdnb[0:1, 512:1024]
    vms = p.do("v", lambda v: (v.memset(v1b[:], 1.0),
                               v.memset(aql0[64:65, :], 1.0))[-1],
               deps=[q_done])

    v1use = {0: None, 1: None}
    aqluse = [None]
    exp_g1 = {}
    av_g1 = {}
    jc1 = [0]
    prev_norm1 = {0: (None, None), 1: (None, None)}
    prev_tcp = None
    prev_dn = None
    prev_tno = None
    prev_arho_tok = None
    for h in range(NH):
        hc, sub = h // 2, 64 * (h % 2)
        vb = h % 2
        idsl = p.ident[sub:sub + 64, sub:sub + 64]
        tr_toks = []
        for jg in range(8):
            par = jg % 2
            ttr = p.do("t", lambda t, hc=hc, sub=sub, jg=jg, par=par, idsl=idsl: [
                t.transpose(ptrv[par][:, 64 * q:64 * q + 64],
                            qT[hc][sub:sub + 64,
                                   128 * (4 * jg + q):128 * (4 * jg + q) + 128],
                            idsl)
                for q in range(4)][-1],
                deps=[q_done, ctoks[0], prev_tcp, prev_dn,
                      tr_toks[-2] if len(tr_toks) >= 2 else None])
            tev = p.do("v", lambda v, jg=jg, par=par: v.tensor_copy(
                v1b[:].rearrange("p (a b) -> p a b", a=32, b=65)
                [:, 4 * jg:4 * jg + 4, 0:64],
                ptrv[par][:, 0:256].rearrange("p (a b) -> p a b", a=4, b=64)),
                deps=[ttr, vms, v1use[0], v1use[1]])
            tr_toks.append(tev)
        v1_ready = tr_toks[-1]
        tnrm = p.do("v", lambda v, vb=vb: [
            v.scalar_tensor_tensor(r1scr[:, 0:64], v1[vb][:, 65 * j:65 * j + 64],
                                   1.0, v1[vb][:, 65 * j:65 * j + 64],
                                   op0=ALU.mult, op1=ALU.mult,
                                   accum_out=nrm[:, j:j + 1])
            for j in range(16)][-1], deps=[v1_ready])
        tnsc = p.do("s", lambda s: (
            s.activation(nrm2n[:], nrm[:], AF.Copy, scale=-2.0),
            s.activation(nrm8[:], nrm[:], AF.Copy, scale=8.0))[-1],
            deps=[tnrm])
        taq = p.do("v", lambda v, hc=hc, sub=sub: v.tensor_copy(
            aql0[0:64, :], qT[hc][sub:sub + 64, :]),
            deps=[q_done, vms, aqluse[0]])

        r1_last = None
        for qt in range(16):
            ex_toks = []
            for half in range(2):
                tmm = p.do("t", lambda t, hc=hc, sub=sub, qt=qt, half=half: [
                    t.matmul(pr1[:, 512 * u:512 * (u + 1)],
                             qT[hc][sub:sub + 64, 128 * qt:128 * (qt + 1)],
                             qT[hc][sub:sub + 64,
                                    2048 * half + 512 * u:
                                    2048 * half + 512 * (u + 1)],
                             start=True, stop=True)
                    for u in range(4)][-1],
                    deps=[q_done, tr_toks[-1],
                          ex_toks[-1] if ex_toks else r1_last,
                          prev_tcp if qt == 0 else None,
                          prev_dn if qt == 0 else None])
                tex = p.do("s", lambda s, qt=qt, half=half: s.activation(
                    r1scr[:, :], pr1[:, :], AF.Exp, bias=nrm2n[:, qt:qt + 1],
                    scale=2.0, accum_out=r1a[:, half:half + 1]),
                    deps=[tmm, tnsc])
                ex_toks.append(tex)
            tsum = p.do("v", lambda v: v.tensor_add(
                r1a[:, 2:3], r1a[:, 0:1], r1a[:, 1:2]), deps=ex_toks)
            tln = p.do("s", lambda s: s.activation(
                r1a[:, 3:4], r1a[:, 2:3], AF.Ln), deps=[tsum])
            r1_last = p.do("v", lambda v, qt=qt: v.scalar_tensor_tensor(
                stabc[:, qt:qt + 1], r1a[:, 3:4], 4.0, nrm8[:, qt:qt + 1],
                op0=ALU.mult, op1=ALU.add), deps=[tln])
        r1_all = r1_last

        for g in range(4):
            qs = 512 * g
            gp = g % 2
            tstb = p.do("t", lambda t, g=g: [
                t.transpose(pdnb[0:1, 512 + 128 * k:640 + 128 * k],
                            stabc[:, 4 * g + k:4 * g + k + 1],
                            p.identf[:, :])
                for k in range(4)][-1],
                deps=[r1_all, ctoks[0], p.identf_tok, prev_tcp, prev_dn])
            tar = p.do("v", lambda v, hc=hc, sub=sub, qs=qs: v.tensor_copy(
                arho[0:64, :], qT[hc][sub:sub + 64, qs:qs + 512]),
                deps=[q_done, prev_arho_tok])
            tar2 = p.do("s", lambda s: s.activation(
                arho[64:65, :], pdnb[0:1, 512:1024], AF.Copy, scale=-0.125),
                deps=[tstb, tar, prev_arho_tok])
            av_last = None
            for jj in range(16):
                par = jj % 2
                def smm(t, jj=jj, par=par):
                    last = None
                    for u in range(2):
                        j = 2 * jj + u
                        last = t.matmul(psc2[par][:, 512 * u:512 * (u + 1)],
                                        aql0[0:65, 128 * j:128 * (j + 1)],
                                        arho[0:65, :], start=True, stop=True)
                    return last
                tsc = p.do("t", smm, deps=[tar2, taq, exp_g1.get(jc1[0] - 2)])
                texp = p.do("s", lambda s, par=par: s.activation(
                    escr[par][:], psc2[par][:, :], AF.Exp, scale=8.0),
                    deps=[tsc, av_g1.get(jc1[0] - 2)])
                exp_g1[jc1[0]] = texp
                def avmm(t, jj=jj, par=par, vb=vb, gp=gp):
                    last = None
                    for u in range(2):
                        j = 2 * jj + u
                        last = t.matmul(pav[gp][0:65, :],
                                        v1[vb][:, 65 * j:65 * (j + 1)],
                                        escr[par][:, 512 * u:512 * (u + 1)],
                                        start=(j == 0), stop=(j == 31))
                    return last
                avdeps = [texp, v1_ready]
                if jj == 0:
                    avdeps += list(prev_norm1[0]) + list(prev_norm1[1])
                av_last = p.do("t", avmm, deps=avdeps)
                av_g1[jc1[0]] = av_last
                jc1[0] += 1
            v1use[vb] = av_last
            aqluse[0] = av_last
            prev_arho_tok = av_last
            tcp = p.do("s", lambda s, gp=gp: s.activation(
                avsb[:], pav[gp][0:64, :], AF.Copy),
                deps=[av_last, prev_tno])
            tdn = p.do("s", lambda s, gp=gp: (
                s.activation(dnrowf[0:1, :], pav[gp][64:65, :], AF.Ln),
                s.activation(dnrow[0:1, :], dnrowf[0:1, :], AF.Exp,
                             scale=-1.0))[-1], deps=[av_last, prev_tno])
            tbc = p.do("t", lambda t: t.matmul(
                pdn, p.ones_row[0:1, 0:64], dnrow[0:1, :],
                start=True, stop=True), deps=[tdn, ctoks[1], prev_tno])
            tno = p.do("v", lambda v, hc=hc, sub=sub, qs=qs: v.tensor_tensor(
                attn1T[hc][sub:sub + 64, qs:qs + 512],
                avsb[:], pdn, op=ALU.mult), deps=[tbc, tcp])
            prev_norm1[gp] = (tcp, tdn)
            prev_tcp, prev_dn, prev_tno = tcp, tdn, tno
    attn_done = prev_tno
    p.close(rQ)

    rWp = p.region()
    wp = [p.rsb(rWp, f"wp{d}", [128, D], BF16) for d in range(8)]
    ones512 = p.rsb(rWp, "ones512", [1, 512], BF16)
    t_o512 = p.do("v", lambda v: v.memset(ones512[:], 1.0), deps=[attn_done])
    bpv = p.rsb(rWp, "bpv", [1, D], BF16)
    pwp = p.rps(rWp, "pwp", [128, 1024])
    wp_tok = p.load("g", [(wp[d][:], WpT_i[128 * d:128 * (d + 1), :])
                          for d in range(8)] + [(bpv[0:1, :], bp_i[0:1, :])],
                    deps=[attn_done])
    evW = {}
    for og in range(8):
        for tg in range(4):
            i = og * 4 + tg
            par = i % 2
            def wpmm(t, og=og, tg=tg, par=par):
                for d in range(8):
                    t.matmul(pwp[:, 512 * par:512 * (par + 1)],
                             wp[d][:, 128 * og:128 * (og + 1)],
                             attn1T[d][:, 512 * tg:512 * (tg + 1)],
                             start=(d == 0), stop=False)
                return t.matmul(pwp[:, 512 * par:512 * (par + 1)],
                                bpv[0:1, 128 * og:128 * (og + 1)],
                                ones512[0:1, :], start=False, stop=True)
            tmm = p.do("t", wpmm, deps=[wp_tok, attn_done, t_o512,
                                        evW.get(i - 2)])
            evW[i] = p.do("s", lambda s, og=og, tg=tg, par=par: s.activation(
                s1T[og][:, 512 * tg:512 * (tg + 1)],
                pwp[:, 512 * par:512 * (par + 1)], AF.Copy), deps=[tmm])
    wp_drain = [evW[31], evW[30]]
    p.close(rWp)

    rF = p.region()
    bufA = [p.rsb(rF, f"bufA{i}", [128, 512], BF16) for i in range(32)]
    bufB = [p.rsb(rF, f"bufB{i}", [128, 512], BF16) for i in range(32)]
    g1b = [p.rsb(rF, f"g1b{i}", [128, 512], BF16) for i in range(8)]
    strw = [p.rsb(rF, f"strw{i}", [128, 1024], BF16) for i in range(6)]
    r_tmp = [p.rsb(rF, f"rtmp{i}", [128, 512], BF16) for i in range(2)]
    scw = p.rsb(rF, "scw", [128, 128], F32)
    w3sv = p.rsb(rF, "w3sv", [128, 8], F32)
    xch = p.rsb(rF, "xch", [128, 512], F32)
    x2sb = p.rsb(rF, "x2sb", [128, 512], F32)
    pf = p.rps(rF, "pf", [128, 8 * 512])

    t_sc = p.load("y", [
        (scw[:, 0:32], w1s_i[:, :]), (scw[:, 32:64], w2s_i[:, :]),
        (scw[:, 64:96], h1s_i[:, :]), (scw[:, 96:128], h2s_i[:, :]),
        (w3sv[:, :], w3s_i[:, :])], deps=wp_drain)

    x2_stores = []
    str_free = {}
    sidx = [0]
    prev_round = [None, None, None]

    def stream_mm(wdram, ocg, rhs_deps, prev_evs, rhs_fn, n_dc, evac_fn):
        mm_tok = None
        for dc in range(n_dc):
            si = sidx[0] % 6
            tld = p.load(wq,
                         [(strw[si][:],
                           wdram[128 * dc:128 * (dc + 1),
                                 1024 * ocg:1024 * (ocg + 1)])],
                         deps=[str_free.get(si)] +
                         ([] if si in str_free else list(entry_deps)))
            sidx[0] += 1
            def mmf(t, dc=dc, si=si):
                last = None
                for oo in range(8):
                    last = t.matmul(pf[:, 512 * oo:512 * (oo + 1)],
                                    strw[si][:, 128 * oo:128 * (oo + 1)],
                                    rhs_fn(dc),
                                    start=(dc == 0), stop=(dc == n_dc - 1))
                return last
            mm_tok = p.do("t", mmf,
                          deps=[tld] + rhs_deps +
                          (prev_evs if dc == 0 else []))
            str_free[si] = mm_tok
        return [evac_fn(ocg, oo, mm_tok) for oo in range(8)]

    for tgi in range(4):
        ts0 = 512 * tgi
        pe0 = [t_sc, evW[31]] + [x for x in prev_round if x is not None]
        evA1 = []
        for ocg in range(4):
            def ev1(ocg, oo, mmt):
                oc = 8 * ocg + oo
                return p.do("s", lambda s, oc=oc, oo=oo: s.activation(
                    bufA[oc][:], pf[:, 512 * oo:512 * (oo + 1)], AF.Silu,
                    scale=scw[:, oc:oc + 1]),
                    deps=[mmt, t_sc, prev_round[0]])
            evA1 += stream_mm(w1T_i, ocg, [t_xn1],
                              (pe0 if ocg == 0 else evA1[-8:]),
                              lambda dc, ts0=ts0: xn1A[dc][:, ts0:ts0 + 512],
                              8, ev1)
        evB1 = []
        ev2_prev = {}
        for ocg in range(4):
            def ev2(ocg, oo, mmt):
                oc = 8 * ocg + oo
                ts = p.do("s", lambda s, oc=oc, oo=oo: s.activation(
                    r_tmp[oc % 2][:], pf[:, 512 * oo:512 * (oo + 1)], AF.Copy,
                    scale=scw[:, 32 + oc:33 + oc]),
                    deps=[mmt, t_sc, ev2_prev.get(oc % 2)])
                tv = p.do("v", lambda v, oc=oc: v.tensor_tensor(
                    bufB[oc][:], bufA[oc][:], r_tmp[oc % 2][:], op=ALU.mult),
                    deps=[ts, evA1[oc], prev_round[1]])
                ev2_prev[oc % 2] = tv
                return tv
            evB1 += stream_mm(w2T_i, ocg, [t_xn1],
                              (evA1[-8:] if ocg == 0 else evB1[-8:]),
                              lambda dc, ts0=ts0: xn1A[dc][:, ts0:ts0 + 512],
                              8, ev2)
        evG1 = []
        for ocg in range(4):
            def ev3(ocg, oo, mmt):
                oc = 8 * ocg + oo
                return p.do("s", lambda s, oc=oc, oo=oo: s.activation(
                    bufA[oc][:], pf[:, 512 * oo:512 * (oo + 1)], AF.Gelu,
                    scale=scw[:, 64 + oc:65 + oc]),
                    deps=[mmt, t_sc, evB1[-1]])
            evG1 += stream_mm(h1T_i, ocg, [evB1[-1]],
                              (evB1[-8:] if ocg == 0 else evG1[-8:]),
                              lambda dc: bufB[dc][:], 32, ev3)
        evG2 = []
        for ocg in range(4):
            def ev4(ocg, oo, mmt):
                oc = 8 * ocg + oo
                return p.do("s", lambda s, oc=oc, oo=oo: s.activation(
                    bufB[oc][:], pf[:, 512 * oo:512 * (oo + 1)], AF.Gelu,
                    scale=scw[:, 96 + oc:97 + oc]),
                    deps=[mmt, t_sc, evG1[-1]])
            evG2 += stream_mm(h2T_i, ocg, [evG1[-1]],
                              (evG1[-8:] if ocg == 0 else evG2[-8:]),
                              lambda dc: bufA[dc][:], 32, ev4)
        def ev5(ocg, oo, mmt):
            return p.do("s", lambda s, oo=oo: s.activation(
                g1b[oo][:], pf[:, 512 * oo:512 * (oo + 1)], AF.Silu,
                scale=w3sv[:, oo:oo + 1]),
                deps=[mmt, t_sc, evG2[-1], prev_round[2]])
        evG3 = stream_mm(w3T_i, 0, [evG2[-1]], evG2[-8:],
                         lambda dc: bufB[dc][:], 32, ev5)
        prev_g = None
        for dc in range(8):
            tlx = p.load("y", [(xch[:], x1_i[128 * dc:128 * (dc + 1),
                                             ts0:ts0 + 512])],
                         deps=[prev_g])
            tg2 = p.do("v", lambda v, dc=dc, ts0=ts0: (
                v.tensor_tensor(g1b[dc][:], s1T[dc][:, ts0:ts0 + 512],
                                g1b[dc][:], op=ALU.mult),
                v.tensor_tensor(x2sb[:], xch[:], g1b[dc][:], op=ALU.add))[-1],
                deps=[tlx, evG3[dc], evW[31],
                      x2_stores[-1] if x2_stores else None])
            tsx = p.load("y", [(x2_o[128 * dc:128 * (dc + 1), ts0:ts0 + 512],
                                x2sb[:])], deps=[tg2])
            x2_stores.append(tsx)
            prev_g = tg2
        prev_round = [evG2[-1], evG3[-1], x2_stores[-1]]
    p.close(rF)
    p.wait_only("y", x2_stores[-8:])

def build_ml4():
    p = P()
    xn1_i = p.din("xn1f", [D, NT], BF16)
    x1_i = p.din("x1Ti", [D, TOWN])
    wq2_i = p.din("Wq2", [128, NH * DH])
    WpT_i = p.din("WpT", [D, D])
    bp_i = p.din("bpr", [1, D])
    w1T_i = p.din("w1T", [D, DHID])
    w2T_i = p.din("w2T", [D, DHID])
    h1T_i = p.din("h1T", [DHID, DHID])
    h2T_i = p.din("h2T", [DHID, DHID])
    w3T_i = p.din("w3T", [DHID, D])
    w1s_i = p.din("w1sc", [128, 32])
    w2s_i = p.din("w2sc", [128, 32])
    h1s_i = p.din("h1sc", [128, 32])
    h2s_i = p.din("h2sc", [128, 32])
    w3s_i = p.din("w3sc", [128, 8])
    x2_o = p.dout("x2T", [D, TOWN])

    p.start()
    ctoks = consts_region(p, want_identf=True)
    ctoks = [ctoks[0], ctoks[1]]
    emit_l1(p, ctoks, xn1_i, x1_i, wq2_i, WpT_i, bp_i, w1T_i, w2T_i,
            h1T_i, h2T_i, w3T_i, w1s_i, w2s_i, h1s_i, h2s_i, w3s_i, x2_o,
            entry_deps=[])
    return p.finish()



def build_fused(dbg=False):
    p = P()
    xTa = p.din("xTa", [D, 4 * NT])
    xta = p.din("xta", [4 * NT, D])
    WqkvT_i = p.din("WqkvT", [D, D])
    WoT_i = p.din("WoT", [D, D])
    phiT_i = p.din("phiT", [D, NPL])
    phibr_i = p.din("phibr", [1, NPL])
    phibc_i = p.din("phibc", [128, 16])
    IndP_i = p.din("IndP", [128, 32])
    IndPT_i = p.din("IndPT", [32, 128])
    expw_i = p.din("expwA", [NEXP, D, D])
    expb_i = p.din("expbA", [1, NEXP * D])
    wq2_i = p.din("Wq2", [128, NH * DH])
    WpT_i = p.din("WpT", [D, D])
    bp_i = p.din("bpr", [1, D])
    w1T_i = p.din("w1T", [D, DHID])
    w2T_i = p.din("w2T", [D, DHID])
    h1T_i = p.din("h1T", [DHID, DHID])
    h2T_i = p.din("h2T", [DHID, DHID])
    w3T_i = p.din("w3T", [DHID, D])
    w1s_i = p.din("w1sc", [128, 32])
    w2s_i = p.din("w2sc", [128, 32])
    h1s_i = p.din("h1sc", [128, 32])
    h2s_i = p.din("h2sc", [128, 32])
    w3s_i = p.din("w3sc", [128, 8])
    x2_o = p.dout("x2T", [D, TOWN])
    if dbg:
        dbg_xn0 = p.dout("dbg_xn0", [D, NT], BF16)
        dbg_s0 = p.dout("dbg_s0", [D, NT], BF16)
        dbg_X = p.dout("dbg_X", [NPL, D])
        dbg_yt = p.dout("dbg_yt", [NPL, D], BF16)
        dbg_x1 = p.dout("dbg_x1", [D, TOWN])
        dbg_xn1 = p.dout("dbg_xn1", [D, NT], BF16)
        dbg_xn6 = p.dout("dbg_xn6", [NPL, D], BF16)
        dbg_xnt = p.dout("dbg_xnt", [D, NPL], BF16)

    xn0_sp = p.nc.dram_tensor("xn0_sp", [D, NT], BF16, kind="Internal").ap()
    s0_sp = p.nc.dram_tensor("s0_sp", [D, NT], BF16, kind="Internal").ap()
    yt_sp = p.nc.dram_tensor("yt_sp", [NPL, D], BF16, kind="Internal").ap()
    x1_sp = p.nc.dram_tensor("x1_sp", [D, TOWN], F32, kind="Internal").ap()
    xn1_sp = p.nc.dram_tensor("xn1_sp", [D, NT], BF16, kind="Internal").ap()
    expwb = p.nc.dram_tensor("expwb", [NEXP, D, D], BF16, kind="Internal").ap()
    expbb = p.nc.dram_tensor("expbb", [1, NEXP * D], BF16, kind="Internal").ap()
    w1b = p.nc.dram_tensor("w1b", [D, DHID], BF16, kind="Internal").ap()
    w2b = p.nc.dram_tensor("w2b", [D, DHID], BF16, kind="Internal").ap()
    h1b = p.nc.dram_tensor("h1b", [DHID, DHID], BF16, kind="Internal").ap()
    h2b = p.nc.dram_tensor("h2b", [DHID, DHID], BF16, kind="Internal").ap()
    w3b = p.nc.dram_tensor("w3b", [DHID, D], BF16, kind="Internal").ap()

    p.start()
    ctoks = consts_region(p, want_identf=True)
    MEAN, MSQ, NV, TMP, AA, BB = 0, 512, 1024, 1536, 2048, 2560

    rX0 = p.region()
    xn0T = [p.rsb(rX0, f"xn0T{d}", [128, NT], BF16) for d in range(8)]
    r1 = p.region()
    xball = p.rsb(r1, "xball", [128, NT], BF16)
    xsq = p.rsb(r1, "xsq", [128, NT], BF16)
    rowscr = [p.rsb(r1, f"rowscr{i}", [1, 3072], F32) for i in range(2)]
    abf = [p.rsb(r1, f"abf{i}", [1, 1024], BF16) for i in range(2)]
    pstat = [p.rps(r1, f"pstat{i}", [1, 1024]) for i in range(2)]
    pab = [p.rps(r1, f"pab{i}", [128, 1024]) for i in range(2)]

    def emit_row_ln(srcT, n_tg, dst_write, extra_ld_deps, loader_eng="g"):
        tg_done = [None] * n_tg
        sq_done = [None] * n_tg
        mm_done = [None] * n_tg
        tv_done = [None] * n_tg
        ts_done = [None] * n_tg
        tbc_done = [None] * n_tg
        for tg in range(n_tg):
            c0 = 512 * tg
            pr = tg % 2
            ldeps = list(extra_ld_deps) + [ctoks[1]] + (
                [sq_done[tg - 1], mm_done[tg - 1], tg_done[tg - 1]] if tg else [])
            tld = p.load(loader_eng, [(xball[:, 512 * d:512 * (d + 1)],
                                       srcT[128 * d:128 * (d + 1), c0:c0 + 512])
                                      for d in range(8)], deps=ldeps)
            sq_done[tg] = p.do("s", lambda s: [
                s.activation(xsq[:, 512 * d:512 * (d + 1)],
                             xball[:, 512 * d:512 * (d + 1)], AF.Square)
                for d in range(8)][-1], deps=[tld])

            def statmm(t, pr=pr):
                for d in range(8):
                    t.matmul(pstat[pr][0:1, 0:512], p.ones_col[:],
                             xball[:, 512 * d:512 * (d + 1)],
                             start=(d == 0), stop=(d == 7))
                last = None
                for d in range(8):
                    last = t.matmul(pstat[pr][0:1, 512:1024], p.ones_col[:],
                                    xsq[:, 512 * d:512 * (d + 1)],
                                    start=(d == 0), stop=(d == 7))
                return last
            mm_done[tg] = p.do("t", statmm,
                               deps=[tld, sq_done[tg], ctoks[1],
                                     tv_done[tg - 2] if tg >= 2 else None])

            def rmath_v(v, pr=pr):
                v.tensor_scalar_mul(rowscr[pr][0:1, MEAN:MEAN + 512],
                                    pstat[pr][0:1, 0:512], 1.0 / D)
                v.tensor_scalar_mul(rowscr[pr][0:1, MSQ:MSQ + 512],
                                    pstat[pr][0:1, 512:1024], 1.0 / D)
                v.tensor_mul(rowscr[pr][0:1, TMP:TMP + 512],
                             rowscr[pr][0:1, MEAN:MEAN + 512],
                             rowscr[pr][0:1, MEAN:MEAN + 512])
                return v.tensor_sub(rowscr[pr][0:1, NV:NV + 512],
                                    rowscr[pr][0:1, TMP:TMP + 512],
                                    rowscr[pr][0:1, MSQ:MSQ + 512])
            tv_done[tg] = p.do("v", rmath_v,
                               deps=[mm_done[tg],
                                     ts_done[tg - 2] if tg >= 2 else None])

            def rmath_s(s, pr=pr):
                s.activation(rowscr[pr][0:1, NV:NV + 512],
                             rowscr[pr][0:1, NV:NV + 512],
                             AF.Ln, bias=p.epsc[0:1, 0:1], scale=-1.0)
                return s.activation(rowscr[pr][0:1, AA:AA + 512],
                                    rowscr[pr][0:1, NV:NV + 512],
                                    AF.Exp, scale=-0.5)
            ts_done[tg] = p.do("s", rmath_s, deps=[tv_done[tg]])
            tab = p.do("v", lambda v, pr=pr: (
                v.tensor_mul(rowscr[pr][0:1, TMP:TMP + 512],
                             rowscr[pr][0:1, MEAN:MEAN + 512],
                             rowscr[pr][0:1, AA:AA + 512]),
                v.tensor_scalar_mul(rowscr[pr][0:1, BB:BB + 512],
                                    rowscr[pr][0:1, TMP:TMP + 512], -1.0),
                v.tensor_copy(abf[pr][0:1, 0:512],
                              rowscr[pr][0:1, AA:AA + 512]),
                v.tensor_copy(abf[pr][0:1, 512:1024],
                              rowscr[pr][0:1, BB:BB + 512]))[-1],
                deps=[ts_done[tg],
                      tbc_done[tg - 2] if tg >= 2 else None])
            tbc = p.do("t", lambda t, pr=pr: (
                t.matmul(pab[pr][:, 0:512], p.ones_row[0:1, :],
                         abf[pr][0:1, 0:512], start=True, stop=True),
                t.matmul(pab[pr][:, 512:1024], p.ones_row[0:1, :],
                         abf[pr][0:1, 512:1024], start=True, stop=True))[-1],
                deps=[tab, ctoks[1], tg_done[tg - 2] if tg >= 2 else None])
            tbc_done[tg] = tbc
            tg_done[tg] = dst_write(tg, c0, [tbc], pr)
        return tg_done, xball

    def xn0_write(tg, c0, deps, pr):
        return p.do("v", lambda v: [
            (v.tensor_tensor(xn0T[d][:, c0:c0 + 512],
                             xball[:, 512 * d:512 * (d + 1)],
                             pab[pr][:, 0:512], op=ALU.mult),
             v.tensor_tensor(xn0T[d][:, c0:c0 + 512], xn0T[d][:, c0:c0 + 512],
                             pab[pr][:, 512:1024], op=ALU.add))
            for d in range(8)][-1][-1], deps=deps)

    tg_done, _ = emit_row_ln(xTa, 8, xn0_write, [])
    xn0_spt = p.load("y", [(xn0_sp[128 * d:128 * (d + 1), :], xn0T[d][:])
                           for d in range(8)], deps=tg_done)
    p.close(r1)

    attn_sp = p.nc.dram_tensor("attn_sp", [D, NT], BF16, kind="Internal").ap()
    rA0 = p.region()
    qkvT = [p.rsb(rA0, f"qkvT{c}", [128, NT], BF16) for c in range(8)]
    rW0 = p.region()
    wqkv = [p.rsb(rW0, f"wqkv{d}", [128, D], BF16) for d in range(8)]
    pqk = p.rps(rW0, "pqk", [128, 1024])
    w_tok = p.load("g", [(wqkv[d][:], WqkvT_i[128 * d:128 * (d + 1), :])
                         for d in range(8)],
                   deps=tg_done + [xn0_spt])
    p1_drain = [tg_done[7]]
    ev2 = {}
    for og in range(8):
        for tg in range(8):
            i = og * 8 + tg
            par = i % 2
            tmm = p.do("t", lambda t, og=og, tg=tg, par=par: [
                t.matmul(pqk[:, 512 * par:512 * (par + 1)],
                         wqkv[d][:, 128 * og:128 * (og + 1)],
                         xn0T[d][:, 512 * tg:512 * (tg + 1)],
                         start=(d == 0), stop=(d == 7))
                for d in range(8)][-1],
                deps=[w_tok, tg_done[tg], ev2.get(i - 2)] + p1_drain)
            ev2[i] = p.do("s", lambda s, og=og, tg=tg, par=par: s.activation(
                qkvT[og][:, 512 * tg:512 * (tg + 1)],
                pqk[:, 512 * par:512 * (par + 1)], AF.Copy),
                deps=[tmm])
    qkv_done = ev2[63]
    p.close(rW0)

    pc_pairs = []
    for n8 in range(8):
        pc_pairs.append((expwb[8 * n8:8 * (n8 + 1), :, :],
                         expw_i[8 * n8:8 * (n8 + 1), :, :]))
    pc_pairs.append((expbb[0:1, :], expb_i[0:1, :]))
    for c4 in range(4):
        pc_pairs.append((w1b[256 * c4:256 * (c4 + 1), :],
                         w1T_i[256 * c4:256 * (c4 + 1), :]))
        pc_pairs.append((w2b[256 * c4:256 * (c4 + 1), :],
                         w2T_i[256 * c4:256 * (c4 + 1), :]))
        pc_pairs.append((w3b[1024 * c4:1024 * (c4 + 1), :],
                         w3T_i[1024 * c4:1024 * (c4 + 1), :]))
    for c8 in range(8):
        pc_pairs.append((h1b[512 * c8:512 * (c8 + 1), :],
                         h1T_i[512 * c8:512 * (c8 + 1), :]))
        pc_pairs.append((h2b[512 * c8:512 * (c8 + 1), :],
                         h2T_i[512 * c8:512 * (c8 + 1), :]))
    pc_tok = p.load("g", pc_pairs)

    attnc = [p.rsb(rA0, f"attnc{i}", [128, NT], BF16) for i in range(2)]
    v1_0 = [p.rsb(rA0, f"v1z{i}", [128, 32 * 65], BF16) for i in range(2)]
    escr0 = [p.rsb(rA0, f"escrz{i}", [128, 1024], BF16) for i in range(2)]
    avsb0 = p.rsb(rA0, "avsbz", [64, 512], BF16)
    dnrow0 = p.rsb(rA0, "dnrowz", [1, 512], BF16)
    dnrowf0 = p.rsb(rA0, "dnrowfz", [1, 512], F32)
    ptr0t = p.rps(rA0, "ptrz", [128, 1024], BF16)
    ptr0 = [ptr0t[:, 0:512], ptr0t[:, 512:1024]]
    psc0 = [p.rps(rA0, f"pscz{i}", [128, 1024]) for i in range(2)]
    pav0t = p.rps(rA0, "pavz", [128, 1024])
    pav0 = [pav0t[:, 0:512], pav0t[:, 512:1024]]
    pdn0 = p.rps(rA0, "pdnz", [64, 512])
    vms0 = p.do("v", lambda v: (v.memset(v1_0[0][:], 1.0),
                                v.memset(v1_0[1][:], 1.0))[-1], deps=[qkv_done])

    v1use = {0: None, 1: None}
    prev_tcp = None
    prev_trc = None
    prev_tno = None
    attn_spills = {}
    exp_g0 = {}
    av_g0 = {}
    jc0 = [0]
    for h in range(NH):
        hc, sub = h // 2, 64 * (h % 2)
        vb = h % 2
        idsl = p.ident[sub:sub + 64, sub:sub + 64]
        tr_toks = []
        for jg in range(8):
            par = jg % 2
            ttr = p.do("t", lambda t, hc=hc, sub=sub, jg=jg, par=par, idsl=idsl: [
                t.transpose(ptr0[par][:, 64 * q:64 * q + 64],
                            qkvT[hc][sub:sub + 64,
                                     128 * (4 * jg + q):128 * (4 * jg + q) + 128],
                            idsl)
                for q in range(4)][-1],
                deps=[qkv_done, ctoks[0],
                      tr_toks[-1] if tr_toks else None])
            tev = p.do("v", lambda v, jg=jg, par=par, vb=vb: v.tensor_copy(
                v1_0[vb][:].rearrange("p (a b) -> p a b", a=32, b=65)
                [:, 4 * jg:4 * jg + 4, 0:64],
                ptr0[par][:, 0:256].rearrange("p (a b) -> p a b", a=4, b=64)),
                deps=[ttr, vms0, v1use[vb]])
            tr_toks.append(tev)
        v1_ready = tr_toks[-1]
        prev_norm = {0: (None, None), 1: (None, None)}
        for g in range(8):
            qs = 512 * g
            gp = g % 2
            av_last = None
            for jj in range(16):
                par = jj % 2
                def smm(t, hc=hc, sub=sub, jj=jj, par=par, qs=qs):
                    last = None
                    for u in range(2):
                        j = 2 * jj + u
                        last = t.matmul(psc0[par][:, 512 * u:512 * (u + 1)],
                                        qkvT[hc][sub:sub + 64,
                                                 128 * j:128 * (j + 1)],
                                        qkvT[hc][sub:sub + 64, qs:qs + 512],
                                        start=True, stop=True)
                    return last
                tsc = p.do("t", smm, deps=[qkv_done, exp_g0.get(jc0[0] - 2)])
                texp = p.do("s", lambda s, par=par: s.activation(
                    escr0[par][:], psc0[par][:], AF.Exp,
                    bias=p.c20[:, 0:1], scale=0.125),
                    deps=[tsc, ctoks[1], av_g0.get(jc0[0] - 2)])
                exp_g0[jc0[0]] = texp
                def avmm(t, jj=jj, par=par, vb=vb, gp=gp):
                    last = None
                    for u in range(2):
                        j = 2 * jj + u
                        last = t.matmul(pav0[gp][0:65, :],
                                        v1_0[vb][:, 65 * j:65 * (j + 1)],
                                        escr0[par][:, 512 * u:512 * (u + 1)],
                                        start=(j == 0), stop=(j == 31))
                    return last
                avdeps = [texp, v1_ready]
                if jj == 0:
                    avdeps += list(prev_norm[0]) + list(prev_norm[1])
                av_last = p.do("t", avmm, deps=avdeps)
                av_g0[jc0[0]] = av_last
                jc0[0] += 1
            v1use[vb] = av_last
            tcp = p.do("s", lambda s, gp=gp: s.activation(
                avsb0[:], pav0[gp][0:64, :], AF.Copy),
                deps=[av_last, prev_tno])
            trc = p.do("s", lambda s, gp=gp: (
                s.activation(dnrowf0[0:1, :], pav0[gp][64:65, :], AF.Ln),
                s.activation(dnrow0[0:1, :], dnrowf0[0:1, :], AF.Exp,
                             scale=-1.0))[-1], deps=[av_last, prev_tno])
            tbc = p.do("t", lambda t: t.matmul(
                pdn0[:, :], p.ones_row[0:1, 0:64], dnrow0[0:1, :],
                start=True, stop=True), deps=[trc, ctoks[1], prev_tno])
            tno = p.do("v", lambda v, hc=hc, sub=sub, qs=qs: v.tensor_tensor(
                attnc[hc % 2][sub:sub + 64, qs:qs + 512],
                avsb0[:], pdn0[:, :], op=ALU.mult),
                deps=[tbc, tcp, attn_spills.get(hc - 2)])
            prev_norm[gp] = (tcp, trc)
            prev_tcp, prev_trc, prev_tno = tcp, trc, tno
        if h % 2 == 1:
            attn_spills[hc] = p.load(
                "y", [(attn_sp[128 * hc:128 * (hc + 1), :], attnc[hc % 2][:])],
                deps=[prev_tno])
    attn_done = attn_spills[7]
    p.close(rA0)
    p.close(rX0)

    rF4 = p.region()
    wo = [p.rsb(rF4, f"wo{d}", [128, D], BF16) for d in range(8)]
    s0sb = p.rsb(rF4, "s0sb", [128, 512], BF16)
    arh = [p.rsb(rF4, f"arh{i}", [128, 512], BF16) for i in range(2)]
    pwo = p.rps(rF4, "pwo", [128, 8 * 512])
    wo_tok = p.load("g", [(wo[d][:], WoT_i[128 * d:128 * (d + 1), :])
                          for d in range(8)],
                    deps=[attn_done, prev_tno, prev_tcp])
    s0_store = []
    arh_free = {}
    ev4 = {}
    for tg in range(8):
        mm_last = None
        for d in range(8):
            par = (tg * 8 + d) % 2
            tla = p.load("y", [(arh[par][:],
                                attn_sp[128 * d:128 * (d + 1),
                                        512 * tg:512 * (tg + 1)])],
                         deps=[arh_free.get(par), attn_spills[d]])
            def womm(t, d=d, par=par):
                last = None
                for og in range(8):
                    last = t.matmul(pwo[:, 512 * og:512 * (og + 1)],
                                    wo[d][:, 128 * og:128 * (og + 1)],
                                    arh[par][:],
                                    start=(d == 0), stop=(d == 7))
                return last
            mm_last = p.do("t", womm,
                           deps=[tla, wo_tok] +
                           ([ev4.get(tg - 1)] if d == 0 else []))
            arh_free[par] = mm_last
        evs = []
        for og in range(8):
            tev = p.do("s", lambda s, og=og: s.activation(
                s0sb[:], pwo[:, 512 * og:512 * (og + 1)], AF.Copy),
                deps=[mm_last, s0_store[-1] if s0_store else None])
            tst = p.load("y", [(s0_sp[128 * og:128 * (og + 1),
                                      512 * tg:512 * (tg + 1)], s0sb[:])],
                         deps=[tev])
            s0_store.append(tst)
            evs.append(tev)
        ev4[tg] = evs[-1]
    p4_drain = [s0_store[-1], ev4[7]]
    p.close(rF4)

    rXa = p.region()
    Xacc = [p.rsb(rXa, f"Xacc{i}", [128, D], F32) for i in range(16)]
    rS = p.region()
    xnsT = [p.rsb(rS, f"xnsT{d}", [128, 1024], BF16) for d in range(8)]
    xnstk = p.rsb(rS, "xnstk", [128, 8 * D], BF16)
    xtkf = p.rsb(rS, "xtkf", [128, D], F32)
    phiw = [p.rsb(rS, f"phiw{d}", [128, NPL], BF16) for d in range(8)]
    phibv = p.rsb(rS, "phibv", [1, NPL], BF16)
    colscr = p.rsb(rS, "colscr", [128, 8], F32)
    dustb = p.rsb(rS, "dustb", [128, D], F32)
    Eb5 = [p.rsb(rS, f"Eb5{i}", [128, 512], BF16) for i in range(2)]
    sd5 = p.rsb(rS, "sd5", [128, 16], F32)
    rsd5 = p.rsb(rS, "rsd5", [128, 16], F32)
    Dg5 = p.rsb(rS, "Dg5", [128, 8 * 512], BF16)
    sb_r = p.rsb(rS, "sb_r", [1, 3072], F32)
    ab_r = p.rsb(rS, "ab_r", [1, 1024], BF16)
    xbs = p.rsb(rS, "xbs", [128, 1024], BF16)
    xqs = p.rsb(rS, "xqs", [128, 1024], BF16)
    pst5 = p.rps(rS, "pst5", [1, 1024])
    pab5 = p.rps(rS, "pab5", [128, 1024])
    plg5 = p.rps(rS, "plg5", [128, 1024])
    pxp5 = p.rps(rS, "pxp5", [128, 1024])
    scr5 = {"sum": colscr[:, 0:1], "ssq": colscr[:, 1:2], "m2": colscr[:, 2:3],
            "msq": colscr[:, 3:4], "nv": colscr[:, 4:5], "dust": dustb[:]}

    phi_tok = p.load("g", [(phiw[d][:], phiT_i[128 * d:128 * (d + 1), :])
                           for d in range(8)] +
                     [(phibv[0:1, :], phibr_i[0:1, :])], deps=p4_drain)
    xacc_init = p.do("v", lambda v: [v.memset(Xacc[i][:], 0.0)
                                     for i in range(16)][-1], deps=p4_drain)

    if dbg:
        p.load("y", [(dbg_xn0[:, :], xn0_sp[:, :])], deps=[xn0_spt])
        p.load("y", [(dbg_s0[:, :], s0_sp[:, :])], deps=[s0_store[-1]])
    f5_gate = p.do("v", lambda v: None, deps=p4_drain + [phi_tok])
    e_free5 = {}
    gcnt5 = [0]
    xpf5 = {}
    xcnt5 = [0]
    sup_free = f5_gate
    xmm5_last = None
    xa_tok = {i: xacc_init for i in range(16)}
    for sup in range(16):
        t0s = 1024 * sup
        tok_toks = []
        xtk_free = sup_free
        for tc in range(8):
            tl = p.load("y", [(xtkf[:], xta[t0s + 128 * tc:t0s + 128 * (tc + 1), :])],
                        deps=[xtk_free])
            tln = ln_cols_emit(p, xtkf[:], colscr[:, 5:6], colscr[:, 6:7], scr5,
                               deps=[tl, ctoks[1]])
            xtk_free = p.do("v", lambda v, tc=tc: (
                v.tensor_scalar_sub(xtkf[:], xtkf[:], colscr[:, 5:6]),
                v.tensor_scalar_mul(xnstk[:, D * tc:D * (tc + 1)], xtkf[:],
                                    colscr[:, 6:7]))[-1], deps=[tln])
            tok_toks.append(xtk_free)
        ftoks = []
        prev = {"sq": None, "mm": None, "ap": sup_free, "tv": None}
        for tg2 in range(2):
            c0 = 512 * tg2
            sqs = []
            mms = []
            for d in range(8):
                tldd = p.load("g", [(xbs[:, 0:512],
                                     xTa[128 * d:128 * (d + 1),
                                         t0s + c0:t0s + c0 + 512])],
                              deps=[prev["mm"], prev["ap"], prev["sq"]])
                tsq = p.do("s", lambda s: s.activation(
                    xqs[:, 0:512], xbs[:, 0:512], AF.Square), deps=[tldd])
                tm = p.do("t", lambda t, d=d: (
                    t.matmul(pst5[0:1, 0:512], p.ones_col[:], xbs[:, 0:512],
                             start=(d == 0), stop=(d == 7)),
                    t.matmul(pst5[0:1, 512:1024], p.ones_col[:], xqs[:, 0:512],
                             start=(d == 0), stop=(d == 7)))[-1],
                    deps=[tldd, tsq, ctoks[1], prev["tv"]])
                prev["mm"] = tm
                prev["sq"] = tm
                sqs.append(tsq)
                mms.append(tm)
            tv = p.do("v", lambda v: (
                v.tensor_scalar_mul(sb_r[0:1, MEAN:MEAN + 512],
                                    pst5[0:1, 0:512], 1.0 / D),
                v.tensor_scalar_mul(sb_r[0:1, MSQ:MSQ + 512],
                                    pst5[0:1, 512:1024], 1.0 / D),
                v.tensor_mul(sb_r[0:1, TMP:TMP + 512],
                             sb_r[0:1, MEAN:MEAN + 512],
                             sb_r[0:1, MEAN:MEAN + 512]),
                v.tensor_sub(sb_r[0:1, NV:NV + 512],
                             sb_r[0:1, TMP:TMP + 512],
                             sb_r[0:1, MSQ:MSQ + 512]))[-1], deps=[mms[-1]])
            prev["tv"] = tv
            tss = p.do("s", lambda s: (
                s.activation(sb_r[0:1, NV:NV + 512], sb_r[0:1, NV:NV + 512],
                             AF.Ln, bias=p.epsc[0:1, 0:1], scale=-1.0),
                s.activation(sb_r[0:1, AA:AA + 512], sb_r[0:1, NV:NV + 512],
                             AF.Exp, scale=-0.5))[-1], deps=[tv])
            tabr = p.do("v", lambda v: (
                v.tensor_mul(sb_r[0:1, TMP:TMP + 512],
                             sb_r[0:1, MEAN:MEAN + 512],
                             sb_r[0:1, AA:AA + 512]),
                v.tensor_scalar_mul(sb_r[0:1, BB:BB + 512],
                                    sb_r[0:1, TMP:TMP + 512], -1.0),
                v.tensor_copy(ab_r[0:1, 0:512], sb_r[0:1, AA:AA + 512]),
                v.tensor_copy(ab_r[0:1, 512:1024], sb_r[0:1, BB:BB + 512]))[-1],
                deps=[tss])
            tbc5 = p.do("t", lambda t: (
                t.matmul(pab5[:, 0:512], p.ones_row[0:1, :], ab_r[0:1, 0:512],
                         start=True, stop=True),
                t.matmul(pab5[:, 512:1024], p.ones_row[0:1, :],
                         ab_r[0:1, 512:1024], start=True, stop=True))[-1],
                deps=[tabr, ctoks[1], ftoks[-1] if ftoks else None])
            ap_prev = None
            for d in range(8):
                tldd = p.load("g", [(xbs[:, 512:1024],
                                     xTa[128 * d:128 * (d + 1),
                                         t0s + c0:t0s + c0 + 512])],
                              deps=[ap_prev, tbc5])
                ap_prev = p.do("v", lambda v, d=d, c0=c0: (
                    v.tensor_tensor(xnsT[d][:, c0:c0 + 512], xbs[:, 512:1024],
                                    pab5[:, 0:512], op=ALU.mult),
                    v.tensor_tensor(xnsT[d][:, c0:c0 + 512],
                                    xnsT[d][:, c0:c0 + 512],
                                    pab5[:, 512:1024], op=ALU.add))[-1],
                    deps=[tldd, tbc5])
            ftoks.append(ap_prev)
        xn_sup = ftoks[-1]

        sup_last = None
        for npg in range(4):
            d_toks = []
            for tc in range(8):
                par = tc % 2
                gi = npg * 8 + tc
                def lgmm(t, npg=npg, tc=tc, par=par):
                    for d in range(8):
                        t.matmul(plg5[:, 512 * par:512 * (par + 1)],
                                 xnsT[d][:, 128 * tc:128 * (tc + 1)],
                                 phiw[d][:, 512 * npg:512 * (npg + 1)],
                                 start=(d == 0), stop=False)
                    return t.matmul(plg5[:, 512 * par:512 * (par + 1)],
                                    p.ones_row[0:1, :],
                                    phibv[0:1, 512 * npg:512 * (npg + 1)],
                                    start=False, stop=True)
                tlg = p.do("t", lgmm, deps=[phi_tok, ctoks[1], xn_sup,
                                            e_free5.get(gcnt5[0] - 2)])
                tex = p.do("s", lambda s, par=par: s.activation(
                    Eb5[par][:], plg5[:, 512 * par:512 * (par + 1)], AF.Exp),
                    deps=[tlg])
                tsd0 = p.do("v", lambda v, par=par: v.tensor_reduce(
                    sd5[:], Eb5[par][:].rearrange("p (a b) -> p a b", a=16, b=32),
                    axis=AX.X, op=ALU.add), deps=[tex, e_free5.get(gcnt5[0] - 1)])
                tsd = p.do("s", lambda s: (
                    s.activation(sd5[:], sd5[:], AF.Ln),
                    s.activation(rsd5[:], sd5[:], AF.Exp, scale=-1.0))[-1],
                    deps=[tsd0])
                tdg = p.do("v", lambda v, tc=tc, par=par: [
                    v.tensor_scalar_mul(
                        Dg5[:, 512 * tc + 32 * n:512 * tc + 32 * (n + 1)],
                        Eb5[par][:, 32 * n:32 * (n + 1)], rsd5[:, n:n + 1])
                    for n in range(16)][-1], deps=[tsd, xmm5_last])
                e_free5[gcnt5[0]] = tdg
                gcnt5[0] += 1
                d_toks.append(tdg)
            for s_ in range(4):
                npc = 4 * npg + s_
                for dg in range(2):
                    i = s_ * 2 + dg
                    par = i % 2
                    def xmm(t, s_=s_, dg=dg, par=par):
                        last = None
                        for tc in range(8):
                            last = t.matmul(
                                pxp5[:, 512 * par:512 * (par + 1)],
                                Dg5[:, 512 * tc + 128 * s_:
                                    512 * tc + 128 * (s_ + 1)],
                                xnstk[:, D * tc + 512 * dg:
                                      D * tc + 512 * (dg + 1)],
                                start=(tc == 0), stop=(tc == 7))
                        return last
                    txm = p.do("t", xmm,
                               deps=d_toks + tok_toks +
                               [xpf5.get(xcnt5[0] - 2)])
                    xmm5_last = txm
                    tacc = p.do("v", lambda v, npc=npc, dg=dg, par=par:
                        v.tensor_tensor(Xacc[npc][:, 512 * dg:512 * (dg + 1)],
                                        Xacc[npc][:, 512 * dg:512 * (dg + 1)],
                                        pxp5[:, 512 * par:512 * (par + 1)],
                                        op=ALU.add),
                        deps=[txm, xa_tok[npc]])
                    xa_tok[npc] = tacc
                    xpf5[xcnt5[0]] = tacc
                    xcnt5[0] += 1
                    sup_last = tacc
        sup_free = sup_last
    p5_tex_last = sup_last
    p.close(rS)

    rY = p.region()
    Xn6 = [p.rsb(rY, f"Xn6_{i}", [128, D], BF16) for i in range(16)]
    XnT6 = [p.rsb(rY, f"XnT6_{d}", [128, NPL], BF16) for d in range(8)]
    cols6 = p.rsb(rY, "cols6", [128, 8], F32)
    dust6 = p.rsb(rY, "dust6", [128, D], F32)
    ebrow = [p.rsb(rY, f"ebrow{i}", [1, D], BF16) for i in range(2)]
    strip6 = [p.rsb(rY, f"strip6_{i}", [128, D], BF16) for i in range(4)]
    ytsb6 = [p.rsb(rY, f"ytsb6_{i}", [32, D], BF16) for i in range(2)]
    ptr6 = [p.rps(rY, f"ptr6_{i}", [128, 1024], BF16) for i in range(2)]
    pyt6t = [p.rps(rY, f"pyt6_{i}", [32, 1024]) for i in range(2)]
    scr6 = {"sum": cols6[:, 0:1], "ssq": cols6[:, 1:2], "m2": cols6[:, 2:3],
            "msq": cols6[:, 3:4], "nv": cols6[:, 4:5], "dust": dust6[:]}

    if dbg:
        dbgx = p.rsb(rY, "dbgx", [128, D], F32)
        tds = None
        for i in range(16):
            tdc = p.do("v", lambda v, i=i: v.tensor_copy(dbgx[:], Xacc[i][:]),
                       deps=[xa_tok[i], tds])
            tds = p.load("y", [(dbg_X[128 * i:128 * (i + 1), :], dbgx[:])],
                         deps=[tdc])
            xa_tok[i] = tds
    xn6_toks = []
    for i in range(16):
        tcl = p.do("v", lambda v, i=i: (
            v.tensor_scalar_max(Xacc[i][:], Xacc[i][:], -33000.0),
            v.tensor_scalar_min(Xacc[i][:], Xacc[i][:], 65000.0))[-1],
            deps=[xa_tok[i]])
        tln = ln_cols_emit(p, Xacc[i][:], cols6[:, 5:6], cols6[:, 6:7], scr6,
                           deps=[tcl, ctoks[1], xn6_toks[-1] if xn6_toks else None])
        tap = p.do("v", lambda v, i=i: (
            v.tensor_scalar_sub(Xacc[i][:], Xacc[i][:], cols6[:, 5:6]),
            v.tensor_scalar_mul(Xacc[i][:], Xacc[i][:], cols6[:, 6:7]))[-1],
            deps=[tln])
        ttn = p.do("s", lambda s, i=i: s.activation(
            Xn6[i][:], Xacc[i][:], AF.Tanh), deps=[tap])
        xn6_toks.append(ttn)
    tr6_last = None
    prev_ev6 = {}
    k6 = 0
    for i in range(16):
        for d in range(8):
            par = k6 % 2
            ttr = p.do("t", lambda t, i=i, d=d, par=par: t.transpose(
                ptr6[par][:, 0:128], Xn6[i][:, 128 * d:128 * (d + 1)],
                p.ident[:]),
                deps=[xn6_toks[i], ctoks[0], prev_ev6.get(k6 - 2)])
            tev = p.do("v", lambda v, i=i, d=d, par=par: v.tensor_copy(
                XnT6[d][:, 128 * i:128 * (i + 1)], ptr6[par][:, 0:128]),
                deps=[ttr])
            prev_ev6[k6] = tev
            tr6_last = tev
            k6 += 1
    dbg_dump6 = []
    if dbg:
        for i in range(16):
            dbg_dump6.append(
                p.load("y", [(dbg_xn6[128 * i:128 * (i + 1), :], Xn6[i][:])],
                       deps=[tr6_last]))
        for dch in range(8):
            dbg_dump6.append(
                p.load("y", [(dbg_xnt[128 * dch:128 * (dch + 1), :],
                              XnT6[dch][:])], deps=[tr6_last]))
    yt_st = []
    sfree6 = {}
    ebuse = {}
    ev6 = {}
    st6 = {}
    for n in range(NEXP):
        np_ = n % 2
        last_mm = None
        for dd in range(8):
            sb_i = (n * 8 + dd) % 4
            tld = p.load("y", [(strip6[sb_i][:],
                                expwb[n, 128 * dd:128 * (dd + 1), :])],
                         deps=[sfree6.get(sb_i, xmm5_last), pc_tok,
                               None if sb_i in sfree6 else p5_tex_last])
            def ymm(t, n=n, dd=dd, sb_i=sb_i, np_=np_):
                last = None
                for hg in range(2):
                    last = t.matmul(pyt6t[np_][0:32, 512 * hg:512 * (hg + 1)],
                                    XnT6[dd][:, 32 * n:32 * (n + 1)],
                                    strip6[sb_i][:, 512 * hg:512 * (hg + 1)],
                                    start=(dd == 0), stop=False)
                return last
            last_mm = p.do("t", ymm, deps=[tld, tr6_last, ev6.get(n - 1)])
            sfree6[sb_i] = last_mm
        teb = p.load("y", [(ebrow[n % 2][0:1, :],
                            expbb[0:1, D * n:D * (n + 1)])],
                     deps=[ebuse.get(n - 2, xmm5_last), pc_tok,
                           None if n >= 2 else p5_tex_last])
        tbias = p.do("t", lambda t, n=n, np_=np_: [
            t.matmul(pyt6t[np_][0:32, 512 * hg:512 * (hg + 1)],
                     p.ones_row[0:1, 0:32],
                     ebrow[n % 2][0:1, 512 * hg:512 * (hg + 1)],
                     start=False, stop=True)
            for hg in range(2)][-1], deps=[teb, ctoks[1], last_mm])
        ebuse[n] = tbias
        tev = p.do("v", lambda v, np_=np_: v.tensor_copy(
            ytsb6[np_][:], pyt6t[np_][0:32, :]),
            deps=[tbias, st6.get(n - 2)])
        ev6[n] = tev
        st = p.load("y", [(yt_sp[PSL * n:PSL * (n + 1), :],
                           ytsb6[np_][0:32, :])], deps=[tev])
        st6[n] = st
        yt_st.append(st)
    if dbg:
        p.load("y", [(dbg_yt[:, :], yt_sp[:, :])], deps=[yt_st[-1]])
    p6_drain = [yt_st[-1], ev6[NEXP - 1], ev6[NEXP - 2], tr6_last] + dbg_dump6
    p.close(rY)
    p.close(rXa)

    xn1_sts = []
    for th in range(2):
        tc0 = TOWN * th
        rHalf = p.region()
        g0T7 = [p.rsb(rHalf, f"g0T7_{th}_{d}", [128, TOWN], BF16) for d in range(8)]
        r7a = p.region()
        CT7 = [p.rsb(r7a, f"CT7_{th}_{i}", [128, TOWN], BF16) for i in range(16)]
        r7aa = p.region()
        xnh = [p.rsb(r7aa, f"xnh{th}_{d}", [128, TOWN], BF16) for d in range(8)]
        phw7 = [p.rsb(r7aa, f"phw7_{th}_{d}", [128, NPL], BF16) for d in range(8)]
        phbc7 = p.rsb(r7aa, f"phbc7_{th}", [128, 16], F32)
        indP7 = p.rsb(r7aa, f"indP7_{th}", [128, 32], BF16)
        indPT7 = p.rsb(r7aa, f"indPT7_{th}", [32, 128], BF16)
        rscb7 = p.rsb(r7aa, f"rscb7_{th}", [32, TOWN], BF16)
        rscf7 = p.rsb(r7aa, f"rscf7_{th}", [32, TOWN], F32)
        plf7 = p.rps(r7aa, f"plf7_{th}", [128, 1024])
        psc7 = p.rps(r7aa, f"psc7_{th}", [32, TOWN])
        prb7 = p.rps(r7aa, f"prb7_{th}", [128, 512])
        t_in7 = p.load("y", [(xnh[d][:], xn0_sp[128 * d:128 * (d + 1),
                                                tc0:tc0 + TOWN])
                             for d in range(8)] +
                       [(phbc7[:], phibc_i[:, :])],
                       deps=p6_drain + [xn0_spt] +
                       ([xn1_sts[th - 1]] if th else []))
        t_ph7 = p.load("g", [(phw7[d][:], phiT_i[128 * d:128 * (d + 1), :])
                             for d in range(8)] +
                       [(indP7[:], IndP_i[:, :]), (indPT7[:], IndPT_i[:, :])],
                       deps=p6_drain + ([xn1_sts[th - 1]] if th else []))
        exp7 = {}
        sc7 = {}
        for npc in range(16):
            for tg in range(4):
                i = npc * 4 + tg
                par = i % 2
                def lfmm(t, npc=npc, tg=tg, par=par):
                    last = None
                    for d in range(8):
                        last = t.matmul(plf7[:, 512 * par:512 * (par + 1)],
                                        phw7[d][:, 128 * npc:128 * (npc + 1)],
                                        xnh[d][:, 512 * tg:512 * (tg + 1)],
                                        start=(d == 0), stop=(d == 7))
                    return last
                tlf = p.do("t", lfmm, deps=[t_ph7, t_in7, exp7.get(i - 2)])
                tex = p.do("s", lambda s, npc=npc, tg=tg, par=par: s.activation(
                    CT7[npc][:, 512 * tg:512 * (tg + 1)],
                    plf7[:, 512 * par:512 * (par + 1)], AF.Exp,
                    bias=phbc7[:, npc:npc + 1]), deps=[tlf, t_in7])
                exp7[i] = tex
                sc7[tg] = p.do("t", lambda t, npc=npc, tg=tg: t.matmul(
                    psc7[:, 512 * tg:512 * (tg + 1)], indP7[:],
                    CT7[npc][:, 512 * tg:512 * (tg + 1)],
                    start=(npc == 0), stop=(npc == 15), skip_group_check=True),
                    deps=[tex, t_ph7])
        trsc7 = p.do("s", lambda s: (
            s.activation(rscf7[:], psc7[:, :], AF.Ln),
            s.activation(rscb7[:], rscf7[:], AF.Exp, scale=-1.0))[-1],
            deps=[sc7[tg] for tg in range(4)])
        prb_rd = None
        ct_done = []
        for tg in range(4):
            tbc7 = p.do("t", lambda t, tg=tg: t.matmul(
                prb7[:, :], indPT7[:], rscb7[:, 512 * tg:512 * (tg + 1)],
                start=True, stop=True), deps=[trsc7, t_ph7, prb_rd])
            mults = []
            for npc in range(16):
                tm = p.do("v", lambda v, npc=npc, tg=tg: v.tensor_tensor(
                    CT7[npc][:, 512 * tg:512 * (tg + 1)],
                    CT7[npc][:, 512 * tg:512 * (tg + 1)], prb7[:, :],
                    op=ALU.mult), deps=[tbc7, exp7[npc * 4 + tg]])
                mults.append(tm)
            prb_rd = mults[-1]
            ct_done += mults
        p.close(r7aa)

        r7bb = p.region()
        ytb7 = [p.rsb(r7bb, f"ytb7_{th}_{i}", [128, D], BF16) for i in range(16)]
        pdis7 = p.rps(r7bb, f"pdis7_{th}", [128, 1024])
        t_yt7 = p.load("y", [(ytb7[i][:], yt_sp[128 * i:128 * (i + 1), :])
                             for i in range(16)], deps=ct_done[-1:] + [prb_rd])
        evA7 = {}
        for Hc in range(8):
            for tg in range(4):
                i = Hc * 4 + tg
                par = i % 2
                tmm = p.do("t", lambda t, Hc=Hc, tg=tg, par=par: [
                    t.matmul(pdis7[:, 512 * par:512 * (par + 1)],
                             ytb7[npc][:, 128 * Hc:128 * (Hc + 1)],
                             CT7[npc][:, 512 * tg:512 * (tg + 1)],
                             start=(npc == 0), stop=(npc == 15))
                    for npc in range(16)][-1],
                    deps=[t_yt7, evA7.get(i - 2)] + ct_done)
                evA7[i] = p.do("s", lambda s, Hc=Hc, tg=tg, par=par:
                    s.activation(g0T7[Hc][:, 512 * tg:512 * (tg + 1)],
                                 pdis7[:, 512 * par:512 * (par + 1)], AF.Silu),
                    deps=[tmm])
        a7_drain = [evA7[31], evA7[30]]
        p.close(r7bb)
        p.close(r7a)

        r7c = p.region()
        s0v7 = [p.rsb(r7c, f"s0v7_{th}_{d}", [128, TOWN], BF16) for d in range(8)]
        x1f7 = [p.rsb(r7c, f"x1f7_{th}_{d}", [128, TOWN], F32) for d in range(8)]
        xn1v7 = [p.rsb(r7c, f"xn1v7_{th}_{d}", [128, TOWN], BF16) for d in range(8)]
        xb167 = p.rsb(r7c, f"xb167_{th}", [128, 512], BF16)
        xsqB7 = p.rsb(r7c, f"xsqB7_{th}", [128, 512], BF16)
        rowsB7 = p.rsb(r7c, f"rowsB7_{th}", [1, 3072], F32)
        abfB7 = p.rsb(r7c, f"abfB7_{th}", [1, 1024], BF16)
        pstatB7 = p.rps(r7c, f"pstatB7_{th}", [1, 1024])
        pabB7 = p.rps(r7c, f"pabB7_{th}", [128, 1024])
        t_s07 = p.load("y", [(s0v7[d][:], s0_sp[128 * d:128 * (d + 1),
                                               tc0:tc0 + TOWN])
                             for d in range(8)], deps=a7_drain)
        t_x7 = p.load("g", [(x1f7[d][:], xTa[128 * d:128 * (d + 1),
                                             tc0:tc0 + TOWN])
                            for d in range(8)], deps=a7_drain)
        x1_toks7 = []
        for d in range(8):
            tx = p.do("v", lambda v, d=d: (
                v.tensor_tensor(g0T7[d][:], s0v7[d][:], g0T7[d][:], op=ALU.mult),
                v.tensor_tensor(x1f7[d][:], x1f7[d][:], g0T7[d][:],
                                op=ALU.add))[-1],
                deps=[t_s07, t_x7, evA7[31]])
            x1_toks7.append(tx)
        if th == 0:
            x1_st7 = p.load("y", [(x1_sp[128 * d:128 * (d + 1), :], x1f7[d][:])
                                  for d in range(8)], deps=x1_toks7)
        else:
            x1_st7 = None
        sqd = None
        tgd = None
        tvd = None
        xn1_toks7 = []
        for tg in range(4):
            c0 = 512 * tg
            sq_toks = []
            for d in range(8):
                tq = p.do("s", lambda s, d=d, c0=c0: (
                    s.activation(xb167[:], x1f7[d][:, c0:c0 + 512], AF.Copy),
                    s.activation(xsqB7[:], x1f7[d][:, c0:c0 + 512],
                                 AF.Square))[-1],
                    deps=[x1_toks7[d], sqd] if sqd else [x1_toks7[d]])
                tmm = p.do("t", lambda t, d=d: (
                    t.matmul(pstatB7[0:1, 0:512], p.ones_col[:], xb167[:],
                             start=(d == 0), stop=(d == 7)),
                    t.matmul(pstatB7[0:1, 512:1024], p.ones_col[:], xsqB7[:],
                             start=(d == 0), stop=(d == 7)))[-1],
                    deps=[tq, ctoks[1], tvd])
                sqd = tmm
                sq_toks.append(tmm)
            tvd = p.do("v", lambda v: (
                v.tensor_scalar_mul(rowsB7[0:1, MEAN:MEAN + 512],
                                    pstatB7[0:1, 0:512], 1.0 / D),
                v.tensor_scalar_mul(rowsB7[0:1, MSQ:MSQ + 512],
                                    pstatB7[0:1, 512:1024], 1.0 / D),
                v.tensor_mul(rowsB7[0:1, TMP:TMP + 512],
                             rowsB7[0:1, MEAN:MEAN + 512],
                             rowsB7[0:1, MEAN:MEAN + 512]),
                v.tensor_sub(rowsB7[0:1, NV:NV + 512],
                             rowsB7[0:1, TMP:TMP + 512],
                             rowsB7[0:1, MSQ:MSQ + 512]))[-1],
                deps=[sq_toks[-1], tgd])
            tsd = p.do("s", lambda s: (
                s.activation(rowsB7[0:1, NV:NV + 512], rowsB7[0:1, NV:NV + 512],
                             AF.Ln, bias=p.epsc[0:1, 0:1], scale=-1.0),
                s.activation(rowsB7[0:1, AA:AA + 512],
                             rowsB7[0:1, NV:NV + 512], AF.Exp,
                             scale=-0.5))[-1], deps=[tvd])
            tab = p.do("v", lambda v: (
                v.tensor_mul(rowsB7[0:1, TMP:TMP + 512],
                             rowsB7[0:1, MEAN:MEAN + 512],
                             rowsB7[0:1, AA:AA + 512]),
                v.tensor_scalar_mul(rowsB7[0:1, BB:BB + 512],
                                    rowsB7[0:1, TMP:TMP + 512], -1.0),
                v.tensor_copy(abfB7[0:1, 0:512], rowsB7[0:1, AA:AA + 512]),
                v.tensor_copy(abfB7[0:1, 512:1024],
                              rowsB7[0:1, BB:BB + 512]))[-1], deps=[tsd])
            tbc = p.do("t", lambda t: (
                t.matmul(pabB7[:, 0:512], p.ones_row[0:1, :], abfB7[0:1, 0:512],
                         start=True, stop=True),
                t.matmul(pabB7[:, 512:1024], p.ones_row[0:1, :],
                         abfB7[0:1, 512:1024], start=True, stop=True))[-1],
                deps=[tab, ctoks[1], tgd])
            tgd = p.do("v", lambda v, c0=c0: [
                (v.tensor_tensor(xn1v7[d][:, c0:c0 + 512], x1f7[d][:, c0:c0 + 512],
                                 pabB7[:, 0:512], op=ALU.mult),
                 v.tensor_tensor(xn1v7[d][:, c0:c0 + 512],
                                 xn1v7[d][:, c0:c0 + 512],
                                 pabB7[:, 512:1024], op=ALU.add))
                for d in range(8)][-1][-1], deps=[tbc])
            xn1_toks7.append(tgd)
        st = p.load("y", [(xn1_sp[128 * d:128 * (d + 1), tc0:tc0 + TOWN],
                           xn1v7[d][:]) for d in range(8)],
                    deps=xn1_toks7 + ([x1_st7] if x1_st7 else []))
        xn1_sts.append(st)
        p.close(r7c)
        p.close(rHalf)
    if dbg:
        p.load("y", [(dbg_x1[:, :], x1_sp[:, :])], deps=xn1_sts)
        p.load("y", [(dbg_xn1[:, :], xn1_sp[:, :])], deps=xn1_sts)
    f7_drain = xn1_sts

    emit_l1(p, ctoks, xn1_sp, x1_sp, wq2_i, WpT_i, bp_i,
            w1b, w2b, h1b, h2b, w3b,
            w1s_i, w2s_i, h1s_i, h2s_i, w3s_i, x2_o, wq="y",
            entry_deps=f7_drain + [pc_tok])
    return p.finish()



_IDENT = np.eye(128, dtype=np.float32)


def _indPT():
    m = np.zeros((32, 128), dtype=np.float32)
    for r in range(128):
        m[r % 32, r] = 1.0
    return m


def _f32(a):
    return np.ascontiguousarray(np.asarray(a, dtype=np.float32))


def _prep_weights(inputs):
    w = {}
    w["WqkvT"] = _f32(np.asarray(inputs["L0_attn_W_qkv"]).T)
    w["WoT"] = _f32(np.asarray(inputs["L0_attn_W_o"]).T)
    w["phiT"] = _f32(np.asarray(inputs["L0_ffn_phi_w"]).T)
    phib = _f32(inputs["L0_ffn_phi_b"]).reshape(-1)
    w["phibr"] = phib.reshape(1, NPL)
    w["phibc"] = _f32(phib.reshape(16, 128).T)
    w["IndPT"] = _indPT()
    w["IndP"] = np.ascontiguousarray(w["IndPT"].T)
    w["ident128"] = _IDENT
    Wq = _f32(inputs["L1_attn_Wq"])
    wq2 = np.zeros((128, NH * DH), np.float32)
    for h in range(NH):
        wq2[0:64, DH * h:DH * (h + 1)] = Wq[h].T
        wq2[64:128, DH * h:DH * (h + 1)] = Wq[h].T
    w["Wq2"] = np.ascontiguousarray(wq2)
    w["WpT"] = _f32(np.asarray(inputs["L1_attn_Wp"]).T)
    w["bpr"] = _f32(inputs["L1_attn_bp"]).reshape(1, D)
    w["w1T"] = _f32(np.asarray(inputs["L1_ffn_w1_w"]).T)
    w["w2T"] = _f32(np.asarray(inputs["L1_ffn_w2_w"]).T)
    w["h1T"] = _f32(np.asarray(inputs["L1_ffn_h1_w"]).T)
    w["h2T"] = _f32(np.asarray(inputs["L1_ffn_h2_w"]).T)
    w["w3T"] = _f32(np.asarray(inputs["L1_ffn_w3_w"]).T)
    w["w1sc"] = _f32(_f32(inputs["L1_ffn_w1_s"]).reshape(32, 128).T)
    w["w2sc"] = _f32(_f32(inputs["L1_ffn_w2_s"]).reshape(32, 128).T)
    w["h1sc"] = _f32(_f32(inputs["L1_ffn_h1_s"]).reshape(32, 128).T)
    w["h2sc"] = _f32(_f32(inputs["L1_ffn_h2_s"]).reshape(32, 128).T)
    w["w3sc"] = _f32(_f32(inputs["L1_ffn_w3_s"]).reshape(8, 128).T)
    return w


def kernel(**inputs):
    x = _f32(inputs["L0_attn_x"])
    w = _prep_weights(inputs)
    expw = _f32(inputs["L0_ffn_expert_w"])
    expb = _f32(inputs["L0_ffn_expert_b"]).reshape(1, NEXP * D)
    ims = []
    for c in range(8):
        b, half = c // 2, c % 2
        own = x[b, TOWN * half:TOWN * (half + 1)]
        oth = x[b, TOWN * (1 - half):TOWN * (2 - half)]
        others = np.concatenate([x[bb] for bb in range(4) if bb != b], 0)
        allt = np.concatenate([own, oth, others], 0)
        im = dict(w)
        im["xTa"] = np.ascontiguousarray(allt.T)
        im["xta"] = np.ascontiguousarray(allt)
        im["expwA"] = expw
        im["expbA"] = expb
        ims.append(im)
    res = run_bass_kernel_spmd(build_fused(), ims, core_ids=list(range(8))).results
    out = np.empty((4, NT, D), np.float32)
    for c in range(8):
        b, half = c // 2, c % 2
        out[b, TOWN * half:TOWN * (half + 1)] = np.asarray(res[c]["x2T"]).T
    return out


def kernel_multi(**inputs):
    x = _f32(inputs["L0_attn_x"])
    w = _prep_weights(inputs)
    expw = _f32(inputs["L0_ffn_expert_w"])
    expb = _f32(inputs["L0_ffn_expert_b"])
    xTs, xtoks = [], []
    for c in range(8):
        b, half = c // 2, c % 2
        own = x[b, TOWN * half:TOWN * (half + 1)]
        oth = x[b, TOWN * (1 - half):TOWN * (2 - half)]
        xTs.append(np.ascontiguousarray(np.concatenate([own, oth], 0).T))
        xtoks.append(np.ascontiguousarray(own))

    im1 = [{
        "xT": xTs[c], "xtok": xtoks[c], "WqkvT": w["WqkvT"], "WoT": w["WoT"],
        "phiT": w["phiT"], "phibr": w["phibr"], "phibc": w["phibc"],
        "IndP": w["IndP"], "IndPT": w["IndPT"], "ident128": _IDENT,
    } for c in range(8)]
    r1 = run_bass_kernel_spmd(build_ml1(), im1, core_ids=list(range(8))).results

    im2 = [{
        "xp_all": np.stack([r1[s]["Xp"][256 * c:256 * (c + 1)] for s in range(8)]),
        "expw": np.ascontiguousarray(expw[EOWN * c:EOWN * (c + 1)]),
        "expb": np.ascontiguousarray(expb[EOWN * c:EOWN * (c + 1)].reshape(1, -1)),
        "ident128": _IDENT,
    } for c in range(8)]
    r2 = run_bass_kernel_spmd(build_ml2(), im2, core_ids=list(range(8))).results

    YtA = np.ascontiguousarray(
        np.concatenate([np.asarray(r2[c]["yt"]).reshape(EOWN * PSL, D)
                        for c in range(8)], 0))
    im3 = [{
        "YtA": YtA, "CTi": r1[c]["CT"], "s0Ti": r1[c]["s0T"],
        "xTown": np.ascontiguousarray(xTs[c][:, 0:TOWN]), "ident128": _IDENT,
    } for c in range(8)]
    r3 = run_bass_kernel_spmd(build_ml3(), im3, core_ids=list(range(8))).results

    im4 = [{
        "xn1f": np.ascontiguousarray(np.concatenate(
            [np.asarray(r3[c]["xn1T"]), np.asarray(r3[c ^ 1]["xn1T"])], 1)),
        "x1Ti": r3[c]["x1T"],
        "Wq2": w["Wq2"], "WpT": w["WpT"], "bpr": w["bpr"],
        "w1T": w["w1T"], "w2T": w["w2T"], "h1T": w["h1T"], "h2T": w["h2T"],
        "w3T": w["w3T"], "w1sc": w["w1sc"], "w2sc": w["w2sc"],
        "h1sc": w["h1sc"], "h2sc": w["h2sc"], "w3sc": w["w3sc"],
        "ident128": _IDENT,
    } for c in range(8)]
    r4 = run_bass_kernel_spmd(build_ml4(), im4, core_ids=list(range(8))).results

    out = np.empty((4, NT, D), np.float32)
    for c in range(8):
        b, half = c // 2, c % 2
        out[b, TOWN * half:TOWN * (half + 1)] = np.asarray(r4[c]["x2T"]).T
    return out
